# version 11
# baseline (speedup 1.0000x reference)
import numpy as np
import concourse.bass as bass
import concourse.mybir as mybir
from concourse.bass_utils import run_bass_kernel_spmd

F32 = mybir.dt.float32
BF16 = mybir.dt.bfloat16
AF = mybir.ActivationFunctionType
ALU = mybir.AluOpType
AX = mybir.AxisListType
fp = np.float32

D = 2048
FC = 16
TOK = 1024
NT = 8
H = 16
DH = 128
DL = 512
NDL = 4
S = 2048
KC = 16
INTER = 8192
SCALE = float(1.0 / np.sqrt(np.float32(128.0)))
EPS = 1e-5
INVD = float(np.float32(1.0) / np.float32(2048.0))


def jmin(c):
    return c // 2


class Prog:
    ENGS = ("tensor", "vector", "scalar", "gpsimd", "sync")

    def __init__(self, nc):
        self.nc = nc
        self.steps = []
        self.sem = {}
        for e in self.ENGS:
            self.sem[e] = _S(nc.alloc_semaphore("sm_" + e))
        self.NQ = 4
        self.qsems = {"sync": [_S(nc.alloc_semaphore(f"sm_dsync{i}")) for i in range(self.NQ)],
                      "gpsimd": [_S(nc.alloc_semaphore(f"sm_dgps{i}")) for i in range(self.NQ)]}
        self.qidx = {"sync": 0, "gpsimd": 0}

    def extra_sem(self, name):
        return _S(self.nc.alloc_semaphore(name))

    def step(self, eng, emit, waits=(), dma=False, inc=None):
        extra = []
        if inc is not None:
            s, amt = inc
        elif dma:
            i = self.qidx[eng] % self.NQ
            self.qidx[eng] += 1
            s = self.qsems[eng][i]
            amt = 16
            if s.n > 0:
                extra = [(s, s.n)]
        else:
            s = self.sem[eng]
            amt = 1
        s.n += amt
        pt = (s, s.n)
        merged = {}
        for (ws, wv) in list(waits) + extra:
            if wv is None or wv <= 0:
                continue
            if id(ws) in merged:
                merged[id(ws)] = (ws, max(merged[id(ws)][1], wv))
            else:
                merged[id(ws)] = (ws, wv)
        self.steps.append((eng, emit, list(merged.values()), (s, amt)))
        return pt

    def pt(self, eng):
        key = eng
        return (self.sem[key], self.sem[key].n)

    def emit(self):
        nc = self.nc
        with nc.Block() as block:
            @block.tensor
            def _(e):
                self._run("tensor", e)

            @block.vector
            def _(e):
                self._run("vector", e)

            @block.scalar
            def _(e):
                self._run("scalar", e)

            @block.gpsimd
            def _(e):
                self._run("gpsimd", e)

            @block.sync
            def _(e):
                self._run("sync", e)

    def _run(self, name, eng):
        for (e, emit, waits, inc) in self.steps:
            if e != name:
                continue
            for (s, v) in waits:
                eng.wait_ge(s.h, v)
            ins = emit(eng)
            assert ins is not None, f"step on {name} returned no instruction"
            ins.then_inc(inc[0].h, inc[1])


class _S:
    __slots__ = ("h", "n")

    def __init__(self, handle):
        self.h = handle
        self.n = 0


class PsumPool:

    def __init__(self, prog, slots):
        self.p = prog
        self.slots = slots
        self.rel = [None] * len(slots)
        self.i = 0

    def take(self):
        i = self.i % len(self.slots)
        self.i += 1
        waits = [self.rel[i]] if self.rel[i] is not None else []
        return i, self.slots[i], waits

    def release(self, i, pt):
        self.rel[i] = pt


def build(debug=()):
    nc = bass.Bass("TRN2", target_bir_lowering=False)
    p = Prog(nc)

    xin = nc.dram_tensor("xin", [TOK, D], F32, kind="ExternalInput")
    wq = nc.dram_tensor("wq", [4, 128, FC, 512], F32, kind="ExternalInput")
    wk = nc.dram_tensor("wk", [4, 128, FC, 512], F32, kind="ExternalInput")
    wv = nc.dram_tensor("wv", [4, 128, FC, 512], F32, kind="ExternalInput")
    wo = nc.dram_tensor("wo", [4, 128, FC, 512], F32, kind="ExternalInput")
    wkl = nc.dram_tensor("wkl", [128, FC, DL], F32, kind="ExternalInput")
    wvl = nc.dram_tensor("wvl", [128, FC, DL], F32, kind="ExternalInput")
    upk = nc.dram_tensor("upk", [128, H, NDL, 128], F32, kind="ExternalInput")
    upv = nc.dram_tensor("upv", [128, H, NDL, 128], F32, kind="ExternalInput")
    gate = nc.dram_tensor("gate", [128, FC, 8], F32, kind="ExternalInput")
    upw = nc.dram_tensor("upw", [32, 128, FC, 256], F32, kind="ExternalInput")
    dnw = nc.dram_tensor("dnw", [4, 4, 128, FC, 512], F32, kind="ExternalInput")
    exw = nc.dram_tensor("exw", [4, 128, FC, 512], F32, kind="ExternalInput")
    wqb = nc.dram_tensor("wqb", [128, FC], F32, kind="ExternalInput")
    wkb = nc.dram_tensor("wkb", [128, FC], F32, kind="ExternalInput")
    wvb = nc.dram_tensor("wvb", [128, FC], F32, kind="ExternalInput")
    wklb = nc.dram_tensor("wklb", [128, NDL], F32, kind="ExternalInput")
    wvlb = nc.dram_tensor("wvlb", [128, NDL], F32, kind="ExternalInput")
    upkb = nc.dram_tensor("upkb", [128, H], F32, kind="ExternalInput")
    upvb = nc.dram_tensor("upvb", [D], F32, kind="ExternalInput")
    wob = nc.dram_tensor("wob", [D], F32, kind="ExternalInput")
    lng = nc.dram_tensor("lng", [D], F32, kind="ExternalInput")
    lnb = nc.dram_tensor("lnb", [D], F32, kind="ExternalInput")
    parity = nc.dram_tensor("parity", [1], F32, kind="ExternalInput")
    out = nc.dram_tensor("out", [TOK, D], F32, kind="ExternalOutput")

    kvk = nc.dram_tensor("kvk", [D + 1, TOK], BF16)
    kall = nc.dram_tensor("kall", [2 * (D + 1), TOK], BF16)
    kvv = nc.dram_tensor("kvv", [TOK, D], BF16)
    vall = nc.dram_tensor("vall", [S, D], BF16)
    xspill = nc.dram_tensor("xspill", [TOK, D], F32)

    DBG = {}

    def dbg(name, shape, dtype=F32):
        if name in debug:
            if name not in DBG:
                DBG[name] = nc.dram_tensor("dbg_" + name, shape, dtype, kind="ExternalOutput")
            return DBG[name]
        return None

    ARENA = 210112
    arena = nc.alloc_sbuf_tensor("arena", [128, ARENA], mybir.dt.int8).ap()

    def carve(off, dtype, shape):
        n = int(np.prod(shape)) * mybir.dt.size(dtype)
        assert off % 64 == 0 and off + n <= ARENA, (off, n)
        v = arena[:, off:off + n].bitcast(dtype)
        if len(shape) > 1:
            kw = {f"a{i}": shape[i] for i in range(len(shape))}
            v = v.rearrange("p (" + " ".join(kw) + ") -> p " + " ".join(kw), **kw)
        return v

    ident = carve(0, BF16, [128])
    tri = carve(256, BF16, [128])
    mlo = carve(512, BF16, [128])
    mhi = carve(768, BF16, [128])
    ones_bf = carve(1024, BF16, [1024])
    ident32 = carve(3072, F32, [128])
    gate_sb = carve(3584, F32, [FC, 8])
    wqb_c = carve(4096, F32, [16])
    wkb_c = carve(4160, F32, [16])
    wvb_c = carve(4224, F32, [16])
    wklb_c = carve(4288, F32, [16])
    wvlb_c = carve(4352, F32, [16])
    upkb_c = carve(4416, F32, [16])
    a2 = carve(4480, F32, [64])
    upvb_rep = carve(4736, BF16, [2048])
    qn_row = carve(8832, F32, [16])
    knmax_row = carve(8896, BF16, [32])
    mscr = carve(8960, BF16, [32])
    mtmp = carve(9024, F32, [16])
    negm = carve(9088, F32, [16])
    avr = carve(9152, F32, [16])
    stats = carve(9216, F32, [64])
    stats2 = carve(9472, F32, [64])
    gmax = carve(9728, F32, [64])
    scale_c = carve(9984, F32, [16])
    par_sb = carve(10048, F32, [16])
    zrow = carve(10112, BF16, [1024])

    R1 = 13504
    xld = carve(R1, F32, [2, 2048])
    xldb = carve(R1 + 16384, BF16, [2, 2048])
    kstg = carve(R1, BF16, [2, 1024])
    vstg = carve(R1 + 4096, BF16, [2, 8, 128])
    ksqb = carve(R1 + 12288, BF16, [1024])
    t1T = carve(R1 + 24576, BF16, [FC, 1024])
    latT = carve(R1 + 57344, BF16, [NDL, 1024])
    eSB = carve(R1, BF16, [2, KC, 1024])
    ybuf = carve(R1, F32, [NT, 2048])

    R2 = R1 + 65536
    xT = carve(R2, BF16, [FC, 1024])
    xTf = xT

    R3 = R2 + 32768
    qT = carve(R3, BF16, [H, 1024])
    wob_rep = carve(R3, F32, [2048])
    xrb = carve(R3 + 8192, F32, [2, 2048])
    xcastb = carve(R3 + 24576, BF16, [2, 2048])
    hq = carve(R3, BF16, [16, 1024])
    logits32 = carve(R3, F32, [1024])
    pbs32 = carve(R3 + 4096, F32, [1024])


    R4 = R3 + 32768
    wbuf = carve(R4, BF16, [2, FC, 512])
    kbuf = carve(R4, BF16, [2, 2048])
    vbuf = carve(R4 + 8192, BF16, [2, KC, 132])
    ostg = carve(R4 + 25600, BF16, [NT, 128])
    p7buf = carve(R4, BF16, [2, FC, 512])

    R5 = R4 + 32768
    upk_sb = carve(R5, BF16, [H, NDL, 128])
    xl32 = carve(R5, F32, [2, 512])
    upv_sb = carve(R5 + 16384, BF16, [H, NDL, 128])
    aoT = carve(R5, BF16, [H, 1024])
    gam_rep = carve(R5, F32, [2048])
    bet_rep = carve(R5 + 8192, F32, [2048])
    xlrb = carve(R5 + 16384, F32, [2, 2048])

    pb = [nc.alloc_psum_tensor(f"pb{i}", [128, 512], F32).ap() for i in range(6)]
    pt_ = [nc.alloc_psum_tensor(f"pt{i}", [128, 1024], BF16).ap() for i in range(2)]
    mm = PsumPool(p, pb[:5])
    aux = PsumPool(p, pb[5:6])
    tp = PsumPool(p, pt_)

    TRI = tri.bitcast(BF16)

    a2v = a2.rearrange("p (i j) -> p i j", i=8)[0:8]

    def prep_ms(g):
        g.memset(ident[:, 0:128], 0.0)
        g.memset(tri[:, 0:128], 0.0)
        g.memset(ones_bf[:, :], 1.0)
        g.memset(ident32[:, 0:128], 0.0)
        g.memset(knmax_row[0:1, :], 0.0)
        g.memset(zrow[0:1, :], 0.0)
        return g.memset(a2[0:8, :], 0.0)

    pt_ms = p.step("gpsimd", prep_ms)

    def prep_a1(g):
        g.affine_select(out=ident[:, 0:128], in_=ident[:, 0:128], compare_op=ALU.not_equal,
                        fill=1.0, base=0, pattern=[[-1, 128]], channel_multiplier=1)
        g.affine_select(out=tri[:, 0:128], in_=tri[:, 0:128], compare_op=ALU.is_gt,
                        fill=1.0, base=0, pattern=[[-1, 128]], channel_multiplier=1)
        g.affine_select(out=ident32[:, 0:128], in_=ident32[:, 0:128], compare_op=ALU.not_equal,
                        fill=1.0, base=0, pattern=[[-1, 128]], channel_multiplier=1)
        return g.affine_select(out=a2v, in_=a2v, compare_op=ALU.not_equal, fill=1.0,
                               base=0, pattern=[[-1, 8], [0, 8]], channel_multiplier=1)

    pt_a1 = p.step("gpsimd", prep_a1, waits=[pt_ms])
    pt_prep = p.step("gpsimd", lambda g: g.affine_select(
        out=a2v, in_=a2v, compare_op=ALU.not_equal, fill=1.0,
        base=0, pattern=[[0, 8], [-1, 8]], channel_multiplier=1), waits=[pt_a1])

    bias_pt = {}
    for nm, dst, srct in (("wqb", wqb_c, wqb), ("wkb", wkb_c, wkb), ("wvb", wvb_c, wvb),
                          ("upkb", upkb_c, upkb)):
        bias_pt[nm] = p.step("sync", (lambda dst=dst, srct=srct: lambda sy: sy.dma_start(
            out=dst[:, :], in_=srct[:, :]))(), dma=True)
    bias_pt["wklb"] = p.step("sync", lambda sy: sy.dma_start(out=wklb_c[:, 0:4], in_=wklb[:, :]), dma=True)
    bias_pt["wvlb"] = p.step("sync", lambda sy: sy.dma_start(out=wvlb_c[:, 0:4], in_=wvlb[:, :]), dma=True)
    pt_upvb = p.step("gpsimd", lambda g: g.dma_start(
        out=upvb_rep[:, :], in_=upvb[None, :].broadcast_to([128, 2048])), dma=True)
    pt_gate = p.step("sync", lambda sy: sy.dma_start(
        out=gate_sb.rearrange("p a b -> p (a b)"), in_=gate.rearrange("p a b -> p (a b)")), dma=True)

    pt_acol = p.step("sync", lambda sy: sy.dma_start(
        out=mtmp[:, 0:1], in_=parity[None, :].broadcast_to([128, 1])), dma=True)
    pt_msk0 = p.step("vector", lambda v: v.tensor_scalar_max(
        mlo[:, 0:128], tri[:, 0:128], mtmp[:, 0:1]), waits=[pt_acol, pt_prep])
    pt_masks = p.step("vector", lambda v: v.tensor_scalar_mul(
        mhi[:, 0:128], tri[:, 0:128], mtmp[:, 0:1]), waits=[pt_acol, pt_prep])

    cast_pts = []
    xt_evac_pts = []
    for t in range(NT):
        par = t % 2
        waits = [cast_pts[t - 2]] if t >= 2 else []
        pt_x = p.step("sync", (lambda t=t, par=par: lambda sy: sy.dma_start(
            out=xld[:, par], in_=xin[t * 128:(t + 1) * 128, :]))(), waits=waits, dma=True)
        pt_c = p.step("vector", (lambda par=par: lambda v: v.tensor_copy(
            xldb[:, par], xld[:, par]))(), waits=[pt_x])
        cast_pts.append(pt_c)
        for g8 in range(2):
            i, slot, twaits = tp.take()
            def mktr(par=par, g8=g8, slot=slot):
                def emit(te):
                    ins = None
                    for k in range(8):
                        c = g8 * 8 + k
                        ins = te.transpose(slot[:, k * 128:(k + 1) * 128],
                                           xldb[:, par, c * 128:(c + 1) * 128],
                                           ident[:, 0:128])
                    return ins
                return emit
            pt_t = p.step("tensor", mktr(), waits=[pt_c, pt_prep] + twaits)
            pt_e = p.step("vector", (lambda t=t, g8=g8, slot=slot: lambda v: v.tensor_copy(
                xT[:, g8 * 8:(g8 + 1) * 8, t * 128:(t + 1) * 128],
                slot.rearrange("p (a b) -> p a b", b=128)))(), waits=[pt_t])
            tp.release(i, pt_e)
            xt_evac_pts.append(pt_e)
    pt_xT = p.pt("vector")

    d = dbg("xT", [128, FC, TOK], BF16)
    if d is not None:
        p.step("sync", lambda sy, d=d: sy.dma_start(out=d.ap(), in_=xT[:, :, :]),
               waits=[pt_xT], dma=True)

    def proj_stream(wdram, bias_c, dst, nof, src_chunk, src_pt, extra_mm_waits=(),
                    first_war=(), bias_wait=None):
        nq = (nof + 3) // 4
        wpts = []
        qmm_pts = []
        evac_pts = [None] * nof
        for q in range(nq):
            par = q % 2
            waits = [qmm_pts[q - 2]] if q >= 2 else list(first_war)
            wpt = p.step("gpsimd", (lambda q=q, par=par: lambda g: g.dma_start(
                out=wbuf[:, par], in_=wdram[q]))(), waits=waits, dma=True)
            wpts.append(wpt)
            for oc in range(min(4, nof - q * 4)):
                of = q * 4 + oc
                for piece in range(2):
                    cs = slice(piece * 512, (piece + 1) * 512)
                    bi, slot, bwaits = mm.take()
                    def mk(par=par, oc=oc, cs=cs, slot=slot):
                        def emit(te):
                            ins = None
                            for c in range(FC):
                                ins = te.matmul(slot[:, 0:512],
                                                wbuf[:, par, c, oc * 128:(oc + 1) * 128],
                                                src_chunk(c)[:, cs],
                                                start=(c == 0), stop=(c == FC - 1))
                            return ins
                        return emit
                    p.step("tensor", mk(), waits=[wpt, src_pt] + bwaits + list(extra_mm_waits))
                    mmpt = p.pt("tensor")
                    def ev(of=of, cs=cs, slot=slot):
                        def emit(v):
                            if bias_c is not None:
                                return v.tensor_scalar_add(dst[:, of, cs], slot[:, 0:512],
                                                           bias_c[:, of:of + 1])
                            return v.tensor_copy(dst[:, of, cs], slot[:, 0:512])
                        return emit
                    ept = p.step("vector", ev(), waits=[mmpt] + ([bias_wait] if bias_wait else []))
                    mm.release(bi, ept)
                    evac_pts[of] = ept
            qmm_pts.append(p.pt("tensor"))
        return evac_pts, p.pt("tensor")

    t1_pts, pt_t1mm = proj_stream([wk[q] for q in range(4)], wkb_c, t1T, FC,
                                  lambda c: xT[:, c], pt_xT, bias_wait=bias_pt["wkb"])
    pt_t1 = p.pt("vector")

    pt_wkl = p.step("gpsimd", lambda g: g.dma_start(out=wbuf[:, 0], in_=wkl[:, :, :]),
                    waits=[pt_t1mm], dma=True)
    lat_pts = []
    for of in range(NDL):
        for piece in range(2):
            cs = slice(piece * 512, (piece + 1) * 512)
            bi, slot, bwaits = mm.take()
            def mk(of=of, cs=cs, slot=slot):
                def emit(te):
                    ins = None
                    for c in range(FC):
                        ins = te.matmul(slot[:, 0:512], wbuf[:, 0, c, of * 128:(of + 1) * 128],
                                        t1T[:, c, cs], start=(c == 0), stop=(c == FC - 1))
                    return ins
                return emit
            p.step("tensor", mk(), waits=[pt_wkl, pt_t1] + bwaits)
            mmpt = p.pt("tensor")
            ept = p.step("vector", (lambda of=of, cs=cs, slot=slot: lambda v: v.tensor_scalar_add(
                latT[:, of, cs], slot[:, 0:512], wklb_c[:, of:of + 1]))(),
                waits=[mmpt, bias_pt["wklb"]])
            mm.release(bi, ept)
            lat_pts.append(ept)
    pt_klat = p.pt("vector")
    pt_klatmm = p.pt("tensor")

    pt_upk = p.step("gpsimd", lambda g: g.dma_start(
        out=upk_sb.rearrange("p a b c -> p (a b c)"), in_=upk.rearrange("p a b c -> p (a b c)")), dma=True)
    pt_upv = p.step("gpsimd", lambda g: g.dma_start(
        out=upv_sb.rearrange("p a b c -> p (a b c)"), in_=upv.rearrange("p a b c -> p (a b c)")), dma=True)

    kvw_pts = []
    ksq_pts = []
    knmm_pts = []
    nrm_comb = []
    for h in range(H):
        par = h % 2
        kpts = []
        for piece in range(2):
            cs = slice(piece * 512, (piece + 1) * 512)
            bi, slot, bwaits = mm.take()
            def mk(h=h, cs=cs, slot=slot):
                def emit(te):
                    ins = None
                    for c in range(NDL):
                        ins = te.matmul(slot[:, 0:512], upk_sb[:, h, c, :], latT[:, c, cs],
                                        start=(c == 0), stop=(c == NDL - 1))
                    return ins
                return emit
            p.step("tensor", mk(), waits=[pt_upk, pt_klat] + bwaits)
            mmpt = p.pt("tensor")
            w = [mmpt, bias_pt["upkb"]]
            if h >= 2:
                w += [kvw_pts[h - 2], ksq_pts[h - 2]]
            ept = p.step("vector", (lambda par=par, h=h, cs=cs, slot=slot: lambda v: v.tensor_scalar_add(
                kstg[:, par, cs], slot[:, 0:512], upkb_c[:, h:h + 1]))(), waits=w)
            mm.release(bi, ept)
            kpts.append(ept)
        wpt = p.step("sync", (lambda h=h, par=par: lambda sy: sy.dma_start(
            out=kvk[h * 128:(h + 1) * 128, :], in_=kstg[:, par]))(), waits=kpts, dma=True)
        kvw_pts.append(wpt)
        sqw = kpts + ([knmm_pts[h - 1]] if h >= 1 else [])
        pt_sq = p.step("scalar", (lambda par=par: lambda sc: sc.activation(
            ksqb[:, :], kstg[:, par], AF.Square, bias=fp(0.0), scale=fp(1.0)))(), waits=sqw)
        ksq_pts.append(pt_sq)
        red_pts = []
        for piece in range(2):
            cs = slice(piece * 512, (piece + 1) * 512)
            bi, slot, bwaits = aux.take()
            pt_nm = p.step("tensor", (lambda cs=cs, slot=slot: lambda te: te.matmul(
                slot[0:1, 0:512], ones_bf[:, 0:1], ksqb[:, cs], start=True, stop=True))(),
                waits=[pt_sq, pt_prep] + bwaits)
            rpt = p.step("vector", (lambda piece=piece, slot=slot: lambda v: v.reduce_max(
                mtmp[0:1, 8 + piece:9 + piece], slot[0:1, 0:512], axis=AX.X))(),
                waits=[pt_nm] + ([nrm_comb[-1]] if nrm_comb else []))
            aux.release(bi, rpt)
            red_pts.append(rpt)
        knmm_pts.append(p.pt("tensor"))
        nrm_comb.append(p.step("vector", (lambda h=h: lambda v: v.tensor_tensor(
            knmax_row[0:1, h:h + 1], mtmp[0:1, 8:9], mtmp[0:1, 9:10], ALU.max))(), waits=red_pts))
    pt_knorm = p.pt("vector")
    pt_knT = p.pt("tensor")
    kvw_pts.append(p.step("sync", lambda sy: sy.dma_start(
        out=kvk[D:D + 1, :], in_=zrow[0:1, :]), waits=[pt_ms], dma=True))
    kvw_pts.append(p.step("sync", lambda sy: sy.dma_start(
        out=kvk[D:D + 1, 0:32], in_=knmax_row[0:1, 0:32]), waits=[pt_knorm], dma=True))

    s_agk = p.extra_sem("s_agk")
    pt_agk = p.step("gpsimd", lambda g: g.collective_compute(
        "AllGather", ALU.bypass, replica_groups=[[0, 1], [2, 3], [4, 5], [6, 7]],
        ins=[kvk.ap()], outs=[kall.ap()]), waits=kvw_pts, inc=(s_agk, 1))

    t2_pts, pt_t2mm = proj_stream([wv[q] for q in range(4)], wvb_c, t1T, FC,
                                  lambda c: xT[:, c], pt_xT, extra_mm_waits=[pt_klatmm],
                                  first_war=[pt_klatmm], bias_wait=bias_pt["wvb"])
    pt_t2 = p.pt("vector")

    pt_wvl = p.step("gpsimd", lambda g: g.dma_start(out=wbuf[:, 0], in_=wvl[:, :, :]),
                    waits=[pt_t2mm], dma=True)
    for of in range(NDL):
        for piece in range(2):
            cs = slice(piece * 512, (piece + 1) * 512)
            bi, slot, bwaits = mm.take()
            def mk(of=of, cs=cs, slot=slot):
                def emit(te):
                    ins = None
                    for c in range(FC):
                        ins = te.matmul(slot[:, 0:512], wbuf[:, 0, c, of * 128:(of + 1) * 128],
                                        t1T[:, c, cs], start=(c == 0), stop=(c == FC - 1))
                    return ins
                return emit
            p.step("tensor", mk(), waits=[pt_wvl, pt_t2] + bwaits)
            mmpt = p.pt("tensor")
            ept = p.step("vector", (lambda of=of, cs=cs, slot=slot: lambda v: v.tensor_scalar_add(
                latT[:, of, cs], slot[:, 0:512], wvlb_c[:, of:of + 1]))(),
                waits=[mmpt, bias_pt["wvlb"]])
            mm.release(bi, ept)
    pt_vlat = p.pt("vector")
    pt_vlmm = p.pt("tensor")

    vst_pts = []
    for h in range(H):
        par = h % 2
        ev_pts = []
        for t in range(NT):
            bi, slot, bwaits = mm.take()
            def mk(h=h, t=t, slot=slot):
                def emit(te):
                    ins = None
                    for c in range(NDL):
                        ins = te.matmul(slot[:, 0:128], latT[:, c, t * 128:(t + 1) * 128],
                                        upv_sb[:, h, c, :], start=(c == 0), stop=(c == NDL - 1))
                    return ins
                return emit
            p.step("tensor", mk(), waits=[pt_upv, pt_vlat] + bwaits)
            mmpt = p.pt("tensor")
            war = [vst_pts[h - 2]] if h >= 2 else []
            ept = p.step("vector", (lambda h=h, t=t, par=par, slot=slot: lambda v: v.tensor_tensor(
                vstg[:, par, t, :], slot[:, 0:128], upvb_rep[:, h * 128:(h + 1) * 128], ALU.add))(),
                waits=[mmpt, pt_upvb] + war)
            mm.release(bi, ept)
            ev_pts.append(ept)
        wpt = p.step("sync", (lambda h=h, par=par: lambda sy: sy.dma_start(
            out=kvv[:, h * 128:(h + 1) * 128].rearrange("(a p) d -> p a d", p=128),
            in_=vstg[:, par]))(), waits=ev_pts, dma=True)
        vst_pts.append(wpt)

    s_agv = p.extra_sem("s_agv")
    pt_agv = p.step("gpsimd", lambda g: g.collective_compute(
        "AllGather", ALU.bypass, replica_groups=[[0, 1], [2, 3], [4, 5], [6, 7]],
        ins=[kvv.ap()], outs=[vall.ap()]), waits=vst_pts, inc=(s_agv, 1))

    d = dbg("kvk", [D + 1, TOK], BF16)
    if d is not None:
        p.step("gpsimd", lambda g, d=d: g.dma_start(out=d.ap(), in_=kvk.ap()),
               waits=kvw_pts, dma=True)
    d = dbg("kvv", [TOK, D], BF16)
    if d is not None:
        p.step("gpsimd", lambda g, d=d: g.dma_start(out=d.ap(), in_=kvv.ap()),
               waits=vst_pts, dma=True)

    q_pts, pt_qmm = proj_stream([wq[q] for q in range(4)], wqb_c, qT, FC,
                                lambda c: xT[:, c], pt_xT, first_war=[pt_vlmm],
                                bias_wait=bias_pt["wqb"])
    qnmm_last = pt_knT
    for h in range(H):
        pt_sq = p.step("scalar", (lambda h=h: lambda sc: sc.activation(
            ksqb[:, :], qT[:, h], AF.Square, bias=fp(0.0), scale=fp(1.0)))(),
            waits=[q_pts[h], qnmm_last])
        red_pts = []
        for piece in range(2):
            cs = slice(piece * 512, (piece + 1) * 512)
            bi, slot, bwaits = aux.take()
            pt_nm = p.step("tensor", (lambda cs=cs, slot=slot: lambda te: te.matmul(
                slot[0:1, 0:512], ones_bf[:, 0:1], ksqb[:, cs], start=True, stop=True))(),
                waits=[pt_sq] + bwaits)
            rpt = p.step("vector", (lambda piece=piece, slot=slot: lambda v: v.reduce_max(
                mtmp[0:1, 8 + piece:9 + piece], slot[0:1, 0:512], axis=AX.X))(),
                waits=[pt_nm] + ([nrm_comb[-1]] if nrm_comb else []))
            aux.release(bi, rpt)
            red_pts.append(rpt)
        qnmm_last = p.pt("tensor")
        nrm_comb.append(p.step("vector", (lambda h=h: lambda v: v.tensor_tensor(
            qn_row[0:1, h:h + 1], mtmp[0:1, 8:9], mtmp[0:1, 9:10], ALU.max))(), waits=red_pts))
    pt_qnorm = p.pt("vector")
    pt_vones = p.step("gpsimd", lambda g: g.memset(vbuf[:, :, :, 128:129], 1.0),
                      waits=[pt_qmm])

    d = dbg("qT", [128, H, TOK], BF16)
    if d is not None:
        p.step("sync", lambda sy, d=d: sy.dma_start(out=d.ap(), in_=qT[:, :, :]),
               waits=[p.pt("vector")], dma=True)

    pt_m1 = p.step("sync", lambda sy: sy.dma_start(out=mscr[0:1, 0:16], in_=kall[D:D + 1, 0:16]),
                   waits=[(s_agk, 1)], dma=True)
    pt_m2 = p.step("sync", lambda sy: sy.dma_start(out=mscr[0:1, 16:32], in_=kall[2 * D + 1:2 * D + 2, 0:16]),
                   waits=[(s_agk, 1)], dma=True)
    pt_m3 = p.step("vector", lambda v: v.reduce_max(mtmp[0:1, 0:1], mscr[0:1, 0:32], axis=AX.X),
                   waits=[pt_m1, pt_m2])
    pt_m4 = p.step("vector", lambda v: v.reduce_max(mtmp[0:1, 1:2], qn_row[0:1, 0:16], axis=AX.X),
                   waits=[pt_qnorm])
    pt_m5 = p.step("vector", lambda v: v.tensor_tensor(mtmp[0:1, 2:3], mtmp[0:1, 0:1],
                                                       mtmp[0:1, 1:2], ALU.mult),
                   waits=[pt_m3, pt_m4])
    pt_m6 = p.step("scalar", lambda sc: sc.activation(mtmp[0:1, 3:4], mtmp[0:1, 2:3], AF.Ln,
                                                      bias=fp(1e-38), scale=fp(1.0)), waits=[pt_m5])
    pt_m7 = p.step("scalar", lambda sc: sc.activation(mtmp[0:1, 4:5], mtmp[0:1, 3:4], AF.Exp,
                                                      bias=fp(0.0), scale=fp(0.5)), waits=[pt_m6])
    pt_m8 = p.step("vector", lambda v: v.tensor_scalar_mul(mtmp[0:1, 5:6], mtmp[0:1, 4:5],
                                                           -SCALE), waits=[pt_m7])
    mdram = nc.dram_tensor("mdram", [1, 1], F32)
    pt_m9 = p.step("sync", lambda sy: sy.dma_start(
        out=mdram[0:1, 0:1], in_=mtmp[0:1, 5:6]), waits=[pt_m8], dma=True)
    pt_mprep = p.step("sync", lambda sy: sy.dma_start(
        out=negm[:, 0:1], in_=mdram[0:1, 0:1].broadcast_to([128, 1])), waits=[pt_m9], dma=True)

    kb_pts = [None] * (H + 2)
    vb_pts = [None] * (H + 2)
    scores_done = [None] * H
    av_done = [None] * H
    avr_mu = [None] * NT
    ostg_war = [None] * NT
    exp_done = [None] * H
    ao_pts = []

    head_ctx = {}

    def emit_av(h):
        par, vpts, pt_mask = head_ctx[h]
        mu_pts = []
        for j in range(NT):
            bi, slot, bwaits = mm.take()
            def mkav(par=par, j=j, slot=slot):
                def emit(te):
                    ins = None
                    for c in range(2 * j + 2):
                        ins = te.matmul(slot[:, 0:129], eSB[:, par, c, j * 128:(j + 1) * 128],
                                        vbuf[:, par, c, 0:129],
                                        start=(c == 0), stop=(c == 2 * j + 1))
                    return ins
                return emit
            p.step("tensor", mkav(), waits=[pt_mask] + vpts + bwaits + [pt_vones])
            avpt = p.pt("tensor")
            pt_r = p.step("vector", (lambda j=j, slot=slot: lambda v: v.reciprocal(
                avr[:, j:j + 1], slot[:, 128:129]))(),
                waits=[avpt] + ([avr_mu[j]] if avr_mu[j] is not None else []))
            pt_mu = p.step("vector", (lambda j=j, slot=slot: lambda v: v.tensor_scalar_mul(
                ostg[:, j], slot[:, 0:128], avr[:, j:j + 1]))(),
                waits=[pt_r] + ([ostg_war[j]] if ostg_war[j] is not None else []))
            avr_mu[j] = pt_mu
            mm.release(bi, pt_mu)
            mu_pts.append(pt_mu)
        ti, tslot, twaits = tp.take()
        def mktr(h=h, tslot=tslot):
            def emit(te):
                ins = None
                for j in range(NT):
                    ins = te.transpose(tslot[:, j * 128:(j + 1) * 128], ostg[:, j],
                                       ident[:, 0:128])
                return ins
            return emit
        pt_tr = p.step("tensor", mktr(), waits=mu_pts + twaits)
        for j in range(NT):
            ostg_war[j] = pt_tr
        pt_ev = p.step("vector", (lambda h=h, tslot=tslot: lambda v: v.tensor_copy(
            aoT[:, h, :], tslot[:, 0:1024]))(), waits=[pt_tr])
        tp.release(ti, pt_ev)
        ao_pts.append((pt_tr, pt_ev))
        av_done[h] = p.pt("tensor")

    for h in range(H):
        par = h % 2
        kwar = [scores_done[h - 2]] if h >= 2 else [pt_qmm]
        vwar = [av_done[h - 2]] if h >= 2 else [pt_qmm]
        kv = kbuf[:, par].rearrange("p (c j) -> p c j", j=128)
        kpts = []
        vpts = []
        for r in range(2):
            kpts.append(p.step("sync", (lambda h=h, r=r, kv=kv: lambda sy: sy.dma_start(
                out=kv[:, r::2, :],
                in_=kall[r * (D + 1) + h * 128: r * (D + 1) + (h + 1) * 128, :]
                .rearrange("p (a b) -> p a b", b=128)))(),
                waits=[(s_agk, 1)] + kwar, dma=True))
            vpts.append(p.step("sync", (lambda h=h, r=r, par=par: lambda sy: sy.dma_start(
                out=vbuf[:, par, r::2, 0:128],
                in_=vall[r * TOK:(r + 1) * TOK, h * 128:(h + 1) * 128]
                .rearrange("(a p) d -> p a d", p=128)))(),
                waits=[(s_agv, 1)] + vwar, dma=True))
        kb_pts[h] = kpts
        vb_pts[h] = vpts

        esb_war = [av_done[h - 2]] if h >= 2 else []
        chunk_exp_pts = []
        for c in range(KC):
            off = jmin(c) * 128
            cpts = []
            while off < TOK:
                wd = min(512, TOK - off)
                bi, slot, bwaits = mm.take()
                p.step("tensor", (lambda par=par, c=c, off=off, wd=wd, slot=slot, h=h: lambda te: te.matmul(
                    slot[:, 0:wd], kbuf[:, par, c * 128:(c + 1) * 128], qT[:, h, off:off + wd],
                    start=True, stop=True))(), waits=kpts + [q_pts[h]] + bwaits)
                mmpt = p.pt("tensor")
                ept = p.step("scalar", (lambda par=par, c=c, off=off, wd=wd, slot=slot: lambda sc: sc.activation(
                    eSB[:, par, c, off:off + wd], slot[:, 0:wd], AF.Exp,
                    bias=negm[:, 0:1], scale=fp(SCALE)))(),
                    waits=[mmpt, pt_mprep] + esb_war)
                mm.release(bi, ept)
                cpts.append(ept)
                off += wd
            chunk_exp_pts.append(cpts)
        scores_done[h] = p.pt("tensor")
        exp_done[h] = p.pt("scalar")

        def mkmask(par=par):
            def emit(v):
                ins = None
                for c in range(KC):
                    m = mlo if c % 2 == 0 else mhi
                    o = jmin(c) * 128
                    ins = v.tensor_tensor(eSB[:, par, c, o:o + 128], eSB[:, par, c, o:o + 128],
                                          m[:, 0:128], ALU.mult)
                return ins
            return emit
        pt_mask = p.step("vector", mkmask(), waits=[exp_done[h], pt_masks])

        if h == 0:
            dkb = dbg("kbuf0", [128, 2048], BF16)
            if dkb is not None:
                p.step("sync", (lambda dkb=dkb: lambda sy: sy.dma_start(
                    out=dkb.ap(), in_=kbuf[:, 0]))(), waits=kpts, dma=True)
            dka = dbg("kall", [2 * (D + 1), TOK], BF16)
            if dka is not None:
                p.step("gpsimd", (lambda dka=dka: lambda g: g.dma_start(
                    out=dka.ap(), in_=kall.ap()))(), waits=[(s_agk, 1)], dma=True)
            d0 = dbg("eS0", [128, KC, TOK], BF16)
            if d0 is not None:
                p.step("sync", (lambda d0=d0: lambda sy: sy.dma_start(
                    out=d0.ap(), in_=eSB[:, 0]))(), waits=[pt_mask], dma=True)
            dm = dbg("masks", [128, 3, 128], BF16)
            if dm is not None:
                p.step("sync", (lambda dm=dm: lambda sy: sy.dma_start(
                    out=dm.ap()[:, 0], in_=tri[:, 0:128]))(), waits=[pt_mask], dma=True)
                p.step("sync", (lambda dm=dm: lambda sy: sy.dma_start(
                    out=dm.ap()[:, 1], in_=mlo[:, 0:128]))(), waits=[pt_mask], dma=True)
                p.step("sync", (lambda dm=dm: lambda sy: sy.dma_start(
                    out=dm.ap()[:, 2], in_=mhi[:, 0:128]))(), waits=[pt_mask], dma=True)
            dn_ = dbg("negm", [1, 16])
            if dn_ is not None:
                p.step("sync", (lambda dn_=dn_: lambda sy: sy.dma_start(
                    out=dn_.ap(), in_=mtmp[0:1, 0:16]))(), waits=[pt_mprep], dma=True)
            dnc = dbg("negmcol", [128, 1])
            if dnc is not None:
                p.step("sync", (lambda dnc=dnc: lambda sy: sy.dma_start(
                    out=dnc.ap(), in_=negm[:, 0:1]))(), waits=[pt_mprep], dma=True)
        head_ctx[h] = (par, vpts, pt_mask)
        if h >= 1:
            emit_av(h - 1)
    emit_av(H - 1)
    pt_attn_v = p.pt("vector")

    d = dbg("aoT", [128, H, TOK], BF16)
    if d is not None:
        p.step("sync", lambda sy, d=d: sy.dma_start(out=d.ap(), in_=aoT[:, :, :]),
               waits=[pt_attn_v], dma=True)
    d = dbg("aoTb", [128, H, TOK], BF16)
    if d is not None:
        p.step("sync", lambda sy, d=d: sy.dma_start(out=d.ap(), in_=aoT[:, :, :]),
               waits=[pt_attn_v], dma=True)

    pt_wob = p.step("sync", lambda sy: sy.dma_start(
        out=wob_rep[:, :], in_=wob[None, :].broadcast_to([128, 2048])),
        waits=[scores_done[H - 1]], dma=True)

    da2 = dbg("aoT2", [128, H, TOK], BF16)
    if da2 is not None:
        p.step("sync", (lambda da2=da2: lambda sy: sy.dma_start(
            out=da2.ap(), in_=aoT[:, :, :]))(), waits=[pt_attn_v], dma=True)

    wo_ev = [[None] * 4 for _ in range(NT)]
    woq_mm = []
    for q in range(4):
        par = q % 2
        wwaits = [woq_mm[q - 2]] if q >= 2 else [av_done[H - 1]]
        wpt = p.step("gpsimd", (lambda q=q, par=par: lambda g: g.dma_start(
            out=wbuf[:, par], in_=wo[q]))(), waits=wwaits, dma=True)
        for t in range(NT):
            bi, slot, bwaits = mm.take()
            def mkwo(t=t, par=par, slot=slot):
                def emit(te):
                    ins = None
                    for hh in range(FC):
                        ins = te.matmul(slot[:, 0:512], aoT[:, hh, t * 128:(t + 1) * 128],
                                        wbuf[:, par, hh, :],
                                        start=(hh == 0), stop=(hh == FC - 1))
                    return ins
                return emit
            p.step("tensor", mkwo(), waits=[wpt, pt_attn_v] + bwaits)
            mmpt = p.pt("tensor")
            ept = p.step("vector", (lambda t=t, q=q, slot=slot: lambda v: v.tensor_tensor(
                ybuf[:, t, q * 512:(q + 1) * 512], slot[:, 0:512],
                wob_rep[:, q * 512:(q + 1) * 512], ALU.add))(), waits=[mmpt, pt_wob])
            mm.release(bi, ept)
            wo_ev[t][q] = ept
        woq_mm.append(p.pt("tensor"))

    lg_pts = []
    xl_mm = [None, None]

    def emit_gate_tile(t, gate_in_waits):
        gi, gslot, gwaits = aux.take()
        for cq in range(4):
            qpar = cq % 2
            bi, slot, bwaits = mm.take()
            tr_last = None
            for i in range(4):
                c = cq * 4 + i
                p.step("tensor", (lambda t=t, c=c, i=i, slot=slot: lambda te: te.transpose(
                    slot[:, i * 128:(i + 1) * 128].bitcast(F32),
                    ybuf[:, t, c * 128:(c + 1) * 128],
                    ident32[:, 0:128]))(), waits=gate_in_waits + [pt_prep] + (bwaits if i == 0 else []))
            tr_last = p.pt("tensor")
            war = [xl_mm[qpar]] if xl_mm[qpar] is not None else []
            pev = p.step("vector", (lambda qpar=qpar, slot=slot: lambda v: v.tensor_copy(
                xl32[:, qpar], slot[:, 0:512]))(), waits=[tr_last] + war)
            mm.release(bi, pev)
            def mkg(cq=cq, qpar=qpar, gslot=gslot):
                def emit(te):
                    ins = None
                    for i in range(4):
                        c = cq * 4 + i
                        ins = te.matmul(gslot[0:8, 0:128], gate_sb[:, c, :],
                                        xl32[:, qpar, i * 128:(i + 1) * 128],
                                        start=(c == 0), stop=(c == FC - 1))
                    return ins
                return emit
            p.step("tensor", mkg(), waits=[pev, pt_gate] + (gwaits if cq == 0 else []))
            xl_mm[qpar] = p.pt("tensor")
        mmpt = p.pt("tensor")
        ept = p.step("vector", (lambda t=t, gslot=gslot: lambda v: v.tensor_copy(
            logits32[0:8, t * 128:(t + 1) * 128], gslot[0:8, 0:128]))(), waits=[mmpt])
        aux.release(gi, ept)
        lg_pts.append(ept)


    ln1_pts = []
    xtf_tr_pts = []
    for t in range(NT):
        par = t % 2
        xwar = [ln1_pts[t - 2][3]] if t >= 2 else [scores_done[H - 1]]
        pt_xr = p.step("sync", (lambda t=t, par=par: lambda sy: sy.dma_start(
            out=xrb[:, par], in_=xin[t * 128:(t + 1) * 128, :]))(),
            waits=xwar, dma=True)
        pt_add = p.step("vector", (lambda t=t, par=par: lambda v: v.scalar_tensor_tensor(
            ybuf[:, t, :], ybuf[:, t, :], 0.0, xrb[:, par, :], ALU.add, ALU.add,
            accum_out=stats2[:, t:t + 1]))(), waits=wo_ev[t] + [pt_xr])
        ev_pts = [pt_add]
        dy1 = dbg("y1d", [TOK, D])
        if dy1 is not None:
            dpt = p.step("sync", (lambda t=t, dy1=dy1: lambda sy: sy.dma_start(
                out=dy1.ap()[t * 128:(t + 1) * 128, :], in_=ybuf[:, t]))(),
                waits=ev_pts, dma=True)
            ev_pts = ev_pts + [dpt]
        pt_s2 = p.step("vector", (lambda t=t: lambda v: v.tensor_scalar_mul(
            stats2[:, t:t + 1], stats2[:, t:t + 1], INVD))(), waits=ev_pts)
        sq_pts = []
        for ob in range(4):
            bi, slot, bwaits = aux.take()
            spt = p.step("scalar", (lambda t=t, ob=ob, slot=slot: lambda sc: sc.activation(
                slot[:, 0:512], ybuf[:, t, ob * 512:(ob + 1) * 512], AF.Square,
                bias=fp(0.0), scale=fp(1.0),
                accum_out=stats[:, 32 + t * 4 + ob:32 + t * 4 + ob + 1]))(),
                waits=ev_pts + bwaits)
            aux.release(bi, spt)
            sq_pts.append(spt)
        pt_s3 = p.step("vector", (lambda t=t: lambda v: v.reduce_sum(
            stats2[:, 16 + t:17 + t], stats[:, 32 + t * 4:32 + t * 4 + 4], axis=AX.X))(), waits=sq_pts)
        pt_s5 = p.step("vector", (lambda t=t: lambda v: v.tensor_scalar_mul(
            stats2[:, 16 + t:17 + t], stats2[:, 16 + t:17 + t], INVD))(), waits=[pt_s3])
        pt_s6 = p.step("vector", (lambda t=t: lambda v: v.tensor_tensor(
            stats2[:, 32 + t:33 + t], stats2[:, t:t + 1], stats2[:, t:t + 1], ALU.mult))(),
            waits=[pt_s2])
        pt_s7 = p.step("vector", (lambda t=t: lambda v: v.tensor_tensor(
            stats2[:, 16 + t:17 + t], stats2[:, 16 + t:17 + t], stats2[:, 32 + t:33 + t],
            ALU.subtract))(), waits=[pt_s5, pt_s6])
        pt_s8 = p.step("scalar", (lambda t=t: lambda sc: sc.activation(
            stats2[:, 32 + t:33 + t], stats2[:, 16 + t:17 + t], AF.Ln, bias=fp(EPS),
            scale=fp(1.0)))(), waits=[pt_s7])
        pt_s9 = p.step("scalar", (lambda t=t: lambda sc: sc.activation(
            stats2[:, 16 + t:17 + t], stats2[:, 32 + t:33 + t], AF.Exp, bias=fp(0.0),
            scale=fp(-0.5)))(), waits=[pt_s8])
        nrm_pts = []
        for ob in range(4):
            npt = p.step("vector", (lambda t=t, ob=ob: lambda v: v.tensor_scalar(
                ybuf[:, t, ob * 512:(ob + 1) * 512], ybuf[:, t, ob * 512:(ob + 1) * 512],
                stats2[:, t:t + 1], stats2[:, 16 + t:17 + t], ALU.subtract, ALU.mult))(),
                waits=[pt_s9] + sq_pts)
            nrm_pts.append(npt)
        emit_gate_tile(t, nrm_pts)
        pt_sp = p.step("sync", (lambda t=t: lambda sy: sy.dma_start(
            out=xspill[t * 128:(t + 1) * 128, :], in_=ybuf[:, t]))(), waits=nrm_pts, dma=True)
        cwar = [xtf_tr_pts[-3]] if len(xtf_tr_pts) >= 3 else [av_done[H - 1]]
        pt_cast = p.step("vector", (lambda t=t, par=par: lambda v: v.tensor_copy(
            xcastb[:, par], ybuf[:, t]))(), waits=nrm_pts + cwar)
        tprs = []
        for g8 in range(2):
            ti, tslot, twaits = tp.take()
            def mktr(par=par, g8=g8, tslot=tslot):
                def emit(te):
                    ins = None
                    for k in range(8):
                        c = g8 * 8 + k
                        ins = te.transpose(tslot[:, k * 128:(k + 1) * 128],
                                           xcastb[:, par, c * 128:(c + 1) * 128],
                                           ident[:, 0:128])
                    return ins
                return emit
            ptr = p.step("tensor", mktr(), waits=[pt_cast] + twaits)
            pev = p.step("vector", (lambda t=t, g8=g8, tslot=tslot: lambda v: v.tensor_copy(
                xTf[:, g8 * 8:(g8 + 1) * 8, t * 128:(t + 1) * 128],
                tslot.rearrange("p (a b) -> p a b", b=128)))(), waits=[ptr])
            tp.release(ti, pev)
            xtf_tr_pts.append(ptr)
            tprs.append(pev)
        ln1_pts.append((pt_sp, tprs[-1], pt_cast, p.pt("vector")))
    pt_xTf = p.pt("vector")
    pt_xTf_T = p.pt("tensor")

    d = dbg("xln1", [TOK, D])
    if d is not None:
        p.step("gpsimd", lambda g, d=d: g.dma_start(out=d.ap(), in_=xspill.ap()),
               waits=[pt for (pt, _, _, _) in ln1_pts], dma=True)

    pb_pts = []
    for piece in range(2):
        cs = slice(piece * 512, (piece + 1) * 512)
        bi, slot, bwaits = aux.take()
        p.step("tensor", (lambda cs=cs, slot=slot: lambda te: te.matmul(
            slot[0:64, 0:512], a2[0:8, 0:64], logits32[0:8, cs], start=True, stop=True))(),
            waits=lg_pts + bwaits + [pt_prep])
        mmpt = p.pt("tensor")
        ept = p.step("vector", (lambda cs=cs, slot=slot: lambda v: v.tensor_copy(
            pbs32[0:64, cs], slot[0:64, 0:512]))(), waits=[mmpt])
        aux.release(bi, ept)
        pb_pts.append(ept)
    sc_pts = []
    for t in range(NT):
        bi, slot, bwaits = mm.take()
        ptr = p.step("tensor", (lambda t=t, slot=slot: lambda te: te.transpose(
            slot[:, 0:64].bitcast(F32), pbs32[0:64, t * 128:(t + 1) * 128],
            ident32[0:64, 0:64]))(), waits=pb_pts + bwaits)
        pcv = p.step("vector", (lambda slot=slot: lambda v: v.tensor_copy(
            gmax[:, 0:64], slot[:, 0:64]))(), waits=[ptr] + (sc_pts[-1:] if sc_pts else []))
        mm.release(bi, pcv)
        pms = p.step("vector", lambda v: v.memset(gmax[:, 0:64:9], -1e30), waits=[pcv])
        prd = p.step("vector", (lambda t=t: lambda v: v.reduce_max(
            scale_c[:, t:t + 1], gmax[:, 0:64], axis=AX.X))(), waits=[pms])
        sc_pts.append(prd)
    pt_gate_all = p.pt("vector")

    p7_last = [pt_xTf_T, pt_xTf_T]
    y2_pts = [[None] * 4 for _ in range(NT)]

    def emit_expert():
        for q in range(4):
            par = q % 2
            wpt = p.step("gpsimd", (lambda q=q, par=par: lambda g: g.dma_start(
                out=p7buf[:, par], in_=exw[q]))(), waits=[p7_last[par]], dma=True)
            for t in range(NT):
                bi, slot, bwaits = mm.take()
                def mke(t=t, par=par, slot=slot):
                    def emit(te):
                        ins = None
                        for c in range(FC):
                            ins = te.matmul(slot[:, 0:512], xTf[:, c, t * 128:(t + 1) * 128],
                                            p7buf[:, par, c, :], start=(c == 0), stop=(c == FC - 1))
                        return ins
                    return emit
                p.step("tensor", mke(), waits=[wpt, pt_xTf] + bwaits)
                mmpt = p.pt("tensor")
                ept = p.step("vector", (lambda t=t, q=q, slot=slot: lambda v: v.tensor_scalar_mul(
                    ybuf[:, t, q * 512:(q + 1) * 512], slot[:, 0:512], scale_c[:, t:t + 1]))(),
                    waits=[mmpt, sc_pts[t], ln1_pts[t][0], ln1_pts[t][2]])
                mm.release(bi, ept)
                y2_pts[t][q] = ept
            p7_last[par] = p.pt("tensor")

    pt_gam = p.step("sync", lambda sy: sy.dma_start(
        out=gam_rep[:, :], in_=lng[None, :].broadcast_to([128, 2048])),
        waits=[pt_xTf_T], dma=True)
    pt_bet = p.step("sync", lambda sy: sy.dma_start(
        out=bet_rep[:, :], in_=lnb[None, :].broadcast_to([128, 2048])),
        waits=[pt_xTf_T], dma=True)

    hq_pts = [None] * 16
    hq_extra_war = []
    for iq in range(4):
        for ib in range(8):
            par = ib % 2
            wpt = p.step("gpsimd", (lambda iq=iq, ib=ib, par=par: lambda g: g.dma_start(
                out=p7buf[:, par, :, 0:256], in_=upw[iq * 8 + ib]))(), waits=[p7_last[par]], dma=True)
            for icin in range(2):
                ic = ib * 2 + icin
                for piece in range(2):
                    cs = slice(piece * 512, (piece + 1) * 512)
                    bi, slot, bwaits = mm.take()
                    def mku(par=par, icin=icin, cs=cs, slot=slot):
                        def emit(te):
                            ins = None
                            for c in range(FC):
                                ins = te.matmul(slot[:, 0:512],
                                                p7buf[:, par, c, icin * 128:(icin + 1) * 128],
                                                xTf[:, c, cs], start=(c == 0), stop=(c == FC - 1))
                            return ins
                        return emit
                    hq_war = ([hq_pts[ic]] if hq_pts[ic] is not None else []) + hq_extra_war
                    p.step("tensor", mku(), waits=[wpt, pt_xTf] + bwaits)
                    mmpt = p.pt("tensor")
                    rpt = p.step("scalar", (lambda ic=ic, cs=cs, slot=slot: lambda sc: sc.activation(
                        hq[:, ic, cs], slot[:, 0:512], AF.Relu, bias=fp(0.0), scale=fp(1.0)))(),
                        waits=[mmpt] + hq_war)
                    qpt = p.step("vector", (lambda ic=ic, cs=cs: lambda v: v.tensor_tensor(
                        hq[:, ic, cs], hq[:, ic, cs], hq[:, ic, cs], ALU.mult))(), waits=[rpt])
                    mm.release(bi, qpt)
            p7_last[par] = p.pt("tensor")
        pt_up_done = p.pt("vector")
        if iq == 0:
            emit_expert()
        if iq == 0:
            dh_ = dbg("hq0", [128, 16, 1024], BF16)
            if dh_ is not None:
                dpt = p.step("sync", (lambda dh_=dh_: lambda sy: sy.dma_start(
                    out=dh_.ap(), in_=hq[:, :, :]))(), waits=[pt_up_done], dma=True)
                hq_extra_war.append(dpt)
        for ob in range(4):
            par = ob % 2
            wpt = p.step("gpsimd", (lambda iq=iq, ob=ob, par=par: lambda g: g.dma_start(
                out=p7buf[:, par], in_=dnw[iq, ob]))(), waits=[p7_last[par]], dma=True)
            for t in range(NT):
                bi, slot, bwaits = mm.take()
                def mkd(t=t, par=par, slot=slot):
                    def emit(te):
                        ins = None
                        for ic in range(16):
                            ins = te.matmul(slot[:, 0:512], hq[:, ic, t * 128:(t + 1) * 128],
                                            p7buf[:, par, ic, :], start=(ic == 0), stop=(ic == 15))
                        return ins
                    return emit
                p.step("tensor", mkd(), waits=[wpt, pt_up_done] + bwaits)
                mmpt = p.pt("tensor")
                if iq < 3:
                    ept = p.step("vector", (lambda t=t, ob=ob, slot=slot: lambda v: v.tensor_tensor(
                        ybuf[:, t, ob * 512:(ob + 1) * 512], ybuf[:, t, ob * 512:(ob + 1) * 512],
                        slot[:, 0:512], ALU.add))(), waits=[mmpt, y2_pts[t][ob]])
                else:
                    ept = p.step("vector", (lambda t=t, ob=ob, slot=slot: lambda v: v.scalar_tensor_tensor(
                        ybuf[:, t, ob * 512:(ob + 1) * 512], slot[:, 0:512], 0.0,
                        ybuf[:, t, ob * 512:(ob + 1) * 512], ALU.add, ALU.add,
                        accum_out=stats[:, t * 4 + ob:t * 4 + ob + 1]))(),
                        waits=[mmpt, y2_pts[t][ob]])
                mm.release(bi, ept)
                y2_pts[t][ob] = ept
            p7_last[par] = p.pt("tensor")
        hq_pts = [p.pt("tensor")] * 16

    dscl = dbg("scl", [128, 8])
    if dscl is not None:
        p.step("sync", (lambda dscl=dscl: lambda sy: sy.dma_start(
            out=dscl.ap(), in_=scale_c[:, 0:8]))(), waits=sc_pts, dma=True)

    out_pts = []
    for t in range(NT):
        par = t % 2
        evs = y2_pts[t]
        dy2 = dbg("y2d", [TOK, D])
        if dy2 is not None:
            dpt = p.step("sync", (lambda t=t, dy2=dy2: lambda sy: sy.dma_start(
                out=dy2.ap()[t * 128:(t + 1) * 128, :], in_=ybuf[:, t]))(),
                waits=evs, dma=True)
            evs = evs + [dpt]
        pt_a = p.step("vector", (lambda t=t: lambda v: v.reduce_sum(
            stats2[:, t:t + 1], stats[:, t * 4:t * 4 + 4], axis=AX.X))(), waits=evs)
        pt_b = p.step("vector", (lambda t=t: lambda v: v.tensor_scalar_mul(
            stats2[:, t:t + 1], stats2[:, t:t + 1], INVD))(), waits=[pt_a])
        sq_pts = []
        for ob in range(4):
            bi, slot, bwaits = aux.take()
            spt = p.step("scalar", (lambda t=t, ob=ob, slot=slot: lambda sc: sc.activation(
                slot[:, 0:512], ybuf[:, t, ob * 512:(ob + 1) * 512], AF.Square,
                bias=fp(0.0), scale=fp(1.0),
                accum_out=stats[:, 32 + t * 4 + ob:32 + t * 4 + ob + 1]))(),
                waits=[evs[ob]] + bwaits)
            aux.release(bi, spt)
            sq_pts.append(spt)
        pt_cv = p.step("vector", (lambda t=t: lambda v: v.reduce_sum(
            stats2[:, 16 + t:17 + t], stats[:, 32 + t * 4:32 + t * 4 + 4], axis=AX.X))(), waits=sq_pts)
        pt_c2 = p.step("vector", (lambda t=t: lambda v: v.tensor_scalar_mul(
            stats2[:, 16 + t:17 + t], stats2[:, 16 + t:17 + t], INVD))(), waits=[pt_cv])
        pt_c3 = p.step("vector", (lambda t=t: lambda v: v.tensor_tensor(
            stats2[:, 32 + t:33 + t], stats2[:, t:t + 1], stats2[:, t:t + 1], ALU.mult))(), waits=[pt_b])
        pt_c4 = p.step("vector", (lambda t=t: lambda v: v.tensor_tensor(
            stats2[:, 16 + t:17 + t], stats2[:, 16 + t:17 + t], stats2[:, 32 + t:33 + t],
            ALU.subtract))(), waits=[pt_c2, pt_c3])
        pt_c5 = p.step("scalar", (lambda t=t: lambda sc: sc.activation(
            stats2[:, 32 + t:33 + t], stats2[:, 16 + t:17 + t], AF.Ln, bias=fp(EPS),
            scale=fp(1.0)))(), waits=[pt_c4])
        pt_c6 = p.step("scalar", (lambda t=t: lambda sc: sc.activation(
            stats2[:, 16 + t:17 + t], stats2[:, 32 + t:33 + t], AF.Exp, bias=fp(0.0),
            scale=fp(-0.5)))(), waits=[pt_c5])
        xw = [out_pts[t - 2]] if t >= 2 else []
        pt_xl = p.step("sync", (lambda t=t, par=par: lambda sy: sy.dma_start(
            out=xlrb[:, par], in_=xspill[t * 128:(t + 1) * 128, :]))(),
            waits=[ln1_pts[t][0]] + xw, dma=True)
        z_pts = []
        for ob in range(4):
            cs = slice(ob * 512, (ob + 1) * 512)
            p1_ = p.step("vector", (lambda t=t, cs=cs: lambda v: v.tensor_scalar(
                ybuf[:, t, cs], ybuf[:, t, cs], stats2[:, t:t + 1], stats2[:, 16 + t:17 + t],
                ALU.subtract, ALU.mult))(), waits=[pt_c6] + sq_pts)
            p2_ = p.step("gpsimd", (lambda t=t, cs=cs: lambda g: g.tensor_tensor(
                ybuf[:, t, cs], ybuf[:, t, cs], gam_rep[:, cs], ALU.mult))(),
                waits=[p1_, pt_gam])
            p3_ = p.step("gpsimd", (lambda t=t, cs=cs: lambda g: g.tensor_tensor(
                ybuf[:, t, cs], ybuf[:, t, cs], bet_rep[:, cs], ALU.add))(),
                waits=[p2_, pt_bet])
            p4_ = p.step("vector", (lambda t=t, cs=cs, ob=ob, par=par: lambda v: v.scalar_tensor_tensor(
                ybuf[:, t, cs], ybuf[:, t, cs], 0.0, xlrb[:, par, cs], ALU.add, ALU.add,
                accum_out=stats[:, t * 4 + ob:t * 4 + ob + 1]))(), waits=[p3_, pt_xl])
            z_pts.append(p4_)
        pt_d1 = p.step("vector", (lambda t=t: lambda v: v.reduce_sum(
            stats2[:, t:t + 1], stats[:, t * 4:t * 4 + 4], axis=AX.X))(), waits=z_pts)
        pt_d2 = p.step("vector", (lambda t=t: lambda v: v.tensor_scalar_mul(
            stats2[:, t:t + 1], stats2[:, t:t + 1], INVD))(), waits=[pt_d1])
        sq2_pts = []
        for ob in range(4):
            bi, slot, bwaits = aux.take()
            spt = p.step("scalar", (lambda t=t, ob=ob, slot=slot: lambda sc: sc.activation(
                slot[:, 0:512], ybuf[:, t, ob * 512:(ob + 1) * 512], AF.Square,
                bias=fp(0.0), scale=fp(1.0),
                accum_out=stats[:, 32 + t * 4 + ob:32 + t * 4 + ob + 1]))(),
                waits=[z_pts[ob]] + bwaits)
            aux.release(bi, spt)
            sq2_pts.append(spt)
        pt_d3 = p.step("vector", (lambda t=t: lambda v: v.reduce_sum(
            stats2[:, 16 + t:17 + t], stats[:, 32 + t * 4:32 + t * 4 + 4], axis=AX.X))(), waits=sq2_pts)
        pt_d4 = p.step("vector", (lambda t=t: lambda v: v.tensor_scalar_mul(
            stats2[:, 16 + t:17 + t], stats2[:, 16 + t:17 + t], INVD))(), waits=[pt_d3])
        pt_d5 = p.step("vector", (lambda t=t: lambda v: v.tensor_tensor(
            stats2[:, 32 + t:33 + t], stats2[:, t:t + 1], stats2[:, t:t + 1], ALU.mult))(), waits=[pt_d2])
        pt_d6 = p.step("vector", (lambda t=t: lambda v: v.tensor_tensor(
            stats2[:, 16 + t:17 + t], stats2[:, 16 + t:17 + t], stats2[:, 32 + t:33 + t],
            ALU.subtract))(), waits=[pt_d4, pt_d5])
        pt_d7 = p.step("scalar", (lambda t=t: lambda sc: sc.activation(
            stats2[:, 32 + t:33 + t], stats2[:, 16 + t:17 + t], AF.Ln, bias=fp(EPS),
            scale=fp(1.0)))(), waits=[pt_d6])
        pt_d8 = p.step("scalar", (lambda t=t: lambda sc: sc.activation(
            stats2[:, 16 + t:17 + t], stats2[:, 32 + t:33 + t], AF.Exp, bias=fp(0.0),
            scale=fp(-0.5)))(), waits=[pt_d7])
        fin_pts = []
        for ob in range(4):
            cs = slice(ob * 512, (ob + 1) * 512)
            fpt = p.step("vector", (lambda t=t, cs=cs: lambda v: v.tensor_scalar(
                ybuf[:, t, cs], ybuf[:, t, cs], stats2[:, t:t + 1], stats2[:, 16 + t:17 + t],
                ALU.subtract, ALU.mult))(), waits=[pt_d8] + sq2_pts)
            fin_pts.append(fpt)
        opt = p.step("sync", (lambda t=t: lambda sy: sy.dma_start(
            out=out[t * 128:(t + 1) * 128, :], in_=ybuf[:, t]))(), waits=fin_pts, dma=True)
        out_pts.append(opt)

    p.step("sync", lambda sy: sy.nop(), waits=out_pts, inc=(p.sem["sync"], 1))

    p.emit()
    return nc, DBG


def _swz_quarters(w):
    dmat, n = w.shape
    return np.ascontiguousarray(
        w.reshape(dmat // 128, 128, n // 512, 512).transpose(2, 1, 0, 3))


def _prep_weights(i):
    wts = {}
    wts["wq"] = _swz_quarters(np.asarray(i["L0_attn_wq_w"], np.float32))
    wts["wk"] = _swz_quarters(np.asarray(i["L0_attn_wk_w"], np.float32))
    wts["wv"] = _swz_quarters(np.asarray(i["L0_attn_wv_w"], np.float32))
    wts["wo"] = _swz_quarters(np.asarray(i["L0_attn_wo_w"], np.float32))
    for nm, key in (("wkl", "L0_attn_wkl_w"), ("wvl", "L0_attn_wvl_w")):
        w = np.asarray(i[key], np.float32)
        wts[nm] = np.ascontiguousarray(w.reshape(16, 128, 512).transpose(1, 0, 2))
    for nm, key in (("upk", "L0_attn_upk_w"), ("upv", "L0_attn_upv_w")):
        w = np.asarray(i[key], np.float32)
        wts[nm] = np.ascontiguousarray(w.reshape(16, 4, 128, 128).transpose(2, 0, 1, 3))
    g = np.asarray(i["L0_ffn_gate_w"], np.float32)
    wts["gate"] = np.ascontiguousarray(g.reshape(16, 128, 8).transpose(1, 0, 2))
    up = np.asarray(i["L0_ffn_up_w"], np.float32)
    wts["upw"] = np.ascontiguousarray(up.reshape(16, 128, 32, 256).transpose(2, 1, 0, 3))
    dn = np.asarray(i["L0_ffn_down_w"], np.float32)
    wts["dnw"] = np.ascontiguousarray(dn.reshape(4, 16, 128, 4, 512).transpose(0, 3, 2, 1, 4))
    wts["exw"] = _swz_quarters(np.asarray(i["L0_ffn_expert_weight"], np.float32))
    for nm, key in (("wqb", "L0_attn_wq_b"), ("wkb", "L0_attn_wk_b"), ("wvb", "L0_attn_wv_b")):
        wts[nm] = np.ascontiguousarray(np.asarray(i[key], np.float32).reshape(16, 128).T)
    for nm, key in (("wklb", "L0_attn_wkl_b"), ("wvlb", "L0_attn_wvl_b")):
        wts[nm] = np.ascontiguousarray(np.asarray(i[key], np.float32).reshape(4, 128).T)
    wts["upkb"] = np.ascontiguousarray(np.asarray(i["L0_attn_upk_b"], np.float32).T)
    wts["upvb"] = np.ascontiguousarray(np.asarray(i["L0_attn_upv_b"], np.float32).reshape(-1))
    wts["wob"] = np.ascontiguousarray(np.asarray(i["L0_attn_wo_b"], np.float32))
    wts["lng"] = np.ascontiguousarray(np.asarray(i["L0_ffn_ln_gamma"], np.float32))
    wts["lnb"] = np.ascontiguousarray(np.asarray(i["L0_ffn_ln_beta"], np.float32))
    return wts


def kernel(**inputs):
    x = np.asarray(inputs["L0_attn_x"], np.float32)
    wts = _prep_weights(inputs)
    nc, _ = build()
    in_maps = []
    for core in range(8):
        b, a = core // 2, core % 2
        xs = np.ascontiguousarray(x[b].reshape(16, 128, D)[a::2].reshape(TOK, D))
        m = {"xin": xs, "parity": np.array([a], np.float32)}
        m.update(wts)
        in_maps.append(m)
    res = run_bass_kernel_spmd(nc, in_maps, core_ids=list(range(8)))
    outp = np.empty((4, 2048, D), np.float32)
    for core in range(8):
        b, a = core // 2, core % 2
        outp[b].reshape(16, 128, D)[a::2] = res.results[core]["out"].reshape(NT, 128, D)
    return outp


# version 12
# speedup vs baseline: 1.0798x; 1.0798x over previous
import numpy as np
import concourse.bass as bass
import concourse.mybir as mybir
from concourse.bass_utils import run_bass_kernel_spmd

F32 = mybir.dt.float32
BF16 = mybir.dt.bfloat16
AF = mybir.ActivationFunctionType
ALU = mybir.AluOpType
AX = mybir.AxisListType
fp = np.float32

D = 2048
FC = 16
TOK = 1024
NT = 8
H = 16
DH = 128
DL = 512
NDL = 4
S = 2048
KC = 16
INTER = 8192
SCALE = float(1.0 / np.sqrt(np.float32(128.0)))
EPS = 1e-5
INVD = float(np.float32(1.0) / np.float32(2048.0))


def jmin(c):
    return c // 2


class Prog:
    ENGS = ("tensor", "vector", "scalar", "gpsimd", "sync")

    def __init__(self, nc):
        self.nc = nc
        self.steps = []
        self.sem = {}
        for e in self.ENGS:
            self.sem[e] = _S(nc.alloc_semaphore("sm_" + e))
        self.NQ = 4
        self.qsems = {"sync": [_S(nc.alloc_semaphore(f"sm_dsync{i}")) for i in range(self.NQ)],
                      "gpsimd": [_S(nc.alloc_semaphore(f"sm_dgps{i}")) for i in range(self.NQ)]}
        self.qidx = {"sync": 0, "gpsimd": 0}

    def extra_sem(self, name):
        return _S(self.nc.alloc_semaphore(name))

    def step(self, eng, emit, waits=(), dma=False, inc=None):
        extra = []
        if inc is not None:
            s, amt = inc
        elif dma:
            i = self.qidx[eng] % self.NQ
            self.qidx[eng] += 1
            s = self.qsems[eng][i]
            amt = 16
            if s.n > 0:
                extra = [(s, s.n)]
        else:
            s = self.sem[eng]
            amt = 1
        s.n += amt
        pt = (s, s.n)
        merged = {}
        for (ws, wv) in list(waits) + extra:
            if wv is None or wv <= 0:
                continue
            if id(ws) in merged:
                merged[id(ws)] = (ws, max(merged[id(ws)][1], wv))
            else:
                merged[id(ws)] = (ws, wv)
        self.steps.append((eng, emit, list(merged.values()), (s, amt)))
        return pt

    def pt(self, eng):
        key = eng
        return (self.sem[key], self.sem[key].n)

    def emit(self):
        nc = self.nc
        with nc.Block() as block:
            @block.tensor
            def _(e):
                self._run("tensor", e)

            @block.vector
            def _(e):
                self._run("vector", e)

            @block.scalar
            def _(e):
                self._run("scalar", e)

            @block.gpsimd
            def _(e):
                self._run("gpsimd", e)

            @block.sync
            def _(e):
                self._run("sync", e)

    def _run(self, name, eng):
        for (e, emit, waits, inc) in self.steps:
            if e != name:
                continue
            for (s, v) in waits:
                eng.wait_ge(s.h, v)
            ins = emit(eng)
            assert ins is not None, f"step on {name} returned no instruction"
            ins.then_inc(inc[0].h, inc[1])


class _S:
    __slots__ = ("h", "n")

    def __init__(self, handle):
        self.h = handle
        self.n = 0


class PsumPool:

    def __init__(self, prog, slots):
        self.p = prog
        self.slots = slots
        self.rel = [None] * len(slots)
        self.i = 0

    def take(self):
        i = self.i % len(self.slots)
        self.i += 1
        waits = [self.rel[i]] if self.rel[i] is not None else []
        return i, self.slots[i], waits

    def release(self, i, pt):
        self.rel[i] = pt


def build(debug=()):
    nc = bass.Bass("TRN2", target_bir_lowering=False)
    p = Prog(nc)

    xin = nc.dram_tensor("xin", [TOK, D], F32, kind="ExternalInput")
    wq = nc.dram_tensor("wq", [4, 128, FC, 512], F32, kind="ExternalInput")
    wk = nc.dram_tensor("wk", [4, 128, FC, 512], F32, kind="ExternalInput")
    wv = nc.dram_tensor("wv", [4, 128, FC, 512], F32, kind="ExternalInput")
    wo = nc.dram_tensor("wo", [4, 128, FC, 512], F32, kind="ExternalInput")
    wkl = nc.dram_tensor("wkl", [128, FC, DL], F32, kind="ExternalInput")
    wvl = nc.dram_tensor("wvl", [128, FC, DL], F32, kind="ExternalInput")
    upk = nc.dram_tensor("upk", [128, H, NDL, 128], F32, kind="ExternalInput")
    upv = nc.dram_tensor("upv", [128, H, NDL, 128], F32, kind="ExternalInput")
    gate = nc.dram_tensor("gate", [128, FC, 8], F32, kind="ExternalInput")
    upw = nc.dram_tensor("upw", [32, 128, FC, 256], F32, kind="ExternalInput")
    dnw = nc.dram_tensor("dnw", [4, 4, 128, FC, 512], F32, kind="ExternalInput")
    exw = nc.dram_tensor("exw", [4, 128, FC, 512], F32, kind="ExternalInput")
    wqb = nc.dram_tensor("wqb", [128, FC], F32, kind="ExternalInput")
    wkb = nc.dram_tensor("wkb", [128, FC], F32, kind="ExternalInput")
    wvb = nc.dram_tensor("wvb", [128, FC], F32, kind="ExternalInput")
    wklb = nc.dram_tensor("wklb", [128, NDL], F32, kind="ExternalInput")
    wvlb = nc.dram_tensor("wvlb", [128, NDL], F32, kind="ExternalInput")
    upkb = nc.dram_tensor("upkb", [128, H], F32, kind="ExternalInput")
    upvb = nc.dram_tensor("upvb", [D], F32, kind="ExternalInput")
    wob = nc.dram_tensor("wob", [D], F32, kind="ExternalInput")
    lng = nc.dram_tensor("lng", [D], F32, kind="ExternalInput")
    lnb = nc.dram_tensor("lnb", [D], F32, kind="ExternalInput")
    parity = nc.dram_tensor("parity", [1], F32, kind="ExternalInput")
    out = nc.dram_tensor("out", [TOK, D], F32, kind="ExternalOutput")

    kvk = nc.dram_tensor("kvk", [D + 1, TOK], BF16)
    kall = nc.dram_tensor("kall", [2 * (D + 1), TOK], BF16)
    kvv = nc.dram_tensor("kvv", [TOK, D], BF16)
    vall = nc.dram_tensor("vall", [S, D], BF16)
    xspill = nc.dram_tensor("xspill", [TOK, D], F32)

    DBG = {}

    def dbg(name, shape, dtype=F32):
        if name in debug:
            if name not in DBG:
                DBG[name] = nc.dram_tensor("dbg_" + name, shape, dtype, kind="ExternalOutput")
            return DBG[name]
        return None

    ARENA = 210112
    arena = nc.alloc_sbuf_tensor("arena", [128, ARENA], mybir.dt.int8).ap()

    def carve(off, dtype, shape):
        n = int(np.prod(shape)) * mybir.dt.size(dtype)
        assert off % 64 == 0 and off + n <= ARENA, (off, n)
        v = arena[:, off:off + n].bitcast(dtype)
        if len(shape) > 1:
            kw = {f"a{i}": shape[i] for i in range(len(shape))}
            v = v.rearrange("p (" + " ".join(kw) + ") -> p " + " ".join(kw), **kw)
        return v

    ident = carve(0, BF16, [128])
    tri = carve(256, BF16, [128])
    mlo = carve(512, BF16, [128])
    mhi = carve(768, BF16, [128])
    ones_bf = carve(1024, BF16, [1024])
    ident32 = carve(3072, F32, [128])
    gate_sb = carve(3584, F32, [FC, 8])
    wqb_c = carve(4096, F32, [16])
    wkb_c = carve(4160, F32, [16])
    wvb_c = carve(4224, F32, [16])
    wklb_c = carve(4288, F32, [16])
    wvlb_c = carve(4352, F32, [16])
    upkb_c = carve(4416, F32, [16])
    a2 = carve(4480, F32, [64])
    upvb_rep = carve(4736, BF16, [2048])
    qn_row = carve(8832, F32, [16])
    knmax_row = carve(8896, BF16, [32])
    mscr = carve(8960, BF16, [32])
    mtmp = carve(9024, F32, [16])
    negm = carve(9088, F32, [16])
    avr = carve(9152, F32, [16])
    stats = carve(9216, F32, [64])
    stats2 = carve(9472, F32, [64])
    gmax = carve(9728, F32, [64])
    scale_c = carve(9984, F32, [16])
    par_sb = carve(10048, F32, [16])
    zrow = carve(10112, BF16, [1024])

    R1 = 13504
    xld = carve(R1, F32, [2, 2048])
    xldb = carve(R1 + 16384, BF16, [2, 2048])
    kstg = carve(R1, BF16, [2, 1024])
    vstg = carve(R1 + 4096, BF16, [2, 8, 128])
    ksqb = carve(R1 + 12288, BF16, [1024])
    t1T = carve(R1 + 24576, BF16, [FC, 1024])
    latT = carve(R1 + 57344, BF16, [NDL, 1024])
    eSB = carve(R1, BF16, [2, KC, 1024])
    ybuf = carve(R1, F32, [NT, 2048])

    R2 = R1 + 65536
    xT = carve(R2, BF16, [FC, 1024])
    xTf = xT

    R3 = R2 + 32768
    qT = carve(R3, BF16, [H, 1024])
    wob_rep = carve(R3, F32, [2048])
    xrb = carve(R3 + 8192, F32, [2, 2048])
    xcastb = carve(R3 + 24576, BF16, [2, 2048])
    hq = carve(R3, BF16, [16, 1024])
    logits32 = carve(R3, F32, [1024])
    pbs32 = carve(R3 + 4096, F32, [1024])


    R4 = R3 + 32768
    wbuf = carve(R4, BF16, [2, FC, 512])
    kbuf = carve(R4, BF16, [2, 2048])
    vbuf = carve(R4 + 8192, BF16, [2, KC, 132])
    ostg = carve(R4 + 25600, BF16, [NT, 128])
    p7buf = carve(R4, BF16, [2, FC, 512])

    R5 = R4 + 32768
    upk_sb = carve(R5, BF16, [H, NDL, 128])
    xl32 = carve(R5, F32, [2, 512])
    upv_sb = carve(R5 + 16384, BF16, [H, NDL, 128])
    aoT = carve(R5, BF16, [H, 1024])
    gam_rep = carve(R5, F32, [2048])
    bet_rep = carve(R5 + 8192, F32, [2048])
    xlrb = carve(R5 + 16384, F32, [2, 2048])

    pb = [nc.alloc_psum_tensor(f"pb{i}", [128, 512], F32).ap() for i in range(6)]
    pt_ = [nc.alloc_psum_tensor(f"pt{i}", [128, 1024], BF16).ap() for i in range(2)]
    mm = PsumPool(p, pb[:5])
    aux = PsumPool(p, pb[5:6])
    tp = PsumPool(p, pt_)

    TRI = tri.bitcast(BF16)

    a2v = a2.rearrange("p (i j) -> p i j", i=8)[0:8]

    def prep_ms(g):
        g.memset(ident[:, 0:128], 0.0)
        g.memset(tri[:, 0:128], 0.0)
        g.memset(ones_bf[:, :], 1.0)
        g.memset(ident32[:, 0:128], 0.0)
        g.memset(knmax_row[0:1, :], 0.0)
        g.memset(zrow[0:1, :], 0.0)
        return g.memset(a2[0:8, :], 0.0)

    pt_ms = p.step("gpsimd", prep_ms)

    def prep_a1(g):
        g.affine_select(out=ident[:, 0:128], in_=ident[:, 0:128], compare_op=ALU.not_equal,
                        fill=1.0, base=0, pattern=[[-1, 128]], channel_multiplier=1)
        g.affine_select(out=tri[:, 0:128], in_=tri[:, 0:128], compare_op=ALU.is_gt,
                        fill=1.0, base=0, pattern=[[-1, 128]], channel_multiplier=1)
        g.affine_select(out=ident32[:, 0:128], in_=ident32[:, 0:128], compare_op=ALU.not_equal,
                        fill=1.0, base=0, pattern=[[-1, 128]], channel_multiplier=1)
        return g.affine_select(out=a2v, in_=a2v, compare_op=ALU.not_equal, fill=1.0,
                               base=0, pattern=[[-1, 8], [0, 8]], channel_multiplier=1)

    pt_a1 = p.step("gpsimd", prep_a1, waits=[pt_ms])
    pt_prep = p.step("gpsimd", lambda g: g.affine_select(
        out=a2v, in_=a2v, compare_op=ALU.not_equal, fill=1.0,
        base=0, pattern=[[0, 8], [-1, 8]], channel_multiplier=1), waits=[pt_a1])

    bias_pt = {}
    for nm, dst, srct in (("wqb", wqb_c, wqb), ("wkb", wkb_c, wkb), ("wvb", wvb_c, wvb),
                          ("upkb", upkb_c, upkb)):
        bias_pt[nm] = p.step("sync", (lambda dst=dst, srct=srct: lambda sy: sy.dma_start(
            out=dst[:, :], in_=srct[:, :]))(), dma=True)
    bias_pt["wklb"] = p.step("sync", lambda sy: sy.dma_start(out=wklb_c[:, 0:4], in_=wklb[:, :]), dma=True)
    bias_pt["wvlb"] = p.step("sync", lambda sy: sy.dma_start(out=wvlb_c[:, 0:4], in_=wvlb[:, :]), dma=True)
    pt_upvb = p.step("gpsimd", lambda g: g.dma_start(
        out=upvb_rep[:, :], in_=upvb[None, :].broadcast_to([128, 2048])), dma=True)
    pt_gate = p.step("sync", lambda sy: sy.dma_start(
        out=gate_sb.rearrange("p a b -> p (a b)"), in_=gate.rearrange("p a b -> p (a b)")), dma=True)

    pt_acol = p.step("sync", lambda sy: sy.dma_start(
        out=mtmp[:, 0:1], in_=parity[None, :].broadcast_to([128, 1])), dma=True)
    pt_msk0 = p.step("vector", lambda v: v.tensor_scalar_max(
        mlo[:, 0:128], tri[:, 0:128], mtmp[:, 0:1]), waits=[pt_acol, pt_prep])
    pt_masks = p.step("vector", lambda v: v.tensor_scalar_mul(
        mhi[:, 0:128], tri[:, 0:128], mtmp[:, 0:1]), waits=[pt_acol, pt_prep])

    cast_pts = []
    xt_evac_pts = []
    for t in range(NT):
        par = t % 2
        waits = [cast_pts[t - 2]] if t >= 2 else []
        pt_x = p.step("sync", (lambda t=t, par=par: lambda sy: sy.dma_start(
            out=xld[:, par], in_=xin[t * 128:(t + 1) * 128, :]))(), waits=waits, dma=True)
        pt_c = p.step("vector", (lambda par=par: lambda v: v.tensor_copy(
            xldb[:, par], xld[:, par]))(), waits=[pt_x])
        cast_pts.append(pt_c)
        for g8 in range(2):
            i, slot, twaits = tp.take()
            def mktr(par=par, g8=g8, slot=slot):
                def emit(te):
                    ins = None
                    for k in range(8):
                        c = g8 * 8 + k
                        ins = te.transpose(slot[:, k * 128:(k + 1) * 128],
                                           xldb[:, par, c * 128:(c + 1) * 128],
                                           ident[:, 0:128])
                    return ins
                return emit
            pt_t = p.step("tensor", mktr(), waits=[pt_c, pt_prep] + twaits)
            pt_e = p.step("vector", (lambda t=t, g8=g8, slot=slot: lambda v: v.tensor_copy(
                xT[:, g8 * 8:(g8 + 1) * 8, t * 128:(t + 1) * 128],
                slot.rearrange("p (a b) -> p a b", b=128)))(), waits=[pt_t])
            tp.release(i, pt_e)
            xt_evac_pts.append(pt_e)
    pt_xT = p.pt("vector")

    d = dbg("xT", [128, FC, TOK], BF16)
    if d is not None:
        p.step("sync", lambda sy, d=d: sy.dma_start(out=d.ap(), in_=xT[:, :, :]),
               waits=[pt_xT], dma=True)

    def proj_stream(wdram, bias_c, dst, nof, src_chunk, src_pt, extra_mm_waits=(),
                    first_war=(), bias_wait=None):
        nq = (nof + 3) // 4
        wpts = []
        qmm_pts = []
        evac_pts = [None] * nof
        for q in range(nq):
            par = q % 2
            waits = [qmm_pts[q - 2]] if q >= 2 else list(first_war)
            wpt = p.step("gpsimd", (lambda q=q, par=par: lambda g: g.dma_start(
                out=wbuf[:, par], in_=wdram[q]))(), waits=waits, dma=True)
            wpts.append(wpt)
            for oc in range(min(4, nof - q * 4)):
                of = q * 4 + oc
                for piece in range(2):
                    cs = slice(piece * 512, (piece + 1) * 512)
                    bi, slot, bwaits = mm.take()
                    def mk(par=par, oc=oc, cs=cs, slot=slot):
                        def emit(te):
                            ins = None
                            for c in range(FC):
                                ins = te.matmul(slot[:, 0:512],
                                                wbuf[:, par, c, oc * 128:(oc + 1) * 128],
                                                src_chunk(c)[:, cs],
                                                start=(c == 0), stop=(c == FC - 1))
                            return ins
                        return emit
                    p.step("tensor", mk(), waits=[wpt, src_pt] + bwaits + list(extra_mm_waits))
                    mmpt = p.pt("tensor")
                    def ev(of=of, cs=cs, slot=slot):
                        def emit(v):
                            if bias_c is not None:
                                return v.tensor_scalar_add(dst[:, of, cs], slot[:, 0:512],
                                                           bias_c[:, of:of + 1])
                            return v.tensor_copy(dst[:, of, cs], slot[:, 0:512])
                        return emit
                    ept = p.step("vector", ev(), waits=[mmpt] + ([bias_wait] if bias_wait else []))
                    mm.release(bi, ept)
                    evac_pts[of] = ept
            qmm_pts.append(p.pt("tensor"))
        return evac_pts, p.pt("tensor")

    t1_pts, pt_t1mm = proj_stream([wk[q] for q in range(4)], wkb_c, t1T, FC,
                                  lambda c: xT[:, c], pt_xT, bias_wait=bias_pt["wkb"])
    pt_t1 = p.pt("vector")

    pt_wkl = p.step("gpsimd", lambda g: g.dma_start(out=wbuf[:, 0], in_=wkl[:, :, :]),
                    waits=[pt_t1mm], dma=True)
    lat_pts = []
    for of in range(NDL):
        for piece in range(2):
            cs = slice(piece * 512, (piece + 1) * 512)
            bi, slot, bwaits = mm.take()
            def mk(of=of, cs=cs, slot=slot):
                def emit(te):
                    ins = None
                    for c in range(FC):
                        ins = te.matmul(slot[:, 0:512], wbuf[:, 0, c, of * 128:(of + 1) * 128],
                                        t1T[:, c, cs], start=(c == 0), stop=(c == FC - 1))
                    return ins
                return emit
            p.step("tensor", mk(), waits=[pt_wkl, pt_t1] + bwaits)
            mmpt = p.pt("tensor")
            ept = p.step("vector", (lambda of=of, cs=cs, slot=slot: lambda v: v.tensor_scalar_add(
                latT[:, of, cs], slot[:, 0:512], wklb_c[:, of:of + 1]))(),
                waits=[mmpt, bias_pt["wklb"]])
            mm.release(bi, ept)
            lat_pts.append(ept)
    pt_klat = p.pt("vector")
    pt_klatmm = p.pt("tensor")

    pt_upk = p.step("gpsimd", lambda g: g.dma_start(
        out=upk_sb.rearrange("p a b c -> p (a b c)"), in_=upk.rearrange("p a b c -> p (a b c)")), dma=True)
    pt_upv = p.step("gpsimd", lambda g: g.dma_start(
        out=upv_sb.rearrange("p a b c -> p (a b c)"), in_=upv.rearrange("p a b c -> p (a b c)")), dma=True)

    kvw_pts = []
    ksq_pts = []
    knmm_pts = []
    nrm_comb = []
    for h in range(H):
        par = h % 2
        kpts = []
        for piece in range(2):
            cs = slice(piece * 512, (piece + 1) * 512)
            bi, slot, bwaits = mm.take()
            def mk(h=h, cs=cs, slot=slot):
                def emit(te):
                    ins = None
                    for c in range(NDL):
                        ins = te.matmul(slot[:, 0:512], upk_sb[:, h, c, :], latT[:, c, cs],
                                        start=(c == 0), stop=(c == NDL - 1))
                    return ins
                return emit
            p.step("tensor", mk(), waits=[pt_upk, pt_klat] + bwaits)
            mmpt = p.pt("tensor")
            w = [mmpt, bias_pt["upkb"]]
            if h >= 2:
                w += [kvw_pts[h - 2], ksq_pts[h - 2]]
            ept = p.step("vector", (lambda par=par, h=h, cs=cs, slot=slot: lambda v: v.tensor_scalar_add(
                kstg[:, par, cs], slot[:, 0:512], upkb_c[:, h:h + 1]))(), waits=w)
            mm.release(bi, ept)
            kpts.append(ept)
        wpt = p.step("sync", (lambda h=h, par=par: lambda sy: sy.dma_start(
            out=kvk[h * 128:(h + 1) * 128, :], in_=kstg[:, par]))(), waits=kpts, dma=True)
        kvw_pts.append(wpt)
        sqw = kpts + ([knmm_pts[h - 1]] if h >= 1 else [])
        pt_sq = p.step("scalar", (lambda par=par: lambda sc: sc.activation(
            ksqb[:, :], kstg[:, par], AF.Square, bias=fp(0.0), scale=fp(1.0)))(), waits=sqw)
        ksq_pts.append(pt_sq)
        red_pts = []
        for piece in range(2):
            cs = slice(piece * 512, (piece + 1) * 512)
            bi, slot, bwaits = aux.take()
            pt_nm = p.step("tensor", (lambda cs=cs, slot=slot: lambda te: te.matmul(
                slot[0:1, 0:512], ones_bf[:, 0:1], ksqb[:, cs], start=True, stop=True))(),
                waits=[pt_sq, pt_prep] + bwaits)
            rpt = p.step("vector", (lambda piece=piece, slot=slot: lambda v: v.reduce_max(
                mtmp[0:1, 8 + piece:9 + piece], slot[0:1, 0:512], axis=AX.X))(),
                waits=[pt_nm] + ([nrm_comb[-1]] if nrm_comb else []))
            aux.release(bi, rpt)
            red_pts.append(rpt)
        knmm_pts.append(p.pt("tensor"))
        nrm_comb.append(p.step("vector", (lambda h=h: lambda v: v.tensor_tensor(
            knmax_row[0:1, h:h + 1], mtmp[0:1, 8:9], mtmp[0:1, 9:10], ALU.max))(), waits=red_pts))
    pt_knorm = p.pt("vector")
    pt_knT = p.pt("tensor")
    kvw_pts.append(p.step("sync", lambda sy: sy.dma_start(
        out=kvk[D:D + 1, :], in_=zrow[0:1, :]), waits=[pt_ms], dma=True))
    kvw_pts.append(p.step("sync", lambda sy: sy.dma_start(
        out=kvk[D:D + 1, 0:32], in_=knmax_row[0:1, 0:32]), waits=[pt_knorm], dma=True))

    s_agk = p.extra_sem("s_agk")
    pt_agk = p.step("gpsimd", lambda g: g.collective_compute(
        "AllGather", ALU.bypass, replica_groups=[[0, 1], [2, 3], [4, 5], [6, 7]],
        ins=[kvk.ap()], outs=[kall.ap()]), waits=kvw_pts, inc=(s_agk, 1))

    t2_pts, pt_t2mm = proj_stream([wv[q] for q in range(4)], wvb_c, t1T, FC,
                                  lambda c: xT[:, c], pt_xT, extra_mm_waits=[pt_klatmm],
                                  first_war=[pt_klatmm], bias_wait=bias_pt["wvb"])
    pt_t2 = p.pt("vector")

    pt_wvl = p.step("gpsimd", lambda g: g.dma_start(out=wbuf[:, 0], in_=wvl[:, :, :]),
                    waits=[pt_t2mm], dma=True)
    for of in range(NDL):
        for piece in range(2):
            cs = slice(piece * 512, (piece + 1) * 512)
            bi, slot, bwaits = mm.take()
            def mk(of=of, cs=cs, slot=slot):
                def emit(te):
                    ins = None
                    for c in range(FC):
                        ins = te.matmul(slot[:, 0:512], wbuf[:, 0, c, of * 128:(of + 1) * 128],
                                        t1T[:, c, cs], start=(c == 0), stop=(c == FC - 1))
                    return ins
                return emit
            p.step("tensor", mk(), waits=[pt_wvl, pt_t2] + bwaits)
            mmpt = p.pt("tensor")
            ept = p.step("vector", (lambda of=of, cs=cs, slot=slot: lambda v: v.tensor_scalar_add(
                latT[:, of, cs], slot[:, 0:512], wvlb_c[:, of:of + 1]))(),
                waits=[mmpt, bias_pt["wvlb"]])
            mm.release(bi, ept)
    pt_vlat = p.pt("vector")
    pt_vlmm = p.pt("tensor")

    vst_pts = []
    for h in range(H):
        par = h % 2
        ev_pts = []
        for t in range(NT):
            bi, slot, bwaits = mm.take()
            def mk(h=h, t=t, slot=slot):
                def emit(te):
                    ins = None
                    for c in range(NDL):
                        ins = te.matmul(slot[:, 0:128], latT[:, c, t * 128:(t + 1) * 128],
                                        upv_sb[:, h, c, :], start=(c == 0), stop=(c == NDL - 1))
                    return ins
                return emit
            p.step("tensor", mk(), waits=[pt_upv, pt_vlat] + bwaits)
            mmpt = p.pt("tensor")
            war = [vst_pts[h - 2]] if h >= 2 else []
            ept = p.step("vector", (lambda h=h, t=t, par=par, slot=slot: lambda v: v.tensor_tensor(
                vstg[:, par, t, :], slot[:, 0:128], upvb_rep[:, h * 128:(h + 1) * 128], ALU.add))(),
                waits=[mmpt, pt_upvb] + war)
            mm.release(bi, ept)
            ev_pts.append(ept)
        wpt = p.step("sync", (lambda h=h, par=par: lambda sy: sy.dma_start(
            out=kvv[:, h * 128:(h + 1) * 128].rearrange("(a p) d -> p a d", p=128),
            in_=vstg[:, par]))(), waits=ev_pts, dma=True)
        vst_pts.append(wpt)

    s_agv = p.extra_sem("s_agv")
    pt_agv = p.step("gpsimd", lambda g: g.collective_compute(
        "AllGather", ALU.bypass, replica_groups=[[0, 1], [2, 3], [4, 5], [6, 7]],
        ins=[kvv.ap()], outs=[vall.ap()]), waits=vst_pts, inc=(s_agv, 1))

    d = dbg("kvk", [D + 1, TOK], BF16)
    if d is not None:
        p.step("gpsimd", lambda g, d=d: g.dma_start(out=d.ap(), in_=kvk.ap()),
               waits=kvw_pts, dma=True)
    d = dbg("kvv", [TOK, D], BF16)
    if d is not None:
        p.step("gpsimd", lambda g, d=d: g.dma_start(out=d.ap(), in_=kvv.ap()),
               waits=vst_pts, dma=True)

    q_pts, pt_qmm = proj_stream([wq[q] for q in range(4)], wqb_c, qT, FC,
                                lambda c: xT[:, c], pt_xT, first_war=[pt_vlmm],
                                bias_wait=bias_pt["wqb"])
    qnmm_last = pt_knT
    for h in range(H):
        pt_sq = p.step("scalar", (lambda h=h: lambda sc: sc.activation(
            ksqb[:, :], qT[:, h], AF.Square, bias=fp(0.0), scale=fp(1.0)))(),
            waits=[q_pts[h], qnmm_last])
        red_pts = []
        for piece in range(2):
            cs = slice(piece * 512, (piece + 1) * 512)
            bi, slot, bwaits = aux.take()
            pt_nm = p.step("tensor", (lambda cs=cs, slot=slot: lambda te: te.matmul(
                slot[0:1, 0:512], ones_bf[:, 0:1], ksqb[:, cs], start=True, stop=True))(),
                waits=[pt_sq] + bwaits)
            rpt = p.step("vector", (lambda piece=piece, slot=slot: lambda v: v.reduce_max(
                mtmp[0:1, 8 + piece:9 + piece], slot[0:1, 0:512], axis=AX.X))(),
                waits=[pt_nm] + ([nrm_comb[-1]] if nrm_comb else []))
            aux.release(bi, rpt)
            red_pts.append(rpt)
        qnmm_last = p.pt("tensor")
        nrm_comb.append(p.step("vector", (lambda h=h: lambda v: v.tensor_tensor(
            qn_row[0:1, h:h + 1], mtmp[0:1, 8:9], mtmp[0:1, 9:10], ALU.max))(), waits=red_pts))
    pt_qnorm = p.pt("vector")
    pt_vones = p.step("gpsimd", lambda g: g.memset(vbuf[:, :, :, 128:129], 1.0),
                      waits=[pt_qmm])

    d = dbg("qT", [128, H, TOK], BF16)
    if d is not None:
        p.step("sync", lambda sy, d=d: sy.dma_start(out=d.ap(), in_=qT[:, :, :]),
               waits=[p.pt("vector")], dma=True)

    pt_m1 = p.step("sync", lambda sy: sy.dma_start(out=mscr[0:1, 0:16], in_=kall[D:D + 1, 0:16]),
                   waits=[(s_agk, 1)], dma=True)
    pt_m2 = p.step("sync", lambda sy: sy.dma_start(out=mscr[0:1, 16:32], in_=kall[2 * D + 1:2 * D + 2, 0:16]),
                   waits=[(s_agk, 1)], dma=True)
    pt_m3 = p.step("vector", lambda v: v.reduce_max(mtmp[0:1, 0:1], mscr[0:1, 0:32], axis=AX.X),
                   waits=[pt_m1, pt_m2])
    pt_m4 = p.step("vector", lambda v: v.reduce_max(mtmp[0:1, 1:2], qn_row[0:1, 0:16], axis=AX.X),
                   waits=[pt_qnorm])
    pt_m5 = p.step("vector", lambda v: v.tensor_tensor(mtmp[0:1, 2:3], mtmp[0:1, 0:1],
                                                       mtmp[0:1, 1:2], ALU.mult),
                   waits=[pt_m3, pt_m4])
    pt_m6 = p.step("scalar", lambda sc: sc.activation(mtmp[0:1, 3:4], mtmp[0:1, 2:3], AF.Ln,
                                                      bias=fp(1e-38), scale=fp(1.0)), waits=[pt_m5])
    pt_m7 = p.step("scalar", lambda sc: sc.activation(mtmp[0:1, 4:5], mtmp[0:1, 3:4], AF.Exp,
                                                      bias=fp(0.0), scale=fp(0.5)), waits=[pt_m6])
    pt_m8 = p.step("vector", lambda v: v.tensor_scalar_mul(mtmp[0:1, 5:6], mtmp[0:1, 4:5],
                                                           -SCALE), waits=[pt_m7])
    mdram = nc.dram_tensor("mdram", [1, 1], F32)
    pt_m9 = p.step("sync", lambda sy: sy.dma_start(
        out=mdram[0:1, 0:1], in_=mtmp[0:1, 5:6]), waits=[pt_m8], dma=True)
    pt_mprep = p.step("sync", lambda sy: sy.dma_start(
        out=negm[:, 0:1], in_=mdram[0:1, 0:1].broadcast_to([128, 1])), waits=[pt_m9], dma=True)

    kb_pts = [None] * (H + 2)
    vb_pts = [None] * (H + 2)
    scores_done = [None] * H
    av_done = [None] * H
    avr_mu = [None] * NT
    ostg_war = [None] * NT
    exp_done = [None] * H
    ao_pts = []

    head_ctx = {}

    def emit_av(h):
        par, vpts, mask_pts, cexp = head_ctx[h]
        mu_pts = []
        for j in range(NT):
            bi, slot, bwaits = mm.take()
            def mkav(par=par, j=j, slot=slot):
                def emit(te):
                    ins = None
                    for c in range(2 * j + 2):
                        ins = te.matmul(slot[:, 0:129], eSB[:, par, c, j * 128:(j + 1) * 128],
                                        vbuf[:, par, c, 0:129],
                                        start=(c == 0), stop=(c == 2 * j + 1))
                    return ins
                return emit
            avw = [mask_pts[2 * j + 1]] + [pt for c in range(2 * j + 2) for pt in cexp[c]]
            p.step("tensor", mkav(), waits=avw + vpts + bwaits + [pt_vones])
            avpt = p.pt("tensor")
            pt_r = p.step("vector", (lambda j=j, slot=slot: lambda v: v.reciprocal(
                avr[:, j:j + 1], slot[:, 128:129]))(),
                waits=[avpt] + ([avr_mu[j]] if avr_mu[j] is not None else []))
            pt_mu = p.step("vector", (lambda j=j, slot=slot: lambda v: v.tensor_scalar_mul(
                ostg[:, j], slot[:, 0:128], avr[:, j:j + 1]))(),
                waits=[pt_r] + ([ostg_war[j]] if ostg_war[j] is not None else []))
            avr_mu[j] = pt_mu
            mm.release(bi, pt_mu)
            mu_pts.append(pt_mu)
        ti, tslot, twaits = tp.take()
        def mktr(h=h, tslot=tslot):
            def emit(te):
                ins = None
                for j in range(NT):
                    ins = te.transpose(tslot[:, j * 128:(j + 1) * 128], ostg[:, j],
                                       ident[:, 0:128])
                return ins
            return emit
        pt_tr = p.step("tensor", mktr(), waits=mu_pts + twaits)
        for j in range(NT):
            ostg_war[j] = pt_tr
        pt_ev = p.step("vector", (lambda h=h, tslot=tslot: lambda v: v.tensor_copy(
            aoT[:, h, :], tslot[:, 0:1024]))(), waits=[pt_tr])
        tp.release(ti, pt_ev)
        ao_pts.append((pt_tr, pt_ev))
        av_done[h] = p.pt("tensor")

    for h in range(H):
        par = h % 2
        kwar = [scores_done[h - 2]] if h >= 2 else [pt_qmm]
        vwar = [av_done[h - 2]] if h >= 2 else [pt_qmm]
        kv = kbuf[:, par].rearrange("p (c j) -> p c j", j=128)
        kpts = []
        vpts = []
        for r in range(2):
            kpts.append(p.step("sync", (lambda h=h, r=r, kv=kv: lambda sy: sy.dma_start(
                out=kv[:, r::2, :],
                in_=kall[r * (D + 1) + h * 128: r * (D + 1) + (h + 1) * 128, :]
                .rearrange("p (a b) -> p a b", b=128)))(),
                waits=[(s_agk, 1)] + kwar, dma=True))
            vpts.append(p.step("sync", (lambda h=h, r=r, par=par: lambda sy: sy.dma_start(
                out=vbuf[:, par, r::2, 0:128],
                in_=vall[r * TOK:(r + 1) * TOK, h * 128:(h + 1) * 128]
                .rearrange("(a p) d -> p a d", p=128)))(),
                waits=[(s_agv, 1)] + vwar, dma=True))
        kb_pts[h] = kpts
        vb_pts[h] = vpts

        esb_war = [av_done[h - 2]] if h >= 2 else []
        chunk_exp_pts = []
        for c in range(KC):
            off = jmin(c) * 128
            cpts = []
            while off < TOK:
                wd = min(512, TOK - off)
                bi, slot, bwaits = mm.take()
                p.step("tensor", (lambda par=par, c=c, off=off, wd=wd, slot=slot, h=h: lambda te: te.matmul(
                    slot[:, 0:wd], kbuf[:, par, c * 128:(c + 1) * 128], qT[:, h, off:off + wd],
                    start=True, stop=True))(), waits=kpts + [q_pts[h]] + bwaits)
                mmpt = p.pt("tensor")
                ept = p.step("scalar", (lambda par=par, c=c, off=off, wd=wd, slot=slot: lambda sc: sc.activation(
                    eSB[:, par, c, off:off + wd], slot[:, 0:wd], AF.Exp,
                    bias=negm[:, 0:1], scale=fp(SCALE)))(),
                    waits=[mmpt, pt_mprep] + esb_war)
                mm.release(bi, ept)
                cpts.append(ept)
                off += wd
            chunk_exp_pts.append(cpts)
        scores_done[h] = p.pt("tensor")
        exp_done[h] = p.pt("scalar")

        mask_pts = []
        for c in range(KC):
            m = mlo if c % 2 == 0 else mhi
            o = jmin(c) * 128
            mask_pts.append(p.step("vector", (lambda par=par, c=c, m=m, o=o: lambda v: v.tensor_tensor(
                eSB[:, par, c, o:o + 128], eSB[:, par, c, o:o + 128],
                m[:, 0:128], ALU.mult))(), waits=[chunk_exp_pts[c][0], pt_masks]))
        pt_mask = mask_pts[-1]

        if h == 0:
            dkb = dbg("kbuf0", [128, 2048], BF16)
            if dkb is not None:
                p.step("sync", (lambda dkb=dkb: lambda sy: sy.dma_start(
                    out=dkb.ap(), in_=kbuf[:, 0]))(), waits=kpts, dma=True)
            dka = dbg("kall", [2 * (D + 1), TOK], BF16)
            if dka is not None:
                p.step("gpsimd", (lambda dka=dka: lambda g: g.dma_start(
                    out=dka.ap(), in_=kall.ap()))(), waits=[(s_agk, 1)], dma=True)
            d0 = dbg("eS0", [128, KC, TOK], BF16)
            if d0 is not None:
                p.step("sync", (lambda d0=d0: lambda sy: sy.dma_start(
                    out=d0.ap(), in_=eSB[:, 0]))(), waits=[pt_mask], dma=True)
            dm = dbg("masks", [128, 3, 128], BF16)
            if dm is not None:
                p.step("sync", (lambda dm=dm: lambda sy: sy.dma_start(
                    out=dm.ap()[:, 0], in_=tri[:, 0:128]))(), waits=[pt_mask], dma=True)
                p.step("sync", (lambda dm=dm: lambda sy: sy.dma_start(
                    out=dm.ap()[:, 1], in_=mlo[:, 0:128]))(), waits=[pt_mask], dma=True)
                p.step("sync", (lambda dm=dm: lambda sy: sy.dma_start(
                    out=dm.ap()[:, 2], in_=mhi[:, 0:128]))(), waits=[pt_mask], dma=True)
            dn_ = dbg("negm", [1, 16])
            if dn_ is not None:
                p.step("sync", (lambda dn_=dn_: lambda sy: sy.dma_start(
                    out=dn_.ap(), in_=mtmp[0:1, 0:16]))(), waits=[pt_mprep], dma=True)
            dnc = dbg("negmcol", [128, 1])
            if dnc is not None:
                p.step("sync", (lambda dnc=dnc: lambda sy: sy.dma_start(
                    out=dnc.ap(), in_=negm[:, 0:1]))(), waits=[pt_mprep], dma=True)
        head_ctx[h] = (par, vpts, mask_pts, chunk_exp_pts)
        if h >= 1:
            emit_av(h - 1)
    emit_av(H - 1)
    pt_attn_v = p.pt("vector")

    d = dbg("aoT", [128, H, TOK], BF16)
    if d is not None:
        p.step("sync", lambda sy, d=d: sy.dma_start(out=d.ap(), in_=aoT[:, :, :]),
               waits=[pt_attn_v], dma=True)
    d = dbg("aoTb", [128, H, TOK], BF16)
    if d is not None:
        p.step("sync", lambda sy, d=d: sy.dma_start(out=d.ap(), in_=aoT[:, :, :]),
               waits=[pt_attn_v], dma=True)

    pt_wob = p.step("sync", lambda sy: sy.dma_start(
        out=wob_rep[:, :], in_=wob[None, :].broadcast_to([128, 2048])),
        waits=[scores_done[H - 1]], dma=True)

    da2 = dbg("aoT2", [128, H, TOK], BF16)
    if da2 is not None:
        p.step("sync", (lambda da2=da2: lambda sy: sy.dma_start(
            out=da2.ap(), in_=aoT[:, :, :]))(), waits=[pt_attn_v], dma=True)

    wo_ev = [[None] * 4 for _ in range(NT)]
    woq_mm = []
    for q in range(4):
        par = q % 2
        wwaits = [woq_mm[q - 2]] if q >= 2 else [av_done[H - 1]]
        wpt = p.step("gpsimd", (lambda q=q, par=par: lambda g: g.dma_start(
            out=wbuf[:, par], in_=wo[q]))(), waits=wwaits, dma=True)
        for t in range(NT):
            bi, slot, bwaits = mm.take()
            def mkwo(t=t, par=par, slot=slot):
                def emit(te):
                    ins = None
                    for hh in range(FC):
                        ins = te.matmul(slot[:, 0:512], aoT[:, hh, t * 128:(t + 1) * 128],
                                        wbuf[:, par, hh, :],
                                        start=(hh == 0), stop=(hh == FC - 1))
                    return ins
                return emit
            p.step("tensor", mkwo(), waits=[wpt, pt_attn_v] + bwaits)
            mmpt = p.pt("tensor")
            ept = p.step("vector", (lambda t=t, q=q, slot=slot: lambda v: v.tensor_tensor(
                ybuf[:, t, q * 512:(q + 1) * 512], slot[:, 0:512],
                wob_rep[:, q * 512:(q + 1) * 512], ALU.add))(), waits=[mmpt, pt_wob])
            mm.release(bi, ept)
            wo_ev[t][q] = ept
        woq_mm.append(p.pt("tensor"))

    lg_pts = []
    xl_mm = [None, None]

    def emit_gate_tile(t, gate_in_waits):
        gi, gslot, gwaits = aux.take()
        for cq in range(4):
            qpar = cq % 2
            bi, slot, bwaits = mm.take()
            tr_last = None
            for i in range(4):
                c = cq * 4 + i
                p.step("tensor", (lambda t=t, c=c, i=i, slot=slot: lambda te: te.transpose(
                    slot[:, i * 128:(i + 1) * 128].bitcast(F32),
                    ybuf[:, t, c * 128:(c + 1) * 128],
                    ident32[:, 0:128]))(), waits=gate_in_waits + [pt_prep] + (bwaits if i == 0 else []))
            tr_last = p.pt("tensor")
            war = [xl_mm[qpar]] if xl_mm[qpar] is not None else []
            pev = p.step("vector", (lambda qpar=qpar, slot=slot: lambda v: v.tensor_copy(
                xl32[:, qpar], slot[:, 0:512]))(), waits=[tr_last] + war)
            mm.release(bi, pev)
            def mkg(cq=cq, qpar=qpar, gslot=gslot):
                def emit(te):
                    ins = None
                    for i in range(4):
                        c = cq * 4 + i
                        ins = te.matmul(gslot[0:8, 0:128], gate_sb[:, c, :],
                                        xl32[:, qpar, i * 128:(i + 1) * 128],
                                        start=(c == 0), stop=(c == FC - 1))
                    return ins
                return emit
            p.step("tensor", mkg(), waits=[pev, pt_gate] + (gwaits if cq == 0 else []))
            xl_mm[qpar] = p.pt("tensor")
        mmpt = p.pt("tensor")
        ept = p.step("vector", (lambda t=t, gslot=gslot: lambda v: v.tensor_copy(
            logits32[0:8, t * 128:(t + 1) * 128], gslot[0:8, 0:128]))(), waits=[mmpt])
        aux.release(gi, ept)
        lg_pts.append(ept)


    ln1_pts = []
    xtf_tr_pts = []
    for t in range(NT):
        par = t % 2
        xwar = [ln1_pts[t - 2][3]] if t >= 2 else [scores_done[H - 1]]
        pt_xr = p.step("sync", (lambda t=t, par=par: lambda sy: sy.dma_start(
            out=xrb[:, par], in_=xin[t * 128:(t + 1) * 128, :]))(),
            waits=xwar, dma=True)
        pt_add = p.step("vector", (lambda t=t, par=par: lambda v: v.scalar_tensor_tensor(
            ybuf[:, t, :], ybuf[:, t, :], 0.0, xrb[:, par, :], ALU.add, ALU.add,
            accum_out=stats2[:, t:t + 1]))(), waits=wo_ev[t] + [pt_xr])
        ev_pts = [pt_add]
        dy1 = dbg("y1d", [TOK, D])
        if dy1 is not None:
            dpt = p.step("sync", (lambda t=t, dy1=dy1: lambda sy: sy.dma_start(
                out=dy1.ap()[t * 128:(t + 1) * 128, :], in_=ybuf[:, t]))(),
                waits=ev_pts, dma=True)
            ev_pts = ev_pts + [dpt]
        pt_s2 = p.step("vector", (lambda t=t: lambda v: v.tensor_scalar_mul(
            stats2[:, t:t + 1], stats2[:, t:t + 1], INVD))(), waits=ev_pts)
        sq_pts = []
        for ob in range(4):
            bi, slot, bwaits = aux.take()
            spt = p.step("scalar", (lambda t=t, ob=ob, slot=slot: lambda sc: sc.activation(
                slot[:, 0:512], ybuf[:, t, ob * 512:(ob + 1) * 512], AF.Square,
                bias=fp(0.0), scale=fp(1.0),
                accum_out=stats[:, 32 + t * 4 + ob:32 + t * 4 + ob + 1]))(),
                waits=ev_pts + bwaits)
            aux.release(bi, spt)
            sq_pts.append(spt)
        pt_s3 = p.step("vector", (lambda t=t: lambda v: v.reduce_sum(
            stats2[:, 16 + t:17 + t], stats[:, 32 + t * 4:32 + t * 4 + 4], axis=AX.X))(), waits=sq_pts)
        pt_s5 = p.step("vector", (lambda t=t: lambda v: v.tensor_scalar_mul(
            stats2[:, 16 + t:17 + t], stats2[:, 16 + t:17 + t], INVD))(), waits=[pt_s3])
        pt_s6 = p.step("vector", (lambda t=t: lambda v: v.tensor_tensor(
            stats2[:, 32 + t:33 + t], stats2[:, t:t + 1], stats2[:, t:t + 1], ALU.mult))(),
            waits=[pt_s2])
        pt_s7 = p.step("vector", (lambda t=t: lambda v: v.tensor_tensor(
            stats2[:, 16 + t:17 + t], stats2[:, 16 + t:17 + t], stats2[:, 32 + t:33 + t],
            ALU.subtract))(), waits=[pt_s5, pt_s6])
        pt_s8 = p.step("scalar", (lambda t=t: lambda sc: sc.activation(
            stats2[:, 32 + t:33 + t], stats2[:, 16 + t:17 + t], AF.Ln, bias=fp(EPS),
            scale=fp(1.0)))(), waits=[pt_s7])
        pt_s9 = p.step("scalar", (lambda t=t: lambda sc: sc.activation(
            stats2[:, 16 + t:17 + t], stats2[:, 32 + t:33 + t], AF.Exp, bias=fp(0.0),
            scale=fp(-0.5)))(), waits=[pt_s8])
        nrm_pts = []
        for ob in range(4):
            npt = p.step("vector", (lambda t=t, ob=ob: lambda v: v.tensor_scalar(
                ybuf[:, t, ob * 512:(ob + 1) * 512], ybuf[:, t, ob * 512:(ob + 1) * 512],
                stats2[:, t:t + 1], stats2[:, 16 + t:17 + t], ALU.subtract, ALU.mult))(),
                waits=[pt_s9] + sq_pts)
            nrm_pts.append(npt)
        emit_gate_tile(t, nrm_pts)
        pt_sp = p.step("sync", (lambda t=t: lambda sy: sy.dma_start(
            out=xspill[t * 128:(t + 1) * 128, :], in_=ybuf[:, t]))(), waits=nrm_pts, dma=True)
        cwar = [xtf_tr_pts[-3]] if len(xtf_tr_pts) >= 3 else [av_done[H - 1]]
        pt_cast = p.step("vector", (lambda t=t, par=par: lambda v: v.tensor_copy(
            xcastb[:, par], ybuf[:, t]))(), waits=nrm_pts + cwar)
        tprs = []
        for g8 in range(2):
            ti, tslot, twaits = tp.take()
            def mktr(par=par, g8=g8, tslot=tslot):
                def emit(te):
                    ins = None
                    for k in range(8):
                        c = g8 * 8 + k
                        ins = te.transpose(tslot[:, k * 128:(k + 1) * 128],
                                           xcastb[:, par, c * 128:(c + 1) * 128],
                                           ident[:, 0:128])
                    return ins
                return emit
            ptr = p.step("tensor", mktr(), waits=[pt_cast] + twaits)
            pev = p.step("vector", (lambda t=t, g8=g8, tslot=tslot: lambda v: v.tensor_copy(
                xTf[:, g8 * 8:(g8 + 1) * 8, t * 128:(t + 1) * 128],
                tslot.rearrange("p (a b) -> p a b", b=128)))(), waits=[ptr])
            tp.release(ti, pev)
            xtf_tr_pts.append(ptr)
            tprs.append(pev)
        ln1_pts.append((pt_sp, tprs[-1], pt_cast, p.pt("vector")))
    pt_xTf = p.pt("vector")
    pt_xTf_T = p.pt("tensor")

    d = dbg("xln1", [TOK, D])
    if d is not None:
        p.step("gpsimd", lambda g, d=d: g.dma_start(out=d.ap(), in_=xspill.ap()),
               waits=[pt for (pt, _, _, _) in ln1_pts], dma=True)

    pb_pts = []
    for piece in range(2):
        cs = slice(piece * 512, (piece + 1) * 512)
        bi, slot, bwaits = aux.take()
        p.step("tensor", (lambda cs=cs, slot=slot: lambda te: te.matmul(
            slot[0:64, 0:512], a2[0:8, 0:64], logits32[0:8, cs], start=True, stop=True))(),
            waits=lg_pts + bwaits + [pt_prep])
        mmpt = p.pt("tensor")
        ept = p.step("vector", (lambda cs=cs, slot=slot: lambda v: v.tensor_copy(
            pbs32[0:64, cs], slot[0:64, 0:512]))(), waits=[mmpt])
        aux.release(bi, ept)
        pb_pts.append(ept)
    sc_pts = []
    for t in range(NT):
        bi, slot, bwaits = mm.take()
        ptr = p.step("tensor", (lambda t=t, slot=slot: lambda te: te.transpose(
            slot[:, 0:64].bitcast(F32), pbs32[0:64, t * 128:(t + 1) * 128],
            ident32[0:64, 0:64]))(), waits=pb_pts + bwaits)
        pcv = p.step("vector", (lambda slot=slot: lambda v: v.tensor_copy(
            gmax[:, 0:64], slot[:, 0:64]))(), waits=[ptr] + (sc_pts[-1:] if sc_pts else []))
        mm.release(bi, pcv)
        pms = p.step("vector", lambda v: v.memset(gmax[:, 0:64:9], -1e30), waits=[pcv])
        prd = p.step("vector", (lambda t=t: lambda v: v.reduce_max(
            scale_c[:, t:t + 1], gmax[:, 0:64], axis=AX.X))(), waits=[pms])
        sc_pts.append(prd)
    pt_gate_all = p.pt("vector")

    p7_last = [pt_xTf_T, pt_xTf_T]
    y2_pts = [[None] * 4 for _ in range(NT)]

    def emit_expert():
        for q in range(4):
            par = q % 2
            wpt = p.step("gpsimd", (lambda q=q, par=par: lambda g: g.dma_start(
                out=p7buf[:, par], in_=exw[q]))(), waits=[p7_last[par]], dma=True)
            for t in range(NT):
                bi, slot, bwaits = mm.take()
                def mke(t=t, par=par, slot=slot):
                    def emit(te):
                        ins = None
                        for c in range(FC):
                            ins = te.matmul(slot[:, 0:512], xTf[:, c, t * 128:(t + 1) * 128],
                                            p7buf[:, par, c, :], start=(c == 0), stop=(c == FC - 1))
                        return ins
                    return emit
                p.step("tensor", mke(), waits=[wpt, pt_xTf] + bwaits)
                mmpt = p.pt("tensor")
                ept = p.step("vector", (lambda t=t, q=q, slot=slot: lambda v: v.tensor_scalar_mul(
                    ybuf[:, t, q * 512:(q + 1) * 512], slot[:, 0:512], scale_c[:, t:t + 1]))(),
                    waits=[mmpt, sc_pts[t], ln1_pts[t][0], ln1_pts[t][2]])
                mm.release(bi, ept)
                y2_pts[t][q] = ept
            p7_last[par] = p.pt("tensor")

    pt_gam = p.step("sync", lambda sy: sy.dma_start(
        out=gam_rep[:, :], in_=lng[None, :].broadcast_to([128, 2048])),
        waits=[pt_xTf_T], dma=True)
    pt_bet = p.step("sync", lambda sy: sy.dma_start(
        out=bet_rep[:, :], in_=lnb[None, :].broadcast_to([128, 2048])),
        waits=[pt_xTf_T], dma=True)

    hq_pts = [None] * 16
    hq_extra_war = []
    for iq in range(4):
        for ib in range(8):
            par = ib % 2
            wpt = p.step("gpsimd", (lambda iq=iq, ib=ib, par=par: lambda g: g.dma_start(
                out=p7buf[:, par, :, 0:256], in_=upw[iq * 8 + ib]))(), waits=[p7_last[par]], dma=True)
            for icin in range(2):
                ic = ib * 2 + icin
                for piece in range(2):
                    cs = slice(piece * 512, (piece + 1) * 512)
                    bi, slot, bwaits = mm.take()
                    def mku(par=par, icin=icin, cs=cs, slot=slot):
                        def emit(te):
                            ins = None
                            for c in range(FC):
                                ins = te.matmul(slot[:, 0:512],
                                                p7buf[:, par, c, icin * 128:(icin + 1) * 128],
                                                xTf[:, c, cs], start=(c == 0), stop=(c == FC - 1))
                            return ins
                        return emit
                    hq_war = ([hq_pts[ic]] if hq_pts[ic] is not None else []) + hq_extra_war
                    p.step("tensor", mku(), waits=[wpt, pt_xTf] + bwaits)
                    mmpt = p.pt("tensor")
                    rpt = p.step("scalar", (lambda ic=ic, cs=cs, slot=slot: lambda sc: sc.activation(
                        hq[:, ic, cs], slot[:, 0:512], AF.Relu, bias=fp(0.0), scale=fp(1.0)))(),
                        waits=[mmpt] + hq_war)
                    qpt = p.step("vector", (lambda ic=ic, cs=cs: lambda v: v.tensor_tensor(
                        hq[:, ic, cs], hq[:, ic, cs], hq[:, ic, cs], ALU.mult))(), waits=[rpt])
                    mm.release(bi, qpt)
            p7_last[par] = p.pt("tensor")
        pt_up_done = p.pt("vector")
        if iq == 0:
            emit_expert()
        if iq == 0:
            dh_ = dbg("hq0", [128, 16, 1024], BF16)
            if dh_ is not None:
                dpt = p.step("sync", (lambda dh_=dh_: lambda sy: sy.dma_start(
                    out=dh_.ap(), in_=hq[:, :, :]))(), waits=[pt_up_done], dma=True)
                hq_extra_war.append(dpt)
        for ob in range(4):
            par = ob % 2
            wpt = p.step("gpsimd", (lambda iq=iq, ob=ob, par=par: lambda g: g.dma_start(
                out=p7buf[:, par], in_=dnw[iq, ob]))(), waits=[p7_last[par]], dma=True)
            for t in range(NT):
                bi, slot, bwaits = mm.take()
                def mkd(t=t, par=par, slot=slot):
                    def emit(te):
                        ins = None
                        for ic in range(16):
                            ins = te.matmul(slot[:, 0:512], hq[:, ic, t * 128:(t + 1) * 128],
                                            p7buf[:, par, ic, :], start=(ic == 0), stop=(ic == 15))
                        return ins
                    return emit
                p.step("tensor", mkd(), waits=[wpt, pt_up_done] + bwaits)
                mmpt = p.pt("tensor")
                if iq < 3:
                    ept = p.step("vector", (lambda t=t, ob=ob, slot=slot: lambda v: v.tensor_tensor(
                        ybuf[:, t, ob * 512:(ob + 1) * 512], ybuf[:, t, ob * 512:(ob + 1) * 512],
                        slot[:, 0:512], ALU.add))(), waits=[mmpt, y2_pts[t][ob]])
                else:
                    ept = p.step("vector", (lambda t=t, ob=ob, slot=slot: lambda v: v.scalar_tensor_tensor(
                        ybuf[:, t, ob * 512:(ob + 1) * 512], slot[:, 0:512], 0.0,
                        ybuf[:, t, ob * 512:(ob + 1) * 512], ALU.add, ALU.add,
                        accum_out=stats[:, t * 4 + ob:t * 4 + ob + 1]))(),
                        waits=[mmpt, y2_pts[t][ob]])
                mm.release(bi, ept)
                y2_pts[t][ob] = ept
            p7_last[par] = p.pt("tensor")
        hq_pts = [p.pt("tensor")] * 16

    dscl = dbg("scl", [128, 8])
    if dscl is not None:
        p.step("sync", (lambda dscl=dscl: lambda sy: sy.dma_start(
            out=dscl.ap(), in_=scale_c[:, 0:8]))(), waits=sc_pts, dma=True)

    out_pts = []
    for t in range(NT):
        par = t % 2
        evs = y2_pts[t]
        dy2 = dbg("y2d", [TOK, D])
        if dy2 is not None:
            dpt = p.step("sync", (lambda t=t, dy2=dy2: lambda sy: sy.dma_start(
                out=dy2.ap()[t * 128:(t + 1) * 128, :], in_=ybuf[:, t]))(),
                waits=evs, dma=True)
            evs = evs + [dpt]
        pt_a = p.step("vector", (lambda t=t: lambda v: v.reduce_sum(
            stats2[:, t:t + 1], stats[:, t * 4:t * 4 + 4], axis=AX.X))(), waits=evs)
        pt_b = p.step("vector", (lambda t=t: lambda v: v.tensor_scalar_mul(
            stats2[:, t:t + 1], stats2[:, t:t + 1], INVD))(), waits=[pt_a])
        sq_pts = []
        for ob in range(4):
            bi, slot, bwaits = aux.take()
            spt = p.step("scalar", (lambda t=t, ob=ob, slot=slot: lambda sc: sc.activation(
                slot[:, 0:512], ybuf[:, t, ob * 512:(ob + 1) * 512], AF.Square,
                bias=fp(0.0), scale=fp(1.0),
                accum_out=stats[:, 32 + t * 4 + ob:32 + t * 4 + ob + 1]))(),
                waits=[evs[ob]] + bwaits)
            aux.release(bi, spt)
            sq_pts.append(spt)
        pt_cv = p.step("vector", (lambda t=t: lambda v: v.reduce_sum(
            stats2[:, 16 + t:17 + t], stats[:, 32 + t * 4:32 + t * 4 + 4], axis=AX.X))(), waits=sq_pts)
        pt_c2 = p.step("vector", (lambda t=t: lambda v: v.tensor_scalar_mul(
            stats2[:, 16 + t:17 + t], stats2[:, 16 + t:17 + t], INVD))(), waits=[pt_cv])
        pt_c3 = p.step("vector", (lambda t=t: lambda v: v.tensor_tensor(
            stats2[:, 32 + t:33 + t], stats2[:, t:t + 1], stats2[:, t:t + 1], ALU.mult))(), waits=[pt_b])
        pt_c4 = p.step("vector", (lambda t=t: lambda v: v.tensor_tensor(
            stats2[:, 16 + t:17 + t], stats2[:, 16 + t:17 + t], stats2[:, 32 + t:33 + t],
            ALU.subtract))(), waits=[pt_c2, pt_c3])
        pt_c5 = p.step("scalar", (lambda t=t: lambda sc: sc.activation(
            stats2[:, 32 + t:33 + t], stats2[:, 16 + t:17 + t], AF.Ln, bias=fp(EPS),
            scale=fp(1.0)))(), waits=[pt_c4])
        pt_c6 = p.step("scalar", (lambda t=t: lambda sc: sc.activation(
            stats2[:, 16 + t:17 + t], stats2[:, 32 + t:33 + t], AF.Exp, bias=fp(0.0),
            scale=fp(-0.5)))(), waits=[pt_c5])
        xw = [out_pts[t - 2]] if t >= 2 else []
        pt_xl = p.step("sync", (lambda t=t, par=par: lambda sy: sy.dma_start(
            out=xlrb[:, par], in_=xspill[t * 128:(t + 1) * 128, :]))(),
            waits=[ln1_pts[t][0]] + xw, dma=True)
        z_pts = []
        for ob in range(4):
            cs = slice(ob * 512, (ob + 1) * 512)
            p1_ = p.step("vector", (lambda t=t, cs=cs: lambda v: v.tensor_scalar(
                ybuf[:, t, cs], ybuf[:, t, cs], stats2[:, t:t + 1], stats2[:, 16 + t:17 + t],
                ALU.subtract, ALU.mult))(), waits=[pt_c6] + sq_pts)
            p2_ = p.step("gpsimd", (lambda t=t, cs=cs: lambda g: g.tensor_tensor(
                ybuf[:, t, cs], ybuf[:, t, cs], gam_rep[:, cs], ALU.mult))(),
                waits=[p1_, pt_gam])
            p3_ = p.step("gpsimd", (lambda t=t, cs=cs: lambda g: g.tensor_tensor(
                ybuf[:, t, cs], ybuf[:, t, cs], bet_rep[:, cs], ALU.add))(),
                waits=[p2_, pt_bet])
            p4_ = p.step("vector", (lambda t=t, cs=cs, ob=ob, par=par: lambda v: v.scalar_tensor_tensor(
                ybuf[:, t, cs], ybuf[:, t, cs], 0.0, xlrb[:, par, cs], ALU.add, ALU.add,
                accum_out=stats[:, t * 4 + ob:t * 4 + ob + 1]))(), waits=[p3_, pt_xl])
            z_pts.append(p4_)
        pt_d1 = p.step("vector", (lambda t=t: lambda v: v.reduce_sum(
            stats2[:, t:t + 1], stats[:, t * 4:t * 4 + 4], axis=AX.X))(), waits=z_pts)
        pt_d2 = p.step("vector", (lambda t=t: lambda v: v.tensor_scalar_mul(
            stats2[:, t:t + 1], stats2[:, t:t + 1], INVD))(), waits=[pt_d1])
        sq2_pts = []
        for ob in range(4):
            bi, slot, bwaits = aux.take()
            spt = p.step("scalar", (lambda t=t, ob=ob, slot=slot: lambda sc: sc.activation(
                slot[:, 0:512], ybuf[:, t, ob * 512:(ob + 1) * 512], AF.Square,
                bias=fp(0.0), scale=fp(1.0),
                accum_out=stats[:, 32 + t * 4 + ob:32 + t * 4 + ob + 1]))(),
                waits=[z_pts[ob]] + bwaits)
            aux.release(bi, spt)
            sq2_pts.append(spt)
        pt_d3 = p.step("vector", (lambda t=t: lambda v: v.reduce_sum(
            stats2[:, 16 + t:17 + t], stats[:, 32 + t * 4:32 + t * 4 + 4], axis=AX.X))(), waits=sq2_pts)
        pt_d4 = p.step("vector", (lambda t=t: lambda v: v.tensor_scalar_mul(
            stats2[:, 16 + t:17 + t], stats2[:, 16 + t:17 + t], INVD))(), waits=[pt_d3])
        pt_d5 = p.step("vector", (lambda t=t: lambda v: v.tensor_tensor(
            stats2[:, 32 + t:33 + t], stats2[:, t:t + 1], stats2[:, t:t + 1], ALU.mult))(), waits=[pt_d2])
        pt_d6 = p.step("vector", (lambda t=t: lambda v: v.tensor_tensor(
            stats2[:, 16 + t:17 + t], stats2[:, 16 + t:17 + t], stats2[:, 32 + t:33 + t],
            ALU.subtract))(), waits=[pt_d4, pt_d5])
        pt_d7 = p.step("scalar", (lambda t=t: lambda sc: sc.activation(
            stats2[:, 32 + t:33 + t], stats2[:, 16 + t:17 + t], AF.Ln, bias=fp(EPS),
            scale=fp(1.0)))(), waits=[pt_d6])
        pt_d8 = p.step("scalar", (lambda t=t: lambda sc: sc.activation(
            stats2[:, 16 + t:17 + t], stats2[:, 32 + t:33 + t], AF.Exp, bias=fp(0.0),
            scale=fp(-0.5)))(), waits=[pt_d7])
        fin_pts = []
        for ob in range(4):
            cs = slice(ob * 512, (ob + 1) * 512)
            fpt = p.step("vector", (lambda t=t, cs=cs: lambda v: v.tensor_scalar(
                ybuf[:, t, cs], ybuf[:, t, cs], stats2[:, t:t + 1], stats2[:, 16 + t:17 + t],
                ALU.subtract, ALU.mult))(), waits=[pt_d8] + sq2_pts)
            fin_pts.append(fpt)
        opt = p.step("sync", (lambda t=t: lambda sy: sy.dma_start(
            out=out[t * 128:(t + 1) * 128, :], in_=ybuf[:, t]))(), waits=fin_pts, dma=True)
        out_pts.append(opt)

    p.step("sync", lambda sy: sy.nop(), waits=out_pts, inc=(p.sem["sync"], 1))

    p.emit()
    return nc, DBG


def _swz_quarters(w):
    dmat, n = w.shape
    return np.ascontiguousarray(
        w.reshape(dmat // 128, 128, n // 512, 512).transpose(2, 1, 0, 3))


def _prep_weights(i):
    wts = {}
    wts["wq"] = _swz_quarters(np.asarray(i["L0_attn_wq_w"], np.float32))
    wts["wk"] = _swz_quarters(np.asarray(i["L0_attn_wk_w"], np.float32))
    wts["wv"] = _swz_quarters(np.asarray(i["L0_attn_wv_w"], np.float32))
    wts["wo"] = _swz_quarters(np.asarray(i["L0_attn_wo_w"], np.float32))
    for nm, key in (("wkl", "L0_attn_wkl_w"), ("wvl", "L0_attn_wvl_w")):
        w = np.asarray(i[key], np.float32)
        wts[nm] = np.ascontiguousarray(w.reshape(16, 128, 512).transpose(1, 0, 2))
    for nm, key in (("upk", "L0_attn_upk_w"), ("upv", "L0_attn_upv_w")):
        w = np.asarray(i[key], np.float32)
        wts[nm] = np.ascontiguousarray(w.reshape(16, 4, 128, 128).transpose(2, 0, 1, 3))
    g = np.asarray(i["L0_ffn_gate_w"], np.float32)
    wts["gate"] = np.ascontiguousarray(g.reshape(16, 128, 8).transpose(1, 0, 2))
    up = np.asarray(i["L0_ffn_up_w"], np.float32)
    wts["upw"] = np.ascontiguousarray(up.reshape(16, 128, 32, 256).transpose(2, 1, 0, 3))
    dn = np.asarray(i["L0_ffn_down_w"], np.float32)
    wts["dnw"] = np.ascontiguousarray(dn.reshape(4, 16, 128, 4, 512).transpose(0, 3, 2, 1, 4))
    wts["exw"] = _swz_quarters(np.asarray(i["L0_ffn_expert_weight"], np.float32))
    for nm, key in (("wqb", "L0_attn_wq_b"), ("wkb", "L0_attn_wk_b"), ("wvb", "L0_attn_wv_b")):
        wts[nm] = np.ascontiguousarray(np.asarray(i[key], np.float32).reshape(16, 128).T)
    for nm, key in (("wklb", "L0_attn_wkl_b"), ("wvlb", "L0_attn_wvl_b")):
        wts[nm] = np.ascontiguousarray(np.asarray(i[key], np.float32).reshape(4, 128).T)
    wts["upkb"] = np.ascontiguousarray(np.asarray(i["L0_attn_upk_b"], np.float32).T)
    wts["upvb"] = np.ascontiguousarray(np.asarray(i["L0_attn_upv_b"], np.float32).reshape(-1))
    wts["wob"] = np.ascontiguousarray(np.asarray(i["L0_attn_wo_b"], np.float32))
    wts["lng"] = np.ascontiguousarray(np.asarray(i["L0_ffn_ln_gamma"], np.float32))
    wts["lnb"] = np.ascontiguousarray(np.asarray(i["L0_ffn_ln_beta"], np.float32))
    return wts


def kernel(**inputs):
    x = np.asarray(inputs["L0_attn_x"], np.float32)
    wts = _prep_weights(inputs)
    nc, _ = build()
    in_maps = []
    for core in range(8):
        b, a = core // 2, core % 2
        xs = np.ascontiguousarray(x[b].reshape(16, 128, D)[a::2].reshape(TOK, D))
        m = {"xin": xs, "parity": np.array([a], np.float32)}
        m.update(wts)
        in_maps.append(m)
    res = run_bass_kernel_spmd(nc, in_maps, core_ids=list(range(8)))
    outp = np.empty((4, 2048, D), np.float32)
    for core in range(8):
        b, a = core // 2, core % 2
        outp[b].reshape(16, 128, D)[a::2] = res.results[core]["out"].reshape(NT, 128, D)
    return outp


# version 13 (final)
# speedup vs baseline: 1.0855x; 1.0053x over previous
import numpy as np
import concourse.bass as bass
import concourse.mybir as mybir
from concourse.bass_utils import run_bass_kernel_spmd

F32 = mybir.dt.float32
BF16 = mybir.dt.bfloat16
AF = mybir.ActivationFunctionType
ALU = mybir.AluOpType
AX = mybir.AxisListType
fp = np.float32

D = 2048
FC = 16
TOK = 1024
NT = 8
H = 16
DH = 128
DL = 512
NDL = 4
S = 2048
KC = 16
INTER = 8192
SCALE = float(1.0 / np.sqrt(np.float32(128.0)))
EPS = 1e-5
INVD = float(np.float32(1.0) / np.float32(2048.0))


def jmin(c):
    return c // 2


class Prog:
    ENGS = ("tensor", "vector", "scalar", "gpsimd", "sync")

    def __init__(self, nc):
        self.nc = nc
        self.steps = []
        self.sem = {}
        for e in self.ENGS:
            self.sem[e] = _S(nc.alloc_semaphore("sm_" + e))
        self.NQ = 4
        self.qsems = {"sync": [_S(nc.alloc_semaphore(f"sm_dsync{i}")) for i in range(self.NQ)],
                      "gpsimd": [_S(nc.alloc_semaphore(f"sm_dgps{i}")) for i in range(self.NQ)]}
        self.qidx = {"sync": 0, "gpsimd": 0}

    def extra_sem(self, name):
        return _S(self.nc.alloc_semaphore(name))

    def step(self, eng, emit, waits=(), dma=False, inc=None):
        extra = []
        if inc is not None:
            s, amt = inc
        elif dma:
            i = self.qidx[eng] % self.NQ
            self.qidx[eng] += 1
            s = self.qsems[eng][i]
            amt = 16
            if s.n > 0:
                extra = [(s, s.n)]
        else:
            s = self.sem[eng]
            amt = 1
        s.n += amt
        pt = (s, s.n)
        merged = {}
        for (ws, wv) in list(waits) + extra:
            if wv is None or wv <= 0:
                continue
            if id(ws) in merged:
                merged[id(ws)] = (ws, max(merged[id(ws)][1], wv))
            else:
                merged[id(ws)] = (ws, wv)
        self.steps.append((eng, emit, list(merged.values()), (s, amt)))
        return pt

    def pt(self, eng):
        key = eng
        return (self.sem[key], self.sem[key].n)

    def emit(self):
        nc = self.nc
        with nc.Block() as block:
            @block.tensor
            def _(e):
                self._run("tensor", e)

            @block.vector
            def _(e):
                self._run("vector", e)

            @block.scalar
            def _(e):
                self._run("scalar", e)

            @block.gpsimd
            def _(e):
                self._run("gpsimd", e)

            @block.sync
            def _(e):
                self._run("sync", e)

    def _run(self, name, eng):
        for (e, emit, waits, inc) in self.steps:
            if e != name:
                continue
            for (s, v) in waits:
                eng.wait_ge(s.h, v)
            ins = emit(eng)
            assert ins is not None, f"step on {name} returned no instruction"
            ins.then_inc(inc[0].h, inc[1])


class _S:
    __slots__ = ("h", "n")

    def __init__(self, handle):
        self.h = handle
        self.n = 0


class PsumPool:

    def __init__(self, prog, slots):
        self.p = prog
        self.slots = slots
        self.rel = [None] * len(slots)
        self.i = 0

    def take(self):
        i = self.i % len(self.slots)
        self.i += 1
        waits = [self.rel[i]] if self.rel[i] is not None else []
        return i, self.slots[i], waits

    def release(self, i, pt):
        self.rel[i] = pt


def build(debug=()):
    nc = bass.Bass("TRN2", target_bir_lowering=False)
    p = Prog(nc)

    xin = nc.dram_tensor("xin", [TOK, D], F32, kind="ExternalInput")
    wq = nc.dram_tensor("wq", [4, 128, FC, 512], F32, kind="ExternalInput")
    wk = nc.dram_tensor("wk", [4, 128, FC, 512], F32, kind="ExternalInput")
    wv = nc.dram_tensor("wv", [4, 128, FC, 512], F32, kind="ExternalInput")
    wo = nc.dram_tensor("wo", [4, 128, FC, 512], F32, kind="ExternalInput")
    wkl = nc.dram_tensor("wkl", [128, FC, DL], F32, kind="ExternalInput")
    wvl = nc.dram_tensor("wvl", [128, FC, DL], F32, kind="ExternalInput")
    upk = nc.dram_tensor("upk", [128, H, NDL, 128], F32, kind="ExternalInput")
    upv = nc.dram_tensor("upv", [128, H, NDL, 128], F32, kind="ExternalInput")
    gate = nc.dram_tensor("gate", [128, FC, 8], F32, kind="ExternalInput")
    upw = nc.dram_tensor("upw", [32, 128, FC, 256], F32, kind="ExternalInput")
    dnw = nc.dram_tensor("dnw", [4, 4, 128, FC, 512], F32, kind="ExternalInput")
    exw = nc.dram_tensor("exw", [4, 128, FC, 512], F32, kind="ExternalInput")
    wqb = nc.dram_tensor("wqb", [128, FC], F32, kind="ExternalInput")
    wkb = nc.dram_tensor("wkb", [128, FC], F32, kind="ExternalInput")
    wvb = nc.dram_tensor("wvb", [128, FC], F32, kind="ExternalInput")
    wklb = nc.dram_tensor("wklb", [128, NDL], F32, kind="ExternalInput")
    wvlb = nc.dram_tensor("wvlb", [128, NDL], F32, kind="ExternalInput")
    upkb = nc.dram_tensor("upkb", [128, H], F32, kind="ExternalInput")
    upvb = nc.dram_tensor("upvb", [D], F32, kind="ExternalInput")
    wob = nc.dram_tensor("wob", [D], F32, kind="ExternalInput")
    lng = nc.dram_tensor("lng", [D], F32, kind="ExternalInput")
    lnb = nc.dram_tensor("lnb", [D], F32, kind="ExternalInput")
    parity = nc.dram_tensor("parity", [1], F32, kind="ExternalInput")
    out = nc.dram_tensor("out", [TOK, D], F32, kind="ExternalOutput")

    kvk = nc.dram_tensor("kvk", [D + 1, TOK], BF16)
    kall = nc.dram_tensor("kall", [2 * (D + 1), TOK], BF16)
    kvv = nc.dram_tensor("kvv", [TOK, D], BF16)
    vall = nc.dram_tensor("vall", [S, D], BF16)
    xspill = nc.dram_tensor("xspill", [TOK, D], F32)

    DBG = {}

    def dbg(name, shape, dtype=F32):
        if name in debug:
            if name not in DBG:
                DBG[name] = nc.dram_tensor("dbg_" + name, shape, dtype, kind="ExternalOutput")
            return DBG[name]
        return None

    ARENA = 210112
    arena = nc.alloc_sbuf_tensor("arena", [128, ARENA], mybir.dt.int8).ap()

    def carve(off, dtype, shape):
        n = int(np.prod(shape)) * mybir.dt.size(dtype)
        assert off % 64 == 0 and off + n <= ARENA, (off, n)
        v = arena[:, off:off + n].bitcast(dtype)
        if len(shape) > 1:
            kw = {f"a{i}": shape[i] for i in range(len(shape))}
            v = v.rearrange("p (" + " ".join(kw) + ") -> p " + " ".join(kw), **kw)
        return v

    ident = carve(0, BF16, [128])
    tri = carve(256, BF16, [128])
    mlo = carve(512, BF16, [128])
    mhi = carve(768, BF16, [128])
    ones_bf = carve(1024, BF16, [1024])
    ident32 = carve(3072, F32, [128])
    gate_sb = carve(3584, F32, [FC, 8])
    wqb_c = carve(4096, F32, [16])
    wkb_c = carve(4160, F32, [16])
    wvb_c = carve(4224, F32, [16])
    wklb_c = carve(4288, F32, [16])
    wvlb_c = carve(4352, F32, [16])
    upkb_c = carve(4416, F32, [16])
    a2 = carve(4480, F32, [64])
    upvb_rep = carve(4736, BF16, [2048])
    qn_row = carve(8832, F32, [16])
    knmax_row = carve(8896, BF16, [32])
    mscr = carve(8960, BF16, [32])
    mtmp = carve(9024, F32, [16])
    negm = carve(9088, F32, [16])
    avr = carve(9152, F32, [16])
    stats = carve(9216, F32, [64])
    stats2 = carve(9472, F32, [64])
    gmax = carve(9728, F32, [64])
    scale_c = carve(9984, F32, [16])
    par_sb = carve(10048, F32, [16])
    zrow = carve(10112, BF16, [1024])

    R1 = 13504
    xld = carve(R1, F32, [2, 2048])
    xldb = carve(R1 + 16384, BF16, [2, 2048])
    kstg = carve(R1, BF16, [2, 1024])
    vstg = carve(R1 + 4096, BF16, [2, 8, 128])
    ksqb = carve(R1 + 12288, BF16, [1024])
    t1T = carve(R1 + 24576, BF16, [FC, 1024])
    latT = carve(R1 + 57344, BF16, [NDL, 1024])
    eSB = carve(R1, BF16, [2, KC, 1024])
    ybuf = carve(R1, F32, [NT, 2048])

    R2 = R1 + 65536
    xT = carve(R2, BF16, [FC, 1024])
    xTf = xT

    R3 = R2 + 32768
    qT = carve(R3, BF16, [H, 1024])
    wob_rep = carve(R3, F32, [2048])
    xrb = carve(R3 + 8192, F32, [2, 2048])
    xcastb = carve(R3 + 24576, BF16, [2, 2048])
    hq = carve(R3, BF16, [16, 1024])
    logits32 = carve(R3, F32, [1024])
    pbs32 = carve(R3 + 4096, F32, [1024])


    R4 = R3 + 32768
    wbuf = carve(R4, BF16, [2, FC, 512])
    kbuf = carve(R4, BF16, [2, 2048])
    vbuf = carve(R4 + 8192, BF16, [2, KC, 132])
    ostg = carve(R4 + 25600, BF16, [NT, 128])
    p7buf = carve(R4, BF16, [2, FC, 512])

    R5 = R4 + 32768
    upk_sb = carve(R5, BF16, [H, NDL, 128])
    xl32 = carve(R5, F32, [2, 512])
    upv_sb = carve(R5 + 16384, BF16, [H, NDL, 128])
    aoT = carve(R5, BF16, [H, 1024])
    gam_rep = carve(R5, F32, [2048])
    bet_rep = carve(R5 + 8192, F32, [2048])
    xlrb = carve(R5 + 16384, F32, [2, 2048])

    pb = [nc.alloc_psum_tensor(f"pb{i}", [128, 512], F32).ap() for i in range(6)]
    pt_ = [nc.alloc_psum_tensor(f"pt{i}", [128, 1024], BF16).ap() for i in range(2)]
    mm = PsumPool(p, pb[:5])
    aux = PsumPool(p, pb[5:6])
    tp = PsumPool(p, pt_)

    TRI = tri.bitcast(BF16)

    a2v = a2.rearrange("p (i j) -> p i j", i=8)[0:8]

    def prep_ms(g):
        g.memset(ident[:, 0:128], 0.0)
        g.memset(tri[:, 0:128], 0.0)
        g.memset(ones_bf[:, :], 1.0)
        g.memset(ident32[:, 0:128], 0.0)
        g.memset(knmax_row[0:1, :], 0.0)
        g.memset(zrow[0:1, :], 0.0)
        return g.memset(a2[0:8, :], 0.0)

    pt_ms = p.step("gpsimd", prep_ms)

    def prep_a1(g):
        g.affine_select(out=ident[:, 0:128], in_=ident[:, 0:128], compare_op=ALU.not_equal,
                        fill=1.0, base=0, pattern=[[-1, 128]], channel_multiplier=1)
        g.affine_select(out=tri[:, 0:128], in_=tri[:, 0:128], compare_op=ALU.is_gt,
                        fill=1.0, base=0, pattern=[[-1, 128]], channel_multiplier=1)
        g.affine_select(out=ident32[:, 0:128], in_=ident32[:, 0:128], compare_op=ALU.not_equal,
                        fill=1.0, base=0, pattern=[[-1, 128]], channel_multiplier=1)
        return g.affine_select(out=a2v, in_=a2v, compare_op=ALU.not_equal, fill=1.0,
                               base=0, pattern=[[-1, 8], [0, 8]], channel_multiplier=1)

    pt_a1 = p.step("gpsimd", prep_a1, waits=[pt_ms])
    pt_prep = p.step("gpsimd", lambda g: g.affine_select(
        out=a2v, in_=a2v, compare_op=ALU.not_equal, fill=1.0,
        base=0, pattern=[[0, 8], [-1, 8]], channel_multiplier=1), waits=[pt_a1])

    bias_pt = {}
    for nm, dst, srct in (("wqb", wqb_c, wqb), ("wkb", wkb_c, wkb), ("wvb", wvb_c, wvb),
                          ("upkb", upkb_c, upkb)):
        bias_pt[nm] = p.step("sync", (lambda dst=dst, srct=srct: lambda sy: sy.dma_start(
            out=dst[:, :], in_=srct[:, :]))(), dma=True)
    bias_pt["wklb"] = p.step("sync", lambda sy: sy.dma_start(out=wklb_c[:, 0:4], in_=wklb[:, :]), dma=True)
    bias_pt["wvlb"] = p.step("sync", lambda sy: sy.dma_start(out=wvlb_c[:, 0:4], in_=wvlb[:, :]), dma=True)
    pt_upvb = p.step("gpsimd", lambda g: g.dma_start(
        out=upvb_rep[:, :], in_=upvb[None, :].broadcast_to([128, 2048])), dma=True)
    pt_gate = p.step("sync", lambda sy: sy.dma_start(
        out=gate_sb.rearrange("p a b -> p (a b)"), in_=gate.rearrange("p a b -> p (a b)")), dma=True)

    pt_acol = p.step("sync", lambda sy: sy.dma_start(
        out=mtmp[:, 0:1], in_=parity[None, :].broadcast_to([128, 1])), dma=True)
    pt_msk0 = p.step("vector", lambda v: v.tensor_scalar_max(
        mlo[:, 0:128], tri[:, 0:128], mtmp[:, 0:1]), waits=[pt_acol, pt_prep])
    pt_masks = p.step("vector", lambda v: v.tensor_scalar_mul(
        mhi[:, 0:128], tri[:, 0:128], mtmp[:, 0:1]), waits=[pt_acol, pt_prep])

    cast_pts = []
    xt_evac_pts = []
    for t in range(NT):
        par = t % 2
        waits = [cast_pts[t - 2]] if t >= 2 else []
        pt_x = p.step("sync", (lambda t=t, par=par: lambda sy: sy.dma_start(
            out=xld[:, par], in_=xin[t * 128:(t + 1) * 128, :]))(), waits=waits, dma=True)
        pt_c = p.step("vector", (lambda par=par: lambda v: v.tensor_copy(
            xldb[:, par], xld[:, par]))(), waits=[pt_x])
        cast_pts.append(pt_c)
        for g8 in range(2):
            i, slot, twaits = tp.take()
            def mktr(par=par, g8=g8, slot=slot):
                def emit(te):
                    ins = None
                    for k in range(8):
                        c = g8 * 8 + k
                        ins = te.transpose(slot[:, k * 128:(k + 1) * 128],
                                           xldb[:, par, c * 128:(c + 1) * 128],
                                           ident[:, 0:128])
                    return ins
                return emit
            pt_t = p.step("tensor", mktr(), waits=[pt_c, pt_prep] + twaits)
            pt_e = p.step("vector", (lambda t=t, g8=g8, slot=slot: lambda v: v.tensor_copy(
                xT[:, g8 * 8:(g8 + 1) * 8, t * 128:(t + 1) * 128],
                slot.rearrange("p (a b) -> p a b", b=128)))(), waits=[pt_t])
            tp.release(i, pt_e)
            xt_evac_pts.append(pt_e)
    pt_xT = p.pt("vector")

    d = dbg("xT", [128, FC, TOK], BF16)
    if d is not None:
        p.step("sync", lambda sy, d=d: sy.dma_start(out=d.ap(), in_=xT[:, :, :]),
               waits=[pt_xT], dma=True)

    def proj_stream(wdram, bias_c, dst, nof, src_chunk, src_pt, extra_mm_waits=(),
                    first_war=(), bias_wait=None):
        nq = (nof + 3) // 4
        wpts = []
        qmm_pts = []
        evac_pts = [None] * nof
        for q in range(nq):
            par = q % 2
            waits = [qmm_pts[q - 2]] if q >= 2 else list(first_war)
            wpt = p.step("gpsimd", (lambda q=q, par=par: lambda g: g.dma_start(
                out=wbuf[:, par], in_=wdram[q]))(), waits=waits, dma=True)
            wpts.append(wpt)
            for oc in range(min(4, nof - q * 4)):
                of = q * 4 + oc
                for piece in range(2):
                    cs = slice(piece * 512, (piece + 1) * 512)
                    bi, slot, bwaits = mm.take()
                    def mk(par=par, oc=oc, cs=cs, slot=slot):
                        def emit(te):
                            ins = None
                            for c in range(FC):
                                ins = te.matmul(slot[:, 0:512],
                                                wbuf[:, par, c, oc * 128:(oc + 1) * 128],
                                                src_chunk(c)[:, cs],
                                                start=(c == 0), stop=(c == FC - 1))
                            return ins
                        return emit
                    p.step("tensor", mk(), waits=[wpt, src_pt] + bwaits + list(extra_mm_waits))
                    mmpt = p.pt("tensor")
                    def ev(of=of, cs=cs, slot=slot):
                        def emit(v):
                            if bias_c is not None:
                                return v.tensor_scalar_add(dst[:, of, cs], slot[:, 0:512],
                                                           bias_c[:, of:of + 1])
                            return v.tensor_copy(dst[:, of, cs], slot[:, 0:512])
                        return emit
                    ept = p.step("vector", ev(), waits=[mmpt] + ([bias_wait] if bias_wait else []))
                    mm.release(bi, ept)
                    evac_pts[of] = ept
            qmm_pts.append(p.pt("tensor"))
        return evac_pts, p.pt("tensor")

    t1_pts, pt_t1mm = proj_stream([wk[q] for q in range(4)], wkb_c, t1T, FC,
                                  lambda c: xT[:, c], pt_xT, bias_wait=bias_pt["wkb"])
    pt_t1 = p.pt("vector")

    pt_wkl = p.step("gpsimd", lambda g: g.dma_start(out=wbuf[:, 0], in_=wkl[:, :, :]),
                    waits=[pt_t1mm], dma=True)
    lat_pts = []
    for of in range(NDL):
        for piece in range(2):
            cs = slice(piece * 512, (piece + 1) * 512)
            bi, slot, bwaits = mm.take()
            def mk(of=of, cs=cs, slot=slot):
                def emit(te):
                    ins = None
                    for c in range(FC):
                        ins = te.matmul(slot[:, 0:512], wbuf[:, 0, c, of * 128:(of + 1) * 128],
                                        t1T[:, c, cs], start=(c == 0), stop=(c == FC - 1))
                    return ins
                return emit
            p.step("tensor", mk(), waits=[pt_wkl, pt_t1] + bwaits)
            mmpt = p.pt("tensor")
            ept = p.step("vector", (lambda of=of, cs=cs, slot=slot: lambda v: v.tensor_scalar_add(
                latT[:, of, cs], slot[:, 0:512], wklb_c[:, of:of + 1]))(),
                waits=[mmpt, bias_pt["wklb"]])
            mm.release(bi, ept)
            lat_pts.append(ept)
    pt_klat = p.pt("vector")
    pt_klatmm = p.pt("tensor")

    pt_upk = p.step("gpsimd", lambda g: g.dma_start(
        out=upk_sb.rearrange("p a b c -> p (a b c)"), in_=upk.rearrange("p a b c -> p (a b c)")), dma=True)
    pt_upv = p.step("gpsimd", lambda g: g.dma_start(
        out=upv_sb.rearrange("p a b c -> p (a b c)"), in_=upv.rearrange("p a b c -> p (a b c)")), dma=True)

    kvw_pts = []
    ksq_pts = []
    knmm_pts = []
    nrm_comb = []
    for h in range(H):
        par = h % 2
        kpts = []
        for piece in range(2):
            cs = slice(piece * 512, (piece + 1) * 512)
            bi, slot, bwaits = mm.take()
            def mk(h=h, cs=cs, slot=slot):
                def emit(te):
                    ins = None
                    for c in range(NDL):
                        ins = te.matmul(slot[:, 0:512], upk_sb[:, h, c, :], latT[:, c, cs],
                                        start=(c == 0), stop=(c == NDL - 1))
                    return ins
                return emit
            p.step("tensor", mk(), waits=[pt_upk, pt_klat] + bwaits)
            mmpt = p.pt("tensor")
            w = [mmpt, bias_pt["upkb"]]
            if h >= 2:
                w += [kvw_pts[h - 2], ksq_pts[h - 2]]
            ept = p.step("vector", (lambda par=par, h=h, cs=cs, slot=slot: lambda v: v.tensor_scalar_add(
                kstg[:, par, cs], slot[:, 0:512], upkb_c[:, h:h + 1]))(), waits=w)
            mm.release(bi, ept)
            kpts.append(ept)
        wpt = p.step("sync", (lambda h=h, par=par: lambda sy: sy.dma_start(
            out=kvk[h * 128:(h + 1) * 128, :], in_=kstg[:, par]))(), waits=kpts, dma=True)
        kvw_pts.append(wpt)
        sqw = kpts + ([knmm_pts[h - 1]] if h >= 1 else [])
        pt_sq = p.step("scalar", (lambda par=par: lambda sc: sc.activation(
            ksqb[:, :], kstg[:, par], AF.Square, bias=fp(0.0), scale=fp(1.0)))(), waits=sqw)
        ksq_pts.append(pt_sq)
        red_pts = []
        for piece in range(2):
            cs = slice(piece * 512, (piece + 1) * 512)
            bi, slot, bwaits = aux.take()
            pt_nm = p.step("tensor", (lambda cs=cs, slot=slot: lambda te: te.matmul(
                slot[0:1, 0:512], ones_bf[:, 0:1], ksqb[:, cs], start=True, stop=True))(),
                waits=[pt_sq, pt_prep] + bwaits)
            rpt = p.step("vector", (lambda piece=piece, slot=slot: lambda v: v.reduce_max(
                mtmp[0:1, 8 + piece:9 + piece], slot[0:1, 0:512], axis=AX.X))(),
                waits=[pt_nm] + ([nrm_comb[-1]] if nrm_comb else []))
            aux.release(bi, rpt)
            red_pts.append(rpt)
        knmm_pts.append(p.pt("tensor"))
        nrm_comb.append(p.step("vector", (lambda h=h: lambda v: v.tensor_tensor(
            knmax_row[0:1, h:h + 1], mtmp[0:1, 8:9], mtmp[0:1, 9:10], ALU.max))(), waits=red_pts))
    pt_knorm = p.pt("vector")
    pt_knT = p.pt("tensor")
    pt_zf = p.step("sync", lambda sy: sy.dma_start(
        out=kvk[D:D + 1, :], in_=zrow[0:1, :]), waits=[pt_ms], dma=True)
    kvw_pts.append(pt_zf)
    kvw_pts.append(p.step("sync", lambda sy: sy.dma_start(
        out=kvk[D:D + 1, 0:32], in_=knmax_row[0:1, 0:32]), waits=[pt_knorm, pt_zf], dma=True))

    s_agk = p.extra_sem("s_agk")
    pt_agk = p.step("gpsimd", lambda g: g.collective_compute(
        "AllGather", ALU.bypass, replica_groups=[[0, 1], [2, 3], [4, 5], [6, 7]],
        ins=[kvk.ap()], outs=[kall.ap()]), waits=kvw_pts, inc=(s_agk, 1))

    t2_pts, pt_t2mm = proj_stream([wv[q] for q in range(4)], wvb_c, t1T, FC,
                                  lambda c: xT[:, c], pt_xT, extra_mm_waits=[pt_klatmm],
                                  first_war=[pt_klatmm], bias_wait=bias_pt["wvb"])
    pt_t2 = p.pt("vector")

    pt_wvl = p.step("gpsimd", lambda g: g.dma_start(out=wbuf[:, 0], in_=wvl[:, :, :]),
                    waits=[pt_t2mm], dma=True)
    for of in range(NDL):
        for piece in range(2):
            cs = slice(piece * 512, (piece + 1) * 512)
            bi, slot, bwaits = mm.take()
            def mk(of=of, cs=cs, slot=slot):
                def emit(te):
                    ins = None
                    for c in range(FC):
                        ins = te.matmul(slot[:, 0:512], wbuf[:, 0, c, of * 128:(of + 1) * 128],
                                        t1T[:, c, cs], start=(c == 0), stop=(c == FC - 1))
                    return ins
                return emit
            p.step("tensor", mk(), waits=[pt_wvl, pt_t2] + bwaits)
            mmpt = p.pt("tensor")
            ept = p.step("vector", (lambda of=of, cs=cs, slot=slot: lambda v: v.tensor_scalar_add(
                latT[:, of, cs], slot[:, 0:512], wvlb_c[:, of:of + 1]))(),
                waits=[mmpt, bias_pt["wvlb"]])
            mm.release(bi, ept)
    pt_vlat = p.pt("vector")
    pt_vlmm = p.pt("tensor")

    vst_pts = []
    for h in range(H):
        par = h % 2
        ev_pts = []
        for t in range(NT):
            bi, slot, bwaits = mm.take()
            def mk(h=h, t=t, slot=slot):
                def emit(te):
                    ins = None
                    for c in range(NDL):
                        ins = te.matmul(slot[:, 0:128], latT[:, c, t * 128:(t + 1) * 128],
                                        upv_sb[:, h, c, :], start=(c == 0), stop=(c == NDL - 1))
                    return ins
                return emit
            p.step("tensor", mk(), waits=[pt_upv, pt_vlat] + bwaits)
            mmpt = p.pt("tensor")
            war = [vst_pts[h - 2]] if h >= 2 else []
            ept = p.step("vector", (lambda h=h, t=t, par=par, slot=slot: lambda v: v.tensor_tensor(
                vstg[:, par, t, :], slot[:, 0:128], upvb_rep[:, h * 128:(h + 1) * 128], ALU.add))(),
                waits=[mmpt, pt_upvb] + war)
            mm.release(bi, ept)
            ev_pts.append(ept)
        wpt = p.step("sync", (lambda h=h, par=par: lambda sy: sy.dma_start(
            out=kvv[:, h * 128:(h + 1) * 128].rearrange("(a p) d -> p a d", p=128),
            in_=vstg[:, par]))(), waits=ev_pts, dma=True)
        vst_pts.append(wpt)

    s_agv = p.extra_sem("s_agv")
    pt_agv = p.step("gpsimd", lambda g: g.collective_compute(
        "AllGather", ALU.bypass, replica_groups=[[0, 1], [2, 3], [4, 5], [6, 7]],
        ins=[kvv.ap()], outs=[vall.ap()]), waits=vst_pts, inc=(s_agv, 1))

    d = dbg("kvk", [D + 1, TOK], BF16)
    if d is not None:
        p.step("gpsimd", lambda g, d=d: g.dma_start(out=d.ap(), in_=kvk.ap()),
               waits=kvw_pts, dma=True)
    d = dbg("kvv", [TOK, D], BF16)
    if d is not None:
        p.step("gpsimd", lambda g, d=d: g.dma_start(out=d.ap(), in_=kvv.ap()),
               waits=vst_pts, dma=True)

    q_pts, pt_qmm = proj_stream([wq[q] for q in range(4)], wqb_c, qT, FC,
                                lambda c: xT[:, c], pt_xT, first_war=[pt_vlmm],
                                bias_wait=bias_pt["wqb"])
    qnmm_last = pt_knT
    for h in range(H):
        pt_sq = p.step("scalar", (lambda h=h: lambda sc: sc.activation(
            ksqb[:, :], qT[:, h], AF.Square, bias=fp(0.0), scale=fp(1.0)))(),
            waits=[q_pts[h], qnmm_last])
        red_pts = []
        for piece in range(2):
            cs = slice(piece * 512, (piece + 1) * 512)
            bi, slot, bwaits = aux.take()
            pt_nm = p.step("tensor", (lambda cs=cs, slot=slot: lambda te: te.matmul(
                slot[0:1, 0:512], ones_bf[:, 0:1], ksqb[:, cs], start=True, stop=True))(),
                waits=[pt_sq] + bwaits)
            rpt = p.step("vector", (lambda piece=piece, slot=slot: lambda v: v.reduce_max(
                mtmp[0:1, 8 + piece:9 + piece], slot[0:1, 0:512], axis=AX.X))(),
                waits=[pt_nm] + ([nrm_comb[-1]] if nrm_comb else []))
            aux.release(bi, rpt)
            red_pts.append(rpt)
        qnmm_last = p.pt("tensor")
        nrm_comb.append(p.step("vector", (lambda h=h: lambda v: v.tensor_tensor(
            qn_row[0:1, h:h + 1], mtmp[0:1, 8:9], mtmp[0:1, 9:10], ALU.max))(), waits=red_pts))
    pt_qnorm = p.pt("vector")
    pt_vones = p.step("gpsimd", lambda g: g.memset(vbuf[:, :, :, 128:129], 1.0),
                      waits=[pt_qmm])

    d = dbg("qT", [128, H, TOK], BF16)
    if d is not None:
        p.step("sync", lambda sy, d=d: sy.dma_start(out=d.ap(), in_=qT[:, :, :]),
               waits=[p.pt("vector")], dma=True)

    pt_m1 = p.step("sync", lambda sy: sy.dma_start(out=mscr[0:1, 0:16], in_=kall[D:D + 1, 0:16]),
                   waits=[(s_agk, 1)], dma=True)
    pt_m2 = p.step("sync", lambda sy: sy.dma_start(out=mscr[0:1, 16:32], in_=kall[2 * D + 1:2 * D + 2, 0:16]),
                   waits=[(s_agk, 1)], dma=True)
    pt_m3 = p.step("vector", lambda v: v.reduce_max(mtmp[0:1, 0:1], mscr[0:1, 0:32], axis=AX.X),
                   waits=[pt_m1, pt_m2])
    pt_m4 = p.step("vector", lambda v: v.reduce_max(mtmp[0:1, 1:2], qn_row[0:1, 0:16], axis=AX.X),
                   waits=[pt_qnorm])
    pt_m5 = p.step("vector", lambda v: v.tensor_tensor(mtmp[0:1, 2:3], mtmp[0:1, 0:1],
                                                       mtmp[0:1, 1:2], ALU.mult),
                   waits=[pt_m3, pt_m4])
    pt_m6 = p.step("scalar", lambda sc: sc.activation(mtmp[0:1, 3:4], mtmp[0:1, 2:3], AF.Ln,
                                                      bias=fp(1e-38), scale=fp(1.0)), waits=[pt_m5])
    pt_m7 = p.step("scalar", lambda sc: sc.activation(mtmp[0:1, 4:5], mtmp[0:1, 3:4], AF.Exp,
                                                      bias=fp(0.0), scale=fp(0.5)), waits=[pt_m6])
    pt_m8 = p.step("vector", lambda v: v.tensor_scalar_mul(mtmp[0:1, 5:6], mtmp[0:1, 4:5],
                                                           -SCALE), waits=[pt_m7])
    mdram = nc.dram_tensor("mdram", [1, 1], F32)
    pt_m9 = p.step("sync", lambda sy: sy.dma_start(
        out=mdram[0:1, 0:1], in_=mtmp[0:1, 5:6]), waits=[pt_m8], dma=True)
    pt_mprep = p.step("sync", lambda sy: sy.dma_start(
        out=negm[:, 0:1], in_=mdram[0:1, 0:1].broadcast_to([128, 1])), waits=[pt_m9], dma=True)

    kb_pts = [None] * (H + 2)
    vb_pts = [None] * (H + 2)
    scores_done = [None] * H
    av_done = [None] * H
    avr_mu = [None] * NT
    ostg_war = [None] * NT
    exp_done = [None] * H
    ao_pts = []

    head_ctx = {}

    def emit_av(h):
        par, vpts, mask_pts, cexp = head_ctx[h]
        mu_pts = []
        for j in range(NT):
            bi, slot, bwaits = mm.take()
            def mkav(par=par, j=j, slot=slot):
                def emit(te):
                    ins = None
                    for c in range(2 * j + 2):
                        ins = te.matmul(slot[:, 0:129], eSB[:, par, c, j * 128:(j + 1) * 128],
                                        vbuf[:, par, c, 0:129],
                                        start=(c == 0), stop=(c == 2 * j + 1))
                    return ins
                return emit
            avw = [mask_pts[2 * j + 1]] + [pt for c in range(2 * j + 2) for pt in cexp[c]]
            p.step("tensor", mkav(), waits=avw + vpts + bwaits + [pt_vones])
            avpt = p.pt("tensor")
            pt_r = p.step("vector", (lambda j=j, slot=slot: lambda v: v.reciprocal(
                avr[:, j:j + 1], slot[:, 128:129]))(),
                waits=[avpt] + ([avr_mu[j]] if avr_mu[j] is not None else []))
            pt_mu = p.step("vector", (lambda j=j, slot=slot: lambda v: v.tensor_scalar_mul(
                ostg[:, j], slot[:, 0:128], avr[:, j:j + 1]))(),
                waits=[pt_r] + ([ostg_war[j]] if ostg_war[j] is not None else []))
            avr_mu[j] = pt_mu
            mm.release(bi, pt_mu)
            mu_pts.append(pt_mu)
        ti, tslot, twaits = tp.take()
        def mktr(h=h, tslot=tslot):
            def emit(te):
                ins = None
                for j in range(NT):
                    ins = te.transpose(tslot[:, j * 128:(j + 1) * 128], ostg[:, j],
                                       ident[:, 0:128])
                return ins
            return emit
        pt_tr = p.step("tensor", mktr(), waits=mu_pts + twaits)
        for j in range(NT):
            ostg_war[j] = pt_tr
        pt_ev = p.step("vector", (lambda h=h, tslot=tslot: lambda v: v.tensor_copy(
            aoT[:, h, :], tslot[:, 0:1024]))(), waits=[pt_tr])
        tp.release(ti, pt_ev)
        ao_pts.append((pt_tr, pt_ev))
        av_done[h] = p.pt("tensor")

    for h in range(H):
        par = h % 2
        kwar = [scores_done[h - 2]] if h >= 2 else [pt_qmm]
        vwar = [av_done[h - 2]] if h >= 2 else [pt_qmm]
        kv = kbuf[:, par].rearrange("p (c j) -> p c j", j=128)
        kpts = []
        vpts = []
        for r in range(2):
            kpts.append(p.step("sync", (lambda h=h, r=r, kv=kv: lambda sy: sy.dma_start(
                out=kv[:, r::2, :],
                in_=kall[r * (D + 1) + h * 128: r * (D + 1) + (h + 1) * 128, :]
                .rearrange("p (a b) -> p a b", b=128)))(),
                waits=[(s_agk, 1)] + kwar, dma=True))
            vpts.append(p.step("sync", (lambda h=h, r=r, par=par: lambda sy: sy.dma_start(
                out=vbuf[:, par, r::2, 0:128],
                in_=vall[r * TOK:(r + 1) * TOK, h * 128:(h + 1) * 128]
                .rearrange("(a p) d -> p a d", p=128)))(),
                waits=[(s_agv, 1)] + vwar, dma=True))
        kb_pts[h] = kpts
        vb_pts[h] = vpts

        esb_war = [av_done[h - 2]] if h >= 2 else []
        chunk_exp_pts = []
        for c in range(KC):
            off = jmin(c) * 128
            cpts = []
            while off < TOK:
                wd = min(512, TOK - off)
                bi, slot, bwaits = mm.take()
                p.step("tensor", (lambda par=par, c=c, off=off, wd=wd, slot=slot, h=h: lambda te: te.matmul(
                    slot[:, 0:wd], kbuf[:, par, c * 128:(c + 1) * 128], qT[:, h, off:off + wd],
                    start=True, stop=True))(), waits=kpts + [q_pts[h]] + bwaits)
                mmpt = p.pt("tensor")
                ept = p.step("scalar", (lambda par=par, c=c, off=off, wd=wd, slot=slot: lambda sc: sc.activation(
                    eSB[:, par, c, off:off + wd], slot[:, 0:wd], AF.Exp,
                    bias=negm[:, 0:1], scale=fp(SCALE)))(),
                    waits=[mmpt, pt_mprep] + esb_war)
                mm.release(bi, ept)
                cpts.append(ept)
                off += wd
            chunk_exp_pts.append(cpts)
        scores_done[h] = p.pt("tensor")
        exp_done[h] = p.pt("scalar")

        mask_pts = []
        for c in range(KC):
            m = mlo if c % 2 == 0 else mhi
            o = jmin(c) * 128
            mask_pts.append(p.step("vector", (lambda par=par, c=c, m=m, o=o: lambda v: v.tensor_tensor(
                eSB[:, par, c, o:o + 128], eSB[:, par, c, o:o + 128],
                m[:, 0:128], ALU.mult))(), waits=[chunk_exp_pts[c][0], pt_masks]))
        pt_mask = mask_pts[-1]

        if h == 0:
            dkb = dbg("kbuf0", [128, 2048], BF16)
            if dkb is not None:
                p.step("sync", (lambda dkb=dkb: lambda sy: sy.dma_start(
                    out=dkb.ap(), in_=kbuf[:, 0]))(), waits=kpts, dma=True)
            dka = dbg("kall", [2 * (D + 1), TOK], BF16)
            if dka is not None:
                p.step("gpsimd", (lambda dka=dka: lambda g: g.dma_start(
                    out=dka.ap(), in_=kall.ap()))(), waits=[(s_agk, 1)], dma=True)
            d0 = dbg("eS0", [128, KC, TOK], BF16)
            if d0 is not None:
                p.step("sync", (lambda d0=d0: lambda sy: sy.dma_start(
                    out=d0.ap(), in_=eSB[:, 0]))(), waits=[pt_mask], dma=True)
            dm = dbg("masks", [128, 3, 128], BF16)
            if dm is not None:
                p.step("sync", (lambda dm=dm: lambda sy: sy.dma_start(
                    out=dm.ap()[:, 0], in_=tri[:, 0:128]))(), waits=[pt_mask], dma=True)
                p.step("sync", (lambda dm=dm: lambda sy: sy.dma_start(
                    out=dm.ap()[:, 1], in_=mlo[:, 0:128]))(), waits=[pt_mask], dma=True)
                p.step("sync", (lambda dm=dm: lambda sy: sy.dma_start(
                    out=dm.ap()[:, 2], in_=mhi[:, 0:128]))(), waits=[pt_mask], dma=True)
            dn_ = dbg("negm", [1, 16])
            if dn_ is not None:
                p.step("sync", (lambda dn_=dn_: lambda sy: sy.dma_start(
                    out=dn_.ap(), in_=mtmp[0:1, 0:16]))(), waits=[pt_mprep], dma=True)
            dnc = dbg("negmcol", [128, 1])
            if dnc is not None:
                p.step("sync", (lambda dnc=dnc: lambda sy: sy.dma_start(
                    out=dnc.ap(), in_=negm[:, 0:1]))(), waits=[pt_mprep], dma=True)
        head_ctx[h] = (par, vpts, mask_pts, chunk_exp_pts)
        if h >= 1:
            emit_av(h - 1)
    emit_av(H - 1)
    pt_attn_v = p.pt("vector")

    d = dbg("aoT", [128, H, TOK], BF16)
    if d is not None:
        p.step("sync", lambda sy, d=d: sy.dma_start(out=d.ap(), in_=aoT[:, :, :]),
               waits=[pt_attn_v], dma=True)
    d = dbg("aoTb", [128, H, TOK], BF16)
    if d is not None:
        p.step("sync", lambda sy, d=d: sy.dma_start(out=d.ap(), in_=aoT[:, :, :]),
               waits=[pt_attn_v], dma=True)

    pt_wob = p.step("sync", lambda sy: sy.dma_start(
        out=wob_rep[:, :], in_=wob[None, :].broadcast_to([128, 2048])),
        waits=[scores_done[H - 1]], dma=True)

    da2 = dbg("aoT2", [128, H, TOK], BF16)
    if da2 is not None:
        p.step("sync", (lambda da2=da2: lambda sy: sy.dma_start(
            out=da2.ap(), in_=aoT[:, :, :]))(), waits=[pt_attn_v], dma=True)

    wo_ev = [[None] * 4 for _ in range(NT)]
    woq_mm = []
    for q in range(4):
        par = q % 2
        wwaits = [woq_mm[q - 2]] if q >= 2 else [av_done[H - 1]]
        wpt = p.step("gpsimd", (lambda q=q, par=par: lambda g: g.dma_start(
            out=wbuf[:, par], in_=wo[q]))(), waits=wwaits, dma=True)
        for t in range(NT):
            bi, slot, bwaits = mm.take()
            def mkwo(t=t, par=par, slot=slot):
                def emit(te):
                    ins = None
                    for hh in range(FC):
                        ins = te.matmul(slot[:, 0:512], aoT[:, hh, t * 128:(t + 1) * 128],
                                        wbuf[:, par, hh, :],
                                        start=(hh == 0), stop=(hh == FC - 1))
                    return ins
                return emit
            p.step("tensor", mkwo(), waits=[wpt, pt_attn_v] + bwaits)
            mmpt = p.pt("tensor")
            ept = p.step("vector", (lambda t=t, q=q, slot=slot: lambda v: v.tensor_tensor(
                ybuf[:, t, q * 512:(q + 1) * 512], slot[:, 0:512],
                wob_rep[:, q * 512:(q + 1) * 512], ALU.add))(), waits=[mmpt, pt_wob])
            mm.release(bi, ept)
            wo_ev[t][q] = ept
        woq_mm.append(p.pt("tensor"))

    lg_pts = []
    xl_mm = [None, None]

    def emit_gate_tile(t, gate_in_waits):
        gi, gslot, gwaits = aux.take()
        for cq in range(4):
            qpar = cq % 2
            bi, slot, bwaits = mm.take()
            tr_last = None
            for i in range(4):
                c = cq * 4 + i
                p.step("tensor", (lambda t=t, c=c, i=i, slot=slot: lambda te: te.transpose(
                    slot[:, i * 128:(i + 1) * 128].bitcast(F32),
                    ybuf[:, t, c * 128:(c + 1) * 128],
                    ident32[:, 0:128]))(), waits=gate_in_waits + [pt_prep] + (bwaits if i == 0 else []))
            tr_last = p.pt("tensor")
            war = [xl_mm[qpar]] if xl_mm[qpar] is not None else []
            pev = p.step("vector", (lambda qpar=qpar, slot=slot: lambda v: v.tensor_copy(
                xl32[:, qpar], slot[:, 0:512]))(), waits=[tr_last] + war)
            mm.release(bi, pev)
            def mkg(cq=cq, qpar=qpar, gslot=gslot):
                def emit(te):
                    ins = None
                    for i in range(4):
                        c = cq * 4 + i
                        ins = te.matmul(gslot[0:8, 0:128], gate_sb[:, c, :],
                                        xl32[:, qpar, i * 128:(i + 1) * 128],
                                        start=(c == 0), stop=(c == FC - 1))
                    return ins
                return emit
            p.step("tensor", mkg(), waits=[pev, pt_gate] + (gwaits if cq == 0 else []))
            xl_mm[qpar] = p.pt("tensor")
        mmpt = p.pt("tensor")
        ept = p.step("vector", (lambda t=t, gslot=gslot: lambda v: v.tensor_copy(
            logits32[0:8, t * 128:(t + 1) * 128], gslot[0:8, 0:128]))(), waits=[mmpt])
        aux.release(gi, ept)
        lg_pts.append(ept)


    ln1_pts = []
    xtf_tr_pts = []
    for t in range(NT):
        par = t % 2
        xwar = [ln1_pts[t - 2][3]] if t >= 2 else [scores_done[H - 1]]
        pt_xr = p.step("sync", (lambda t=t, par=par: lambda sy: sy.dma_start(
            out=xrb[:, par], in_=xin[t * 128:(t + 1) * 128, :]))(),
            waits=xwar, dma=True)
        pt_add = p.step("vector", (lambda t=t, par=par: lambda v: v.scalar_tensor_tensor(
            ybuf[:, t, :], ybuf[:, t, :], 0.0, xrb[:, par, :], ALU.add, ALU.add,
            accum_out=stats2[:, t:t + 1]))(), waits=wo_ev[t] + [pt_xr])
        ev_pts = [pt_add]
        dy1 = dbg("y1d", [TOK, D])
        if dy1 is not None:
            dpt = p.step("sync", (lambda t=t, dy1=dy1: lambda sy: sy.dma_start(
                out=dy1.ap()[t * 128:(t + 1) * 128, :], in_=ybuf[:, t]))(),
                waits=ev_pts, dma=True)
            ev_pts = ev_pts + [dpt]
        pt_s2 = p.step("vector", (lambda t=t: lambda v: v.tensor_scalar_mul(
            stats2[:, t:t + 1], stats2[:, t:t + 1], INVD))(), waits=ev_pts)
        sq_pts = []
        for ob in range(4):
            bi, slot, bwaits = aux.take()
            spt = p.step("scalar", (lambda t=t, ob=ob, slot=slot: lambda sc: sc.activation(
                slot[:, 0:512], ybuf[:, t, ob * 512:(ob + 1) * 512], AF.Square,
                bias=fp(0.0), scale=fp(1.0),
                accum_out=stats[:, 32 + t * 4 + ob:32 + t * 4 + ob + 1]))(),
                waits=ev_pts + bwaits)
            aux.release(bi, spt)
            sq_pts.append(spt)
        pt_s3 = p.step("vector", (lambda t=t: lambda v: v.reduce_sum(
            stats2[:, 16 + t:17 + t], stats[:, 32 + t * 4:32 + t * 4 + 4], axis=AX.X))(), waits=sq_pts)
        pt_s5 = p.step("vector", (lambda t=t: lambda v: v.tensor_scalar_mul(
            stats2[:, 16 + t:17 + t], stats2[:, 16 + t:17 + t], INVD))(), waits=[pt_s3])
        pt_s6 = p.step("vector", (lambda t=t: lambda v: v.tensor_tensor(
            stats2[:, 32 + t:33 + t], stats2[:, t:t + 1], stats2[:, t:t + 1], ALU.mult))(),
            waits=[pt_s2])
        pt_s7 = p.step("vector", (lambda t=t: lambda v: v.tensor_tensor(
            stats2[:, 16 + t:17 + t], stats2[:, 16 + t:17 + t], stats2[:, 32 + t:33 + t],
            ALU.subtract))(), waits=[pt_s5, pt_s6])
        pt_s8 = p.step("scalar", (lambda t=t: lambda sc: sc.activation(
            stats2[:, 32 + t:33 + t], stats2[:, 16 + t:17 + t], AF.Ln, bias=fp(EPS),
            scale=fp(1.0)))(), waits=[pt_s7])
        pt_s9 = p.step("scalar", (lambda t=t: lambda sc: sc.activation(
            stats2[:, 16 + t:17 + t], stats2[:, 32 + t:33 + t], AF.Exp, bias=fp(0.0),
            scale=fp(-0.5)))(), waits=[pt_s8])
        nrm_pts = []
        for ob in range(4):
            npt = p.step("vector", (lambda t=t, ob=ob: lambda v: v.tensor_scalar(
                ybuf[:, t, ob * 512:(ob + 1) * 512], ybuf[:, t, ob * 512:(ob + 1) * 512],
                stats2[:, t:t + 1], stats2[:, 16 + t:17 + t], ALU.subtract, ALU.mult))(),
                waits=[pt_s9] + sq_pts)
            nrm_pts.append(npt)
        emit_gate_tile(t, nrm_pts)
        pt_sp = p.step("sync", (lambda t=t: lambda sy: sy.dma_start(
            out=xspill[t * 128:(t + 1) * 128, :], in_=ybuf[:, t]))(), waits=nrm_pts, dma=True)
        cwar = [xtf_tr_pts[-3]] if len(xtf_tr_pts) >= 3 else [av_done[H - 1]]
        pt_cast = p.step("vector", (lambda t=t, par=par: lambda v: v.tensor_copy(
            xcastb[:, par], ybuf[:, t]))(), waits=nrm_pts + cwar)
        tprs = []
        for g8 in range(2):
            ti, tslot, twaits = tp.take()
            def mktr(par=par, g8=g8, tslot=tslot):
                def emit(te):
                    ins = None
                    for k in range(8):
                        c = g8 * 8 + k
                        ins = te.transpose(tslot[:, k * 128:(k + 1) * 128],
                                           xcastb[:, par, c * 128:(c + 1) * 128],
                                           ident[:, 0:128])
                    return ins
                return emit
            ptr = p.step("tensor", mktr(), waits=[pt_cast] + twaits)
            pev = p.step("vector", (lambda t=t, g8=g8, tslot=tslot: lambda v: v.tensor_copy(
                xTf[:, g8 * 8:(g8 + 1) * 8, t * 128:(t + 1) * 128],
                tslot.rearrange("p (a b) -> p a b", b=128)))(), waits=[ptr])
            tp.release(ti, pev)
            xtf_tr_pts.append(ptr)
            tprs.append(pev)
        ln1_pts.append((pt_sp, tprs[-1], pt_cast, p.pt("vector")))
    pt_xTf = p.pt("vector")
    pt_xTf_T = p.pt("tensor")

    d = dbg("xln1", [TOK, D])
    if d is not None:
        p.step("gpsimd", lambda g, d=d: g.dma_start(out=d.ap(), in_=xspill.ap()),
               waits=[pt for (pt, _, _, _) in ln1_pts], dma=True)

    pb_pts = []
    for piece in range(2):
        cs = slice(piece * 512, (piece + 1) * 512)
        bi, slot, bwaits = aux.take()
        p.step("tensor", (lambda cs=cs, slot=slot: lambda te: te.matmul(
            slot[0:64, 0:512], a2[0:8, 0:64], logits32[0:8, cs], start=True, stop=True))(),
            waits=lg_pts + bwaits + [pt_prep])
        mmpt = p.pt("tensor")
        ept = p.step("vector", (lambda cs=cs, slot=slot: lambda v: v.tensor_copy(
            pbs32[0:64, cs], slot[0:64, 0:512]))(), waits=[mmpt])
        aux.release(bi, ept)
        pb_pts.append(ept)
    sc_pts = []
    for t in range(NT):
        bi, slot, bwaits = mm.take()
        ptr = p.step("tensor", (lambda t=t, slot=slot: lambda te: te.transpose(
            slot[:, 0:64].bitcast(F32), pbs32[0:64, t * 128:(t + 1) * 128],
            ident32[0:64, 0:64]))(), waits=pb_pts + bwaits)
        pcv = p.step("vector", (lambda slot=slot: lambda v: v.tensor_copy(
            gmax[:, 0:64], slot[:, 0:64]))(), waits=[ptr] + (sc_pts[-1:] if sc_pts else []))
        mm.release(bi, pcv)
        pms = p.step("vector", lambda v: v.memset(gmax[:, 0:64:9], -1e30), waits=[pcv])
        prd = p.step("vector", (lambda t=t: lambda v: v.reduce_max(
            scale_c[:, t:t + 1], gmax[:, 0:64], axis=AX.X))(), waits=[pms])
        sc_pts.append(prd)
    pt_gate_all = p.pt("vector")

    p7_last = [pt_xTf_T, pt_xTf_T]
    y2_pts = [[None] * 4 for _ in range(NT)]

    def emit_expert():
        for q in range(4):
            par = q % 2
            wpt = p.step("gpsimd", (lambda q=q, par=par: lambda g: g.dma_start(
                out=p7buf[:, par], in_=exw[q]))(), waits=[p7_last[par]], dma=True)
            for t in range(NT):
                bi, slot, bwaits = mm.take()
                def mke(t=t, par=par, slot=slot):
                    def emit(te):
                        ins = None
                        for c in range(FC):
                            ins = te.matmul(slot[:, 0:512], xTf[:, c, t * 128:(t + 1) * 128],
                                            p7buf[:, par, c, :], start=(c == 0), stop=(c == FC - 1))
                        return ins
                    return emit
                p.step("tensor", mke(), waits=[wpt, pt_xTf] + bwaits)
                mmpt = p.pt("tensor")
                ept = p.step("vector", (lambda t=t, q=q, slot=slot: lambda v: v.tensor_scalar_mul(
                    ybuf[:, t, q * 512:(q + 1) * 512], slot[:, 0:512], scale_c[:, t:t + 1]))(),
                    waits=[mmpt, sc_pts[t], ln1_pts[t][0], ln1_pts[t][2]])
                mm.release(bi, ept)
                y2_pts[t][q] = ept
            p7_last[par] = p.pt("tensor")

    pt_gam = p.step("sync", lambda sy: sy.dma_start(
        out=gam_rep[:, :], in_=lng[None, :].broadcast_to([128, 2048])),
        waits=[pt_xTf_T], dma=True)
    pt_bet = p.step("sync", lambda sy: sy.dma_start(
        out=bet_rep[:, :], in_=lnb[None, :].broadcast_to([128, 2048])),
        waits=[pt_xTf_T], dma=True)

    hq_pts = [None] * 16
    hq_extra_war = []
    for iq in range(4):
        for ib in range(8):
            par = ib % 2
            wpt = p.step("gpsimd", (lambda iq=iq, ib=ib, par=par: lambda g: g.dma_start(
                out=p7buf[:, par, :, 0:256], in_=upw[iq * 8 + ib]))(), waits=[p7_last[par]], dma=True)
            for icin in range(2):
                ic = ib * 2 + icin
                for piece in range(2):
                    cs = slice(piece * 512, (piece + 1) * 512)
                    bi, slot, bwaits = mm.take()
                    def mku(par=par, icin=icin, cs=cs, slot=slot):
                        def emit(te):
                            ins = None
                            for c in range(FC):
                                ins = te.matmul(slot[:, 0:512],
                                                p7buf[:, par, c, icin * 128:(icin + 1) * 128],
                                                xTf[:, c, cs], start=(c == 0), stop=(c == FC - 1))
                            return ins
                        return emit
                    hq_war = ([hq_pts[ic]] if hq_pts[ic] is not None else []) + hq_extra_war
                    p.step("tensor", mku(), waits=[wpt, pt_xTf] + bwaits)
                    mmpt = p.pt("tensor")
                    rpt = p.step("scalar", (lambda ic=ic, cs=cs, slot=slot: lambda sc: sc.activation(
                        hq[:, ic, cs], slot[:, 0:512], AF.Relu, bias=fp(0.0), scale=fp(1.0)))(),
                        waits=[mmpt] + hq_war)
                    qpt = p.step("vector", (lambda ic=ic, cs=cs: lambda v: v.tensor_tensor(
                        hq[:, ic, cs], hq[:, ic, cs], hq[:, ic, cs], ALU.mult))(), waits=[rpt])
                    mm.release(bi, qpt)
            p7_last[par] = p.pt("tensor")
        pt_up_done = p.pt("vector")
        if iq == 0:
            emit_expert()
        if iq == 0:
            dh_ = dbg("hq0", [128, 16, 1024], BF16)
            if dh_ is not None:
                dpt = p.step("sync", (lambda dh_=dh_: lambda sy: sy.dma_start(
                    out=dh_.ap(), in_=hq[:, :, :]))(), waits=[pt_up_done], dma=True)
                hq_extra_war.append(dpt)
        for ob in range(4):
            par = ob % 2
            wpt = p.step("gpsimd", (lambda iq=iq, ob=ob, par=par: lambda g: g.dma_start(
                out=p7buf[:, par], in_=dnw[iq, ob]))(), waits=[p7_last[par]], dma=True)
            for t in range(NT):
                bi, slot, bwaits = mm.take()
                def mkd(t=t, par=par, slot=slot):
                    def emit(te):
                        ins = None
                        for ic in range(16):
                            ins = te.matmul(slot[:, 0:512], hq[:, ic, t * 128:(t + 1) * 128],
                                            p7buf[:, par, ic, :], start=(ic == 0), stop=(ic == 15))
                        return ins
                    return emit
                p.step("tensor", mkd(), waits=[wpt, pt_up_done] + bwaits)
                mmpt = p.pt("tensor")
                if iq < 3:
                    ept = p.step("vector", (lambda t=t, ob=ob, slot=slot: lambda v: v.tensor_tensor(
                        ybuf[:, t, ob * 512:(ob + 1) * 512], ybuf[:, t, ob * 512:(ob + 1) * 512],
                        slot[:, 0:512], ALU.add))(), waits=[mmpt, y2_pts[t][ob]])
                else:
                    ept = p.step("vector", (lambda t=t, ob=ob, slot=slot: lambda v: v.scalar_tensor_tensor(
                        ybuf[:, t, ob * 512:(ob + 1) * 512], slot[:, 0:512], 0.0,
                        ybuf[:, t, ob * 512:(ob + 1) * 512], ALU.add, ALU.add,
                        accum_out=stats[:, t * 4 + ob:t * 4 + ob + 1]))(),
                        waits=[mmpt, y2_pts[t][ob]])
                mm.release(bi, ept)
                y2_pts[t][ob] = ept
            p7_last[par] = p.pt("tensor")
        hq_pts = [p.pt("tensor")] * 16

    dscl = dbg("scl", [128, 8])
    if dscl is not None:
        p.step("sync", (lambda dscl=dscl: lambda sy: sy.dma_start(
            out=dscl.ap(), in_=scale_c[:, 0:8]))(), waits=sc_pts, dma=True)

    out_pts = []
    for t in range(NT):
        par = t % 2
        evs = y2_pts[t]
        dy2 = dbg("y2d", [TOK, D])
        if dy2 is not None:
            dpt = p.step("sync", (lambda t=t, dy2=dy2: lambda sy: sy.dma_start(
                out=dy2.ap()[t * 128:(t + 1) * 128, :], in_=ybuf[:, t]))(),
                waits=evs, dma=True)
            evs = evs + [dpt]
        pt_a = p.step("vector", (lambda t=t: lambda v: v.reduce_sum(
            stats2[:, t:t + 1], stats[:, t * 4:t * 4 + 4], axis=AX.X))(), waits=evs)
        pt_b = p.step("vector", (lambda t=t: lambda v: v.tensor_scalar_mul(
            stats2[:, t:t + 1], stats2[:, t:t + 1], INVD))(), waits=[pt_a])
        sq_pts = []
        for ob in range(4):
            bi, slot, bwaits = aux.take()
            spt = p.step("scalar", (lambda t=t, ob=ob, slot=slot: lambda sc: sc.activation(
                slot[:, 0:512], ybuf[:, t, ob * 512:(ob + 1) * 512], AF.Square,
                bias=fp(0.0), scale=fp(1.0),
                accum_out=stats[:, 32 + t * 4 + ob:32 + t * 4 + ob + 1]))(),
                waits=[evs[ob]] + bwaits)
            aux.release(bi, spt)
            sq_pts.append(spt)
        pt_cv = p.step("vector", (lambda t=t: lambda v: v.reduce_sum(
            stats2[:, 16 + t:17 + t], stats[:, 32 + t * 4:32 + t * 4 + 4], axis=AX.X))(), waits=sq_pts)
        pt_c2 = p.step("vector", (lambda t=t: lambda v: v.tensor_scalar_mul(
            stats2[:, 16 + t:17 + t], stats2[:, 16 + t:17 + t], INVD))(), waits=[pt_cv])
        pt_c3 = p.step("vector", (lambda t=t: lambda v: v.tensor_tensor(
            stats2[:, 32 + t:33 + t], stats2[:, t:t + 1], stats2[:, t:t + 1], ALU.mult))(), waits=[pt_b])
        pt_c4 = p.step("vector", (lambda t=t: lambda v: v.tensor_tensor(
            stats2[:, 16 + t:17 + t], stats2[:, 16 + t:17 + t], stats2[:, 32 + t:33 + t],
            ALU.subtract))(), waits=[pt_c2, pt_c3])
        pt_c5 = p.step("scalar", (lambda t=t: lambda sc: sc.activation(
            stats2[:, 32 + t:33 + t], stats2[:, 16 + t:17 + t], AF.Ln, bias=fp(EPS),
            scale=fp(1.0)))(), waits=[pt_c4])
        pt_c6 = p.step("scalar", (lambda t=t: lambda sc: sc.activation(
            stats2[:, 16 + t:17 + t], stats2[:, 32 + t:33 + t], AF.Exp, bias=fp(0.0),
            scale=fp(-0.5)))(), waits=[pt_c5])
        xw = [out_pts[t - 2]] if t >= 2 else []
        pt_xl = p.step("sync", (lambda t=t, par=par: lambda sy: sy.dma_start(
            out=xlrb[:, par], in_=xspill[t * 128:(t + 1) * 128, :]))(),
            waits=[ln1_pts[t][0]] + xw, dma=True)
        z_pts = []
        for ob in range(4):
            cs = slice(ob * 512, (ob + 1) * 512)
            p1_ = p.step("vector", (lambda t=t, cs=cs: lambda v: v.tensor_scalar(
                ybuf[:, t, cs], ybuf[:, t, cs], stats2[:, t:t + 1], stats2[:, 16 + t:17 + t],
                ALU.subtract, ALU.mult))(), waits=[pt_c6] + sq_pts)
            p2_ = p.step("gpsimd", (lambda t=t, cs=cs: lambda g: g.tensor_tensor(
                ybuf[:, t, cs], ybuf[:, t, cs], gam_rep[:, cs], ALU.mult))(),
                waits=[p1_, pt_gam])
            p3_ = p.step("gpsimd", (lambda t=t, cs=cs: lambda g: g.tensor_tensor(
                ybuf[:, t, cs], ybuf[:, t, cs], bet_rep[:, cs], ALU.add))(),
                waits=[p2_, pt_bet])
            p4_ = p.step("vector", (lambda t=t, cs=cs, ob=ob, par=par: lambda v: v.scalar_tensor_tensor(
                ybuf[:, t, cs], ybuf[:, t, cs], 0.0, xlrb[:, par, cs], ALU.add, ALU.add,
                accum_out=stats[:, t * 4 + ob:t * 4 + ob + 1]))(), waits=[p3_, pt_xl])
            z_pts.append(p4_)
        pt_d1 = p.step("vector", (lambda t=t: lambda v: v.reduce_sum(
            stats2[:, t:t + 1], stats[:, t * 4:t * 4 + 4], axis=AX.X))(), waits=z_pts)
        pt_d2 = p.step("vector", (lambda t=t: lambda v: v.tensor_scalar_mul(
            stats2[:, t:t + 1], stats2[:, t:t + 1], INVD))(), waits=[pt_d1])
        sq2_pts = []
        for ob in range(4):
            bi, slot, bwaits = aux.take()
            spt = p.step("scalar", (lambda t=t, ob=ob, slot=slot: lambda sc: sc.activation(
                slot[:, 0:512], ybuf[:, t, ob * 512:(ob + 1) * 512], AF.Square,
                bias=fp(0.0), scale=fp(1.0),
                accum_out=stats[:, 32 + t * 4 + ob:32 + t * 4 + ob + 1]))(),
                waits=[z_pts[ob]] + bwaits)
            aux.release(bi, spt)
            sq2_pts.append(spt)
        pt_d3 = p.step("vector", (lambda t=t: lambda v: v.reduce_sum(
            stats2[:, 16 + t:17 + t], stats[:, 32 + t * 4:32 + t * 4 + 4], axis=AX.X))(), waits=sq2_pts)
        pt_d4 = p.step("vector", (lambda t=t: lambda v: v.tensor_scalar_mul(
            stats2[:, 16 + t:17 + t], stats2[:, 16 + t:17 + t], INVD))(), waits=[pt_d3])
        pt_d5 = p.step("vector", (lambda t=t: lambda v: v.tensor_tensor(
            stats2[:, 32 + t:33 + t], stats2[:, t:t + 1], stats2[:, t:t + 1], ALU.mult))(), waits=[pt_d2])
        pt_d6 = p.step("vector", (lambda t=t: lambda v: v.tensor_tensor(
            stats2[:, 16 + t:17 + t], stats2[:, 16 + t:17 + t], stats2[:, 32 + t:33 + t],
            ALU.subtract))(), waits=[pt_d4, pt_d5])
        pt_d7 = p.step("scalar", (lambda t=t: lambda sc: sc.activation(
            stats2[:, 32 + t:33 + t], stats2[:, 16 + t:17 + t], AF.Ln, bias=fp(EPS),
            scale=fp(1.0)))(), waits=[pt_d6])
        pt_d8 = p.step("scalar", (lambda t=t: lambda sc: sc.activation(
            stats2[:, 16 + t:17 + t], stats2[:, 32 + t:33 + t], AF.Exp, bias=fp(0.0),
            scale=fp(-0.5)))(), waits=[pt_d7])
        fin_pts = []
        for ob in range(4):
            cs = slice(ob * 512, (ob + 1) * 512)
            fpt = p.step("vector", (lambda t=t, cs=cs: lambda v: v.tensor_scalar(
                ybuf[:, t, cs], ybuf[:, t, cs], stats2[:, t:t + 1], stats2[:, 16 + t:17 + t],
                ALU.subtract, ALU.mult))(), waits=[pt_d8] + sq2_pts)
            fin_pts.append(fpt)
        opt = p.step("sync", (lambda t=t: lambda sy: sy.dma_start(
            out=out[t * 128:(t + 1) * 128, :], in_=ybuf[:, t]))(), waits=fin_pts, dma=True)
        out_pts.append(opt)

    p.step("sync", lambda sy: sy.nop(), waits=out_pts, inc=(p.sem["sync"], 1))

    p.emit()
    return nc, DBG


def _swz_quarters(w):
    dmat, n = w.shape
    return np.ascontiguousarray(
        w.reshape(dmat // 128, 128, n // 512, 512).transpose(2, 1, 0, 3))


def _prep_weights(i):
    wts = {}
    wts["wq"] = _swz_quarters(np.asarray(i["L0_attn_wq_w"], np.float32))
    wts["wk"] = _swz_quarters(np.asarray(i["L0_attn_wk_w"], np.float32))
    wts["wv"] = _swz_quarters(np.asarray(i["L0_attn_wv_w"], np.float32))
    wts["wo"] = _swz_quarters(np.asarray(i["L0_attn_wo_w"], np.float32))
    for nm, key in (("wkl", "L0_attn_wkl_w"), ("wvl", "L0_attn_wvl_w")):
        w = np.asarray(i[key], np.float32)
        wts[nm] = np.ascontiguousarray(w.reshape(16, 128, 512).transpose(1, 0, 2))
    for nm, key in (("upk", "L0_attn_upk_w"), ("upv", "L0_attn_upv_w")):
        w = np.asarray(i[key], np.float32)
        wts[nm] = np.ascontiguousarray(w.reshape(16, 4, 128, 128).transpose(2, 0, 1, 3))
    g = np.asarray(i["L0_ffn_gate_w"], np.float32)
    wts["gate"] = np.ascontiguousarray(g.reshape(16, 128, 8).transpose(1, 0, 2))
    up = np.asarray(i["L0_ffn_up_w"], np.float32)
    wts["upw"] = np.ascontiguousarray(up.reshape(16, 128, 32, 256).transpose(2, 1, 0, 3))
    dn = np.asarray(i["L0_ffn_down_w"], np.float32)
    wts["dnw"] = np.ascontiguousarray(dn.reshape(4, 16, 128, 4, 512).transpose(0, 3, 2, 1, 4))
    wts["exw"] = _swz_quarters(np.asarray(i["L0_ffn_expert_weight"], np.float32))
    for nm, key in (("wqb", "L0_attn_wq_b"), ("wkb", "L0_attn_wk_b"), ("wvb", "L0_attn_wv_b")):
        wts[nm] = np.ascontiguousarray(np.asarray(i[key], np.float32).reshape(16, 128).T)
    for nm, key in (("wklb", "L0_attn_wkl_b"), ("wvlb", "L0_attn_wvl_b")):
        wts[nm] = np.ascontiguousarray(np.asarray(i[key], np.float32).reshape(4, 128).T)
    wts["upkb"] = np.ascontiguousarray(np.asarray(i["L0_attn_upk_b"], np.float32).T)
    wts["upvb"] = np.ascontiguousarray(np.asarray(i["L0_attn_upv_b"], np.float32).reshape(-1))
    wts["wob"] = np.ascontiguousarray(np.asarray(i["L0_attn_wo_b"], np.float32))
    wts["lng"] = np.ascontiguousarray(np.asarray(i["L0_ffn_ln_gamma"], np.float32))
    wts["lnb"] = np.ascontiguousarray(np.asarray(i["L0_ffn_ln_beta"], np.float32))
    return wts


def kernel(**inputs):
    x = np.asarray(inputs["L0_attn_x"], np.float32)
    wts = _prep_weights(inputs)
    nc, _ = build()
    in_maps = []
    for core in range(8):
        b, a = core // 2, core % 2
        xs = np.ascontiguousarray(x[b].reshape(16, 128, D)[a::2].reshape(TOK, D))
        m = {"xin": xs, "parity": np.array([a], np.float32)}
        m.update(wts)
        in_maps.append(m)
    res = run_bass_kernel_spmd(nc, in_maps, core_ids=list(range(8)))
    outp = np.empty((4, 2048, D), np.float32)
    for core in range(8):
        b, a = core // 2, core % 2
        outp[b].reshape(16, 128, D)[a::2] = res.results[core]["out"].reshape(NT, 128, D)
    return outp


# version 14
# speedup vs baseline: 1.0886x; 1.0029x over previous
import numpy as np
import concourse.bass as bass
import concourse.mybir as mybir
from concourse.bass_utils import run_bass_kernel_spmd

F32 = mybir.dt.float32
BF16 = mybir.dt.bfloat16
AF = mybir.ActivationFunctionType
ALU = mybir.AluOpType
AX = mybir.AxisListType
fp = np.float32

D = 2048
FC = 16
TOK = 1024
NT = 8
H = 16
DH = 128
DL = 512
NDL = 4
S = 2048
KC = 16
INTER = 8192
SCALE = float(1.0 / np.sqrt(np.float32(128.0)))
EPS = 1e-5
INVD = float(np.float32(1.0) / np.float32(2048.0))


def jmin(c):
    return c // 2


class Prog:
    ENGS = ("tensor", "vector", "scalar", "gpsimd", "sync")

    def __init__(self, nc):
        self.nc = nc
        self.steps = []
        self.sem = {}
        for e in self.ENGS:
            self.sem[e] = _S(nc.alloc_semaphore("sm_" + e))
        self.NQ = 4
        self.qsems = {"sync": [_S(nc.alloc_semaphore(f"sm_dsync{i}")) for i in range(self.NQ)],
                      "gpsimd": [_S(nc.alloc_semaphore(f"sm_dgps{i}")) for i in range(self.NQ)]}
        self.qidx = {"sync": 0, "gpsimd": 0}

    def extra_sem(self, name):
        return _S(self.nc.alloc_semaphore(name))

    def step(self, eng, emit, waits=(), dma=False, inc=None):
        extra = []
        if inc is not None:
            s, amt = inc
        elif dma:
            i = self.qidx[eng] % self.NQ
            self.qidx[eng] += 1
            s = self.qsems[eng][i]
            amt = 16
            if s.n > 0:
                extra = [(s, s.n)]
        else:
            s = self.sem[eng]
            amt = 1
        s.n += amt
        pt = (s, s.n)
        merged = {}
        for (ws, wv) in list(waits) + extra:
            if wv is None or wv <= 0:
                continue
            if id(ws) in merged:
                merged[id(ws)] = (ws, max(merged[id(ws)][1], wv))
            else:
                merged[id(ws)] = (ws, wv)
        self.steps.append((eng, emit, list(merged.values()), (s, amt)))
        return pt

    def pt(self, eng):
        key = eng
        return (self.sem[key], self.sem[key].n)

    def emit(self):
        nc = self.nc
        with nc.Block() as block:
            @block.tensor
            def _(e):
                self._run("tensor", e)

            @block.vector
            def _(e):
                self._run("vector", e)

            @block.scalar
            def _(e):
                self._run("scalar", e)

            @block.gpsimd
            def _(e):
                self._run("gpsimd", e)

            @block.sync
            def _(e):
                self._run("sync", e)

    def _run(self, name, eng):
        for (e, emit, waits, inc) in self.steps:
            if e != name:
                continue
            for (s, v) in waits:
                eng.wait_ge(s.h, v)
            ins = emit(eng)
            assert ins is not None, f"step on {name} returned no instruction"
            ins.then_inc(inc[0].h, inc[1])


class _S:
    __slots__ = ("h", "n")

    def __init__(self, handle):
        self.h = handle
        self.n = 0


class PsumPool:

    def __init__(self, prog, slots):
        self.p = prog
        self.slots = slots
        self.rel = [None] * len(slots)
        self.i = 0

    def take(self):
        i = self.i % len(self.slots)
        self.i += 1
        waits = [self.rel[i]] if self.rel[i] is not None else []
        return i, self.slots[i], waits

    def release(self, i, pt):
        self.rel[i] = pt


def build(debug=()):
    nc = bass.Bass("TRN2", target_bir_lowering=False)
    p = Prog(nc)

    xin = nc.dram_tensor("xin", [TOK, D], F32, kind="ExternalInput")
    wq = nc.dram_tensor("wq", [4, 128, FC, 512], F32, kind="ExternalInput")
    wk = nc.dram_tensor("wk", [4, 128, FC, 512], F32, kind="ExternalInput")
    wv = nc.dram_tensor("wv", [4, 128, FC, 512], F32, kind="ExternalInput")
    wo = nc.dram_tensor("wo", [4, 128, FC, 512], F32, kind="ExternalInput")
    wkl = nc.dram_tensor("wkl", [128, FC, DL], F32, kind="ExternalInput")
    wvl = nc.dram_tensor("wvl", [128, FC, DL], F32, kind="ExternalInput")
    upk = nc.dram_tensor("upk", [128, H, NDL, 128], F32, kind="ExternalInput")
    upv = nc.dram_tensor("upv", [128, H, NDL, 128], F32, kind="ExternalInput")
    gate = nc.dram_tensor("gate", [128, FC, 8], F32, kind="ExternalInput")
    upw = nc.dram_tensor("upw", [32, 128, FC, 256], F32, kind="ExternalInput")
    dnw = nc.dram_tensor("dnw", [4, 4, 128, FC, 512], F32, kind="ExternalInput")
    exw = nc.dram_tensor("exw", [4, 128, FC, 512], F32, kind="ExternalInput")
    wqb = nc.dram_tensor("wqb", [128, FC], F32, kind="ExternalInput")
    wkb = nc.dram_tensor("wkb", [128, FC], F32, kind="ExternalInput")
    wvb = nc.dram_tensor("wvb", [128, FC], F32, kind="ExternalInput")
    wklb = nc.dram_tensor("wklb", [128, NDL], F32, kind="ExternalInput")
    wvlb = nc.dram_tensor("wvlb", [128, NDL], F32, kind="ExternalInput")
    upkb = nc.dram_tensor("upkb", [128, H], F32, kind="ExternalInput")
    upvb = nc.dram_tensor("upvb", [D], F32, kind="ExternalInput")
    wob = nc.dram_tensor("wob", [D], F32, kind="ExternalInput")
    lng = nc.dram_tensor("lng", [D], F32, kind="ExternalInput")
    lnb = nc.dram_tensor("lnb", [D], F32, kind="ExternalInput")
    parity = nc.dram_tensor("parity", [1], F32, kind="ExternalInput")
    out = nc.dram_tensor("out", [TOK, D], F32, kind="ExternalOutput")

    kvk = nc.dram_tensor("kvk", [D + 1, TOK], BF16)
    kall = nc.dram_tensor("kall", [2 * (D + 1), TOK], BF16)
    kvv = nc.dram_tensor("kvv", [TOK, D], BF16)
    vall = nc.dram_tensor("vall", [S, D], BF16)
    xspill = nc.dram_tensor("xspill", [TOK, D], F32)

    DBG = {}

    def dbg(name, shape, dtype=F32):
        if name in debug:
            if name not in DBG:
                DBG[name] = nc.dram_tensor("dbg_" + name, shape, dtype, kind="ExternalOutput")
            return DBG[name]
        return None

    ARENA = 210112
    arena = nc.alloc_sbuf_tensor("arena", [128, ARENA], mybir.dt.int8).ap()

    def carve(off, dtype, shape):
        n = int(np.prod(shape)) * mybir.dt.size(dtype)
        assert off % 64 == 0 and off + n <= ARENA, (off, n)
        v = arena[:, off:off + n].bitcast(dtype)
        if len(shape) > 1:
            kw = {f"a{i}": shape[i] for i in range(len(shape))}
            v = v.rearrange("p (" + " ".join(kw) + ") -> p " + " ".join(kw), **kw)
        return v

    ident = carve(0, BF16, [128])
    tri = carve(256, BF16, [128])
    mlo = carve(512, BF16, [128])
    mhi = carve(768, BF16, [128])
    ones_bf = carve(1024, BF16, [1024])
    ident32 = carve(3072, F32, [128])
    gate_sb = carve(3584, F32, [FC, 8])
    wqb_c = carve(4096, F32, [16])
    wkb_c = carve(4160, F32, [16])
    wvb_c = carve(4224, F32, [16])
    wklb_c = carve(4288, F32, [16])
    wvlb_c = carve(4352, F32, [16])
    upkb_c = carve(4416, F32, [16])
    a2 = carve(4480, F32, [64])
    upvb_rep = carve(4736, BF16, [2048])
    qn_row = carve(8832, F32, [16])
    knmax_row = carve(8896, BF16, [32])
    mscr = carve(8960, BF16, [32])
    mtmp = carve(9024, F32, [16])
    negm = carve(9088, F32, [16])
    avr = carve(9152, F32, [16])
    stats = carve(9216, F32, [64])
    stats2 = carve(9472, F32, [64])
    gmax = carve(9728, F32, [64])
    scale_c = carve(9984, F32, [16])
    par_sb = carve(10048, F32, [16])
    zrow = carve(10112, BF16, [1024])

    R1 = 13504
    xld = carve(R1, F32, [2, 2048])
    xldb = carve(R1 + 16384, BF16, [2, 2048])
    kstg = carve(R1, BF16, [2, 1024])
    vstg = carve(R1 + 4096, BF16, [2, 8, 128])
    ksqb = carve(R1 + 12288, BF16, [1024])
    t1T = carve(R1 + 24576, BF16, [FC, 1024])
    latT = carve(R1 + 57344, BF16, [NDL, 1024])
    eSB = carve(R1, BF16, [2, KC, 1024])
    ybuf = carve(R1, F32, [NT, 2048])

    R2 = R1 + 65536
    xT = carve(R2, BF16, [FC, 1024])
    xTf = xT

    R3 = R2 + 32768
    qT = carve(R3, BF16, [H, 1024])
    wob_rep = carve(R3, F32, [2048])
    xrb = carve(R3 + 8192, F32, [2, 2048])
    xcastb = carve(R3 + 24576, BF16, [2, 2048])
    hq = carve(R3, BF16, [16, 1024])
    logits32 = carve(R3, F32, [1024])
    pbs32 = carve(R3 + 4096, F32, [1024])


    R4 = R3 + 32768
    wbuf = carve(R4, BF16, [2, FC, 512])
    kbuf = carve(R4, BF16, [2, 2048])
    vbuf = carve(R4 + 8192, BF16, [2, KC, 132])
    ostg = carve(R4 + 25600, BF16, [NT, 128])
    p7buf = carve(R4, BF16, [2, FC, 512])

    R5 = R4 + 32768
    upk_sb = carve(R5, BF16, [H, NDL, 128])
    xl32 = carve(R5, F32, [2, 512])
    upv_sb = carve(R5 + 16384, BF16, [H, NDL, 128])
    aoT = carve(R5, BF16, [H, 1024])
    gam_rep = carve(R5, F32, [2048])
    bet_rep = carve(R5 + 8192, F32, [2048])
    xlrb = carve(R5 + 16384, F32, [2, 2048])

    pb = [nc.alloc_psum_tensor(f"pb{i}", [128, 512], F32).ap() for i in range(6)]
    pt_ = [nc.alloc_psum_tensor(f"pt{i}", [128, 1024], BF16).ap() for i in range(2)]
    mm = PsumPool(p, pb[:5])
    aux = PsumPool(p, pb[5:6])
    tp = PsumPool(p, pt_)

    TRI = tri.bitcast(BF16)

    a2v = a2.rearrange("p (i j) -> p i j", i=8)[0:8]

    def prep_ms(g):
        g.memset(ident[:, 0:128], 0.0)
        g.memset(tri[:, 0:128], 0.0)
        g.memset(ones_bf[:, :], 1.0)
        g.memset(ident32[:, 0:128], 0.0)
        g.memset(knmax_row[0:1, :], 0.0)
        g.memset(zrow[0:1, :], 0.0)
        return g.memset(a2[0:8, :], 0.0)

    pt_ms = p.step("gpsimd", prep_ms)

    def prep_a1(g):
        g.affine_select(out=ident[:, 0:128], in_=ident[:, 0:128], compare_op=ALU.not_equal,
                        fill=1.0, base=0, pattern=[[-1, 128]], channel_multiplier=1)
        g.affine_select(out=tri[:, 0:128], in_=tri[:, 0:128], compare_op=ALU.is_gt,
                        fill=1.0, base=0, pattern=[[-1, 128]], channel_multiplier=1)
        g.affine_select(out=ident32[:, 0:128], in_=ident32[:, 0:128], compare_op=ALU.not_equal,
                        fill=1.0, base=0, pattern=[[-1, 128]], channel_multiplier=1)
        return g.affine_select(out=a2v, in_=a2v, compare_op=ALU.not_equal, fill=1.0,
                               base=0, pattern=[[-1, 8], [0, 8]], channel_multiplier=1)

    pt_a1 = p.step("gpsimd", prep_a1, waits=[pt_ms])
    pt_prep = p.step("gpsimd", lambda g: g.affine_select(
        out=a2v, in_=a2v, compare_op=ALU.not_equal, fill=1.0,
        base=0, pattern=[[0, 8], [-1, 8]], channel_multiplier=1), waits=[pt_a1])

    bias_pt = {}
    for nm, dst, srct in (("wqb", wqb_c, wqb), ("wkb", wkb_c, wkb), ("wvb", wvb_c, wvb),
                          ("upkb", upkb_c, upkb)):
        bias_pt[nm] = p.step("sync", (lambda dst=dst, srct=srct: lambda sy: sy.dma_start(
            out=dst[:, :], in_=srct[:, :]))(), dma=True)
    bias_pt["wklb"] = p.step("sync", lambda sy: sy.dma_start(out=wklb_c[:, 0:4], in_=wklb[:, :]), dma=True)
    bias_pt["wvlb"] = p.step("sync", lambda sy: sy.dma_start(out=wvlb_c[:, 0:4], in_=wvlb[:, :]), dma=True)
    pt_upvb = p.step("gpsimd", lambda g: g.dma_start(
        out=upvb_rep[:, :], in_=upvb[None, :].broadcast_to([128, 2048])), dma=True)
    pt_gate = p.step("sync", lambda sy: sy.dma_start(
        out=gate_sb.rearrange("p a b -> p (a b)"), in_=gate.rearrange("p a b -> p (a b)")), dma=True)

    pt_acol = p.step("sync", lambda sy: sy.dma_start(
        out=mtmp[:, 0:1], in_=parity[None, :].broadcast_to([128, 1])), dma=True)
    pt_msk0 = p.step("vector", lambda v: v.tensor_scalar_max(
        mlo[:, 0:128], tri[:, 0:128], mtmp[:, 0:1]), waits=[pt_acol, pt_prep])
    pt_masks = p.step("vector", lambda v: v.tensor_scalar_mul(
        mhi[:, 0:128], tri[:, 0:128], mtmp[:, 0:1]), waits=[pt_acol, pt_prep])

    cast_pts = []
    xt_evac_pts = []
    for t in range(NT):
        par = t % 2
        waits = [cast_pts[t - 2]] if t >= 2 else []
        pt_x = p.step("sync", (lambda t=t, par=par: lambda sy: sy.dma_start(
            out=xld[:, par], in_=xin[t * 128:(t + 1) * 128, :]))(), waits=waits, dma=True)
        pt_c = p.step("vector", (lambda par=par: lambda v: v.tensor_copy(
            xldb[:, par], xld[:, par]))(), waits=[pt_x])
        cast_pts.append(pt_c)
        for g8 in range(2):
            i, slot, twaits = tp.take()
            def mktr(par=par, g8=g8, slot=slot):
                def emit(te):
                    ins = None
                    for k in range(8):
                        c = g8 * 8 + k
                        ins = te.transpose(slot[:, k * 128:(k + 1) * 128],
                                           xldb[:, par, c * 128:(c + 1) * 128],
                                           ident[:, 0:128])
                    return ins
                return emit
            pt_t = p.step("tensor", mktr(), waits=[pt_c, pt_prep] + twaits)
            pt_e = p.step("vector", (lambda t=t, g8=g8, slot=slot: lambda v: v.tensor_copy(
                xT[:, g8 * 8:(g8 + 1) * 8, t * 128:(t + 1) * 128],
                slot.rearrange("p (a b) -> p a b", b=128)))(), waits=[pt_t])
            tp.release(i, pt_e)
            xt_evac_pts.append(pt_e)
    pt_xT = p.pt("vector")

    d = dbg("xT", [128, FC, TOK], BF16)
    if d is not None:
        p.step("sync", lambda sy, d=d: sy.dma_start(out=d.ap(), in_=xT[:, :, :]),
               waits=[pt_xT], dma=True)

    def proj_stream(wdram, bias_c, dst, nof, src_chunk, src_pt, extra_mm_waits=(),
                    first_war=(), bias_wait=None):
        nq = (nof + 3) // 4
        wpts = []
        qmm_pts = []
        evac_pts = [None] * nof
        for q in range(nq):
            par = q % 2
            waits = [qmm_pts[q - 2]] if q >= 2 else list(first_war)
            wpt = p.step("gpsimd", (lambda q=q, par=par: lambda g: g.dma_start(
                out=wbuf[:, par], in_=wdram[q]))(), waits=waits, dma=True)
            wpts.append(wpt)
            for oc in range(min(4, nof - q * 4)):
                of = q * 4 + oc
                for piece in range(2):
                    cs = slice(piece * 512, (piece + 1) * 512)
                    bi, slot, bwaits = mm.take()
                    def mk(par=par, oc=oc, cs=cs, slot=slot):
                        def emit(te):
                            ins = None
                            for c in range(FC):
                                ins = te.matmul(slot[:, 0:512],
                                                wbuf[:, par, c, oc * 128:(oc + 1) * 128],
                                                src_chunk(c)[:, cs],
                                                start=(c == 0), stop=(c == FC - 1))
                            return ins
                        return emit
                    p.step("tensor", mk(), waits=[wpt, src_pt] + bwaits + list(extra_mm_waits))
                    mmpt = p.pt("tensor")
                    def ev(of=of, cs=cs, slot=slot):
                        def emit(v):
                            if bias_c is not None:
                                return v.tensor_scalar_add(dst[:, of, cs], slot[:, 0:512],
                                                           bias_c[:, of:of + 1])
                            return v.tensor_copy(dst[:, of, cs], slot[:, 0:512])
                        return emit
                    ept = p.step("vector", ev(), waits=[mmpt] + ([bias_wait] if bias_wait else []))
                    mm.release(bi, ept)
                    evac_pts[of] = ept
            qmm_pts.append(p.pt("tensor"))
        return evac_pts, p.pt("tensor")

    t1_pts, pt_t1mm = proj_stream([wk[q] for q in range(4)], wkb_c, t1T, FC,
                                  lambda c: xT[:, c], pt_xT, bias_wait=bias_pt["wkb"])
    pt_t1 = p.pt("vector")

    pt_wkl = p.step("gpsimd", lambda g: g.dma_start(out=wbuf[:, 0], in_=wkl[:, :, :]),
                    waits=[pt_t1mm], dma=True)
    lat_pts = []
    for of in range(NDL):
        for piece in range(2):
            cs = slice(piece * 512, (piece + 1) * 512)
            bi, slot, bwaits = mm.take()
            def mk(of=of, cs=cs, slot=slot):
                def emit(te):
                    ins = None
                    for c in range(FC):
                        ins = te.matmul(slot[:, 0:512], wbuf[:, 0, c, of * 128:(of + 1) * 128],
                                        t1T[:, c, cs], start=(c == 0), stop=(c == FC - 1))
                    return ins
                return emit
            p.step("tensor", mk(), waits=[pt_wkl, pt_t1] + bwaits)
            mmpt = p.pt("tensor")
            ept = p.step("vector", (lambda of=of, cs=cs, slot=slot: lambda v: v.tensor_scalar_add(
                latT[:, of, cs], slot[:, 0:512], wklb_c[:, of:of + 1]))(),
                waits=[mmpt, bias_pt["wklb"]])
            mm.release(bi, ept)
            lat_pts.append(ept)
    pt_klat = p.pt("vector")
    pt_klatmm = p.pt("tensor")

    pt_upk = p.step("gpsimd", lambda g: g.dma_start(
        out=upk_sb.rearrange("p a b c -> p (a b c)"), in_=upk.rearrange("p a b c -> p (a b c)")), dma=True)
    pt_upv = p.step("gpsimd", lambda g: g.dma_start(
        out=upv_sb.rearrange("p a b c -> p (a b c)"), in_=upv.rearrange("p a b c -> p (a b c)")), dma=True)

    kvw_pts = []
    ksq_pts = []
    knmm_pts = []
    nrm_comb = []
    for h in range(H):
        par = h % 2
        kpts = []
        for piece in range(2):
            cs = slice(piece * 512, (piece + 1) * 512)
            bi, slot, bwaits = mm.take()
            def mk(h=h, cs=cs, slot=slot):
                def emit(te):
                    ins = None
                    for c in range(NDL):
                        ins = te.matmul(slot[:, 0:512], upk_sb[:, h, c, :], latT[:, c, cs],
                                        start=(c == 0), stop=(c == NDL - 1))
                    return ins
                return emit
            p.step("tensor", mk(), waits=[pt_upk, pt_klat] + bwaits)
            mmpt = p.pt("tensor")
            w = [mmpt, bias_pt["upkb"]]
            if h >= 2:
                w += [kvw_pts[h - 2], ksq_pts[h - 2]]
            ept = p.step("vector", (lambda par=par, h=h, cs=cs, slot=slot: lambda v: v.tensor_scalar_add(
                kstg[:, par, cs], slot[:, 0:512], upkb_c[:, h:h + 1]))(), waits=w)
            mm.release(bi, ept)
            kpts.append(ept)
        wpt = p.step("sync", (lambda h=h, par=par: lambda sy: sy.dma_start(
            out=kvk[h * 128:(h + 1) * 128, :], in_=kstg[:, par]))(), waits=kpts, dma=True)
        kvw_pts.append(wpt)
        sqw = kpts + ([knmm_pts[h - 1]] if h >= 1 else [])
        pt_sq = p.step("scalar", (lambda par=par: lambda sc: sc.activation(
            ksqb[:, :], kstg[:, par], AF.Square, bias=fp(0.0), scale=fp(1.0)))(), waits=sqw)
        ksq_pts.append(pt_sq)
        red_pts = []
        for piece in range(2):
            cs = slice(piece * 512, (piece + 1) * 512)
            bi, slot, bwaits = aux.take()
            pt_nm = p.step("tensor", (lambda cs=cs, slot=slot: lambda te: te.matmul(
                slot[0:1, 0:512], ones_bf[:, 0:1], ksqb[:, cs], start=True, stop=True))(),
                waits=[pt_sq, pt_prep] + bwaits)
            rpt = p.step("vector", (lambda piece=piece, slot=slot: lambda v: v.reduce_max(
                mtmp[0:1, 8 + piece:9 + piece], slot[0:1, 0:512], axis=AX.X))(),
                waits=[pt_nm] + ([nrm_comb[-1]] if nrm_comb else []))
            aux.release(bi, rpt)
            red_pts.append(rpt)
        knmm_pts.append(p.pt("tensor"))
        nrm_comb.append(p.step("vector", (lambda h=h: lambda v: v.tensor_tensor(
            knmax_row[0:1, h:h + 1], mtmp[0:1, 8:9], mtmp[0:1, 9:10], ALU.max))(), waits=red_pts))
    pt_knorm = p.pt("vector")
    pt_knT = p.pt("tensor")
    pt_zf = p.step("sync", lambda sy: sy.dma_start(
        out=kvk[D:D + 1, :], in_=zrow[0:1, :]), waits=[pt_ms], dma=True)
    kvw_pts.append(pt_zf)
    kvw_pts.append(p.step("sync", lambda sy: sy.dma_start(
        out=kvk[D:D + 1, 0:32], in_=knmax_row[0:1, 0:32]), waits=[pt_knorm, pt_zf], dma=True))

    s_agk = p.extra_sem("s_agk")
    pt_agk = p.step("gpsimd", lambda g: g.collective_compute(
        "AllGather", ALU.bypass, replica_groups=[[0, 1], [2, 3], [4, 5], [6, 7]],
        ins=[kvk.ap()], outs=[kall.ap()]), waits=kvw_pts, inc=(s_agk, 1))

    t2_pts, pt_t2mm = proj_stream([wv[q] for q in range(4)], wvb_c, t1T, FC,
                                  lambda c: xT[:, c], pt_xT, extra_mm_waits=[pt_klatmm],
                                  first_war=[pt_klatmm], bias_wait=bias_pt["wvb"])
    pt_t2 = p.pt("vector")

    pt_wvl = p.step("gpsimd", lambda g: g.dma_start(out=wbuf[:, 0], in_=wvl[:, :, :]),
                    waits=[pt_t2mm], dma=True)
    for of in range(NDL):
        for piece in range(2):
            cs = slice(piece * 512, (piece + 1) * 512)
            bi, slot, bwaits = mm.take()
            def mk(of=of, cs=cs, slot=slot):
                def emit(te):
                    ins = None
                    for c in range(FC):
                        ins = te.matmul(slot[:, 0:512], wbuf[:, 0, c, of * 128:(of + 1) * 128],
                                        t1T[:, c, cs], start=(c == 0), stop=(c == FC - 1))
                    return ins
                return emit
            p.step("tensor", mk(), waits=[pt_wvl, pt_t2] + bwaits)
            mmpt = p.pt("tensor")
            ept = p.step("vector", (lambda of=of, cs=cs, slot=slot: lambda v: v.tensor_scalar_add(
                latT[:, of, cs], slot[:, 0:512], wvlb_c[:, of:of + 1]))(),
                waits=[mmpt, bias_pt["wvlb"]])
            mm.release(bi, ept)
    pt_vlat = p.pt("vector")
    pt_vlmm = p.pt("tensor")

    vst_pts = []
    for h in range(H):
        par = h % 2
        ev_pts = []
        for t in range(NT):
            bi, slot, bwaits = mm.take()
            def mk(h=h, t=t, slot=slot):
                def emit(te):
                    ins = None
                    for c in range(NDL):
                        ins = te.matmul(slot[:, 0:128], latT[:, c, t * 128:(t + 1) * 128],
                                        upv_sb[:, h, c, :], start=(c == 0), stop=(c == NDL - 1))
                    return ins
                return emit
            p.step("tensor", mk(), waits=[pt_upv, pt_vlat] + bwaits)
            mmpt = p.pt("tensor")
            war = [vst_pts[h - 2]] if h >= 2 else []
            ept = p.step("vector", (lambda h=h, t=t, par=par, slot=slot: lambda v: v.tensor_tensor(
                vstg[:, par, t, :], slot[:, 0:128], upvb_rep[:, h * 128:(h + 1) * 128], ALU.add))(),
                waits=[mmpt, pt_upvb] + war)
            mm.release(bi, ept)
            ev_pts.append(ept)
        wpt = p.step("sync", (lambda h=h, par=par: lambda sy: sy.dma_start(
            out=kvv[:, h * 128:(h + 1) * 128].rearrange("(a p) d -> p a d", p=128),
            in_=vstg[:, par]))(), waits=ev_pts, dma=True)
        vst_pts.append(wpt)

    s_agv = p.extra_sem("s_agv")
    pt_agv = p.step("gpsimd", lambda g: g.collective_compute(
        "AllGather", ALU.bypass, replica_groups=[[0, 1], [2, 3], [4, 5], [6, 7]],
        ins=[kvv.ap()], outs=[vall.ap()]), waits=vst_pts, inc=(s_agv, 1))

    d = dbg("kvk", [D + 1, TOK], BF16)
    if d is not None:
        p.step("gpsimd", lambda g, d=d: g.dma_start(out=d.ap(), in_=kvk.ap()),
               waits=kvw_pts, dma=True)
    d = dbg("kvv", [TOK, D], BF16)
    if d is not None:
        p.step("gpsimd", lambda g, d=d: g.dma_start(out=d.ap(), in_=kvv.ap()),
               waits=vst_pts, dma=True)

    q_pts, pt_qmm = proj_stream([wq[q] for q in range(4)], wqb_c, qT, FC,
                                lambda c: xT[:, c], pt_xT, first_war=[pt_vlmm],
                                bias_wait=bias_pt["wqb"])
    qnmm_last = pt_knT
    for h in range(H):
        pt_sq = p.step("scalar", (lambda h=h: lambda sc: sc.activation(
            ksqb[:, :], qT[:, h], AF.Square, bias=fp(0.0), scale=fp(1.0)))(),
            waits=[q_pts[h], qnmm_last])
        red_pts = []
        for piece in range(2):
            cs = slice(piece * 512, (piece + 1) * 512)
            bi, slot, bwaits = aux.take()
            pt_nm = p.step("tensor", (lambda cs=cs, slot=slot: lambda te: te.matmul(
                slot[0:1, 0:512], ones_bf[:, 0:1], ksqb[:, cs], start=True, stop=True))(),
                waits=[pt_sq] + bwaits)
            rpt = p.step("vector", (lambda piece=piece, slot=slot: lambda v: v.reduce_max(
                mtmp[0:1, 8 + piece:9 + piece], slot[0:1, 0:512], axis=AX.X))(),
                waits=[pt_nm] + ([nrm_comb[-1]] if nrm_comb else []))
            aux.release(bi, rpt)
            red_pts.append(rpt)
        qnmm_last = p.pt("tensor")
        nrm_comb.append(p.step("vector", (lambda h=h: lambda v: v.tensor_tensor(
            qn_row[0:1, h:h + 1], mtmp[0:1, 8:9], mtmp[0:1, 9:10], ALU.max))(), waits=red_pts))
    pt_qnorm = p.pt("vector")
    pt_vones = p.step("gpsimd", lambda g: g.memset(vbuf[:, :, :, 128:129], 1.0),
                      waits=[pt_qmm])

    d = dbg("qT", [128, H, TOK], BF16)
    if d is not None:
        p.step("sync", lambda sy, d=d: sy.dma_start(out=d.ap(), in_=qT[:, :, :]),
               waits=[p.pt("vector")], dma=True)

    pt_m1 = p.step("sync", lambda sy: sy.dma_start(out=mscr[0:1, 0:16], in_=kall[D:D + 1, 0:16]),
                   waits=[(s_agk, 1)], dma=True)
    pt_m2 = p.step("sync", lambda sy: sy.dma_start(out=mscr[0:1, 16:32], in_=kall[2 * D + 1:2 * D + 2, 0:16]),
                   waits=[(s_agk, 1)], dma=True)
    pt_m3 = p.step("vector", lambda v: v.reduce_max(mtmp[0:1, 0:1], mscr[0:1, 0:32], axis=AX.X),
                   waits=[pt_m1, pt_m2])
    pt_m4 = p.step("vector", lambda v: v.reduce_max(mtmp[0:1, 1:2], qn_row[0:1, 0:16], axis=AX.X),
                   waits=[pt_qnorm])
    pt_m5 = p.step("vector", lambda v: v.tensor_tensor(mtmp[0:1, 2:3], mtmp[0:1, 0:1],
                                                       mtmp[0:1, 1:2], ALU.mult),
                   waits=[pt_m3, pt_m4])
    pt_m6 = p.step("scalar", lambda sc: sc.activation(mtmp[0:1, 3:4], mtmp[0:1, 2:3], AF.Ln,
                                                      bias=fp(1e-38), scale=fp(1.0)), waits=[pt_m5])
    pt_m7 = p.step("scalar", lambda sc: sc.activation(mtmp[0:1, 4:5], mtmp[0:1, 3:4], AF.Exp,
                                                      bias=fp(0.0), scale=fp(0.5)), waits=[pt_m6])
    pt_m8 = p.step("vector", lambda v: v.tensor_scalar_mul(mtmp[0:1, 5:6], mtmp[0:1, 4:5],
                                                           -SCALE), waits=[pt_m7])
    mdram = nc.dram_tensor("mdram", [1, 1], F32)
    pt_m9 = p.step("sync", lambda sy: sy.dma_start(
        out=mdram[0:1, 0:1], in_=mtmp[0:1, 5:6]), waits=[pt_m8], dma=True)
    pt_mprep = p.step("sync", lambda sy: sy.dma_start(
        out=negm[:, 0:1], in_=mdram[0:1, 0:1].broadcast_to([128, 1])), waits=[pt_m9], dma=True)

    kb_pts = [None] * (H + 2)
    vb_pts = [None] * (H + 2)
    scores_done = [None] * H
    av_done = [None] * H
    avr_mu = [None] * NT
    ostg_war = [None] * NT
    exp_done = [None] * H
    ao_pts = []

    head_ctx = {}

    def emit_av(h):
        par, vpts, mask_pts, cexp = head_ctx[h]
        mu_pts = []
        for j in range(NT):
            bi, slot, bwaits = mm.take()
            def mkav(par=par, j=j, slot=slot):
                def emit(te):
                    ins = None
                    for c in range(2 * j + 2):
                        ins = te.matmul(slot[:, 0:129], eSB[:, par, c, j * 128:(j + 1) * 128],
                                        vbuf[:, par, c, 0:129],
                                        start=(c == 0), stop=(c == 2 * j + 1))
                    return ins
                return emit
            avw = [mask_pts[2 * j + 1]] + [pt for c in range(2 * j + 2) for pt in cexp[c]]
            p.step("tensor", mkav(), waits=avw + vpts + bwaits + [pt_vones])
            avpt = p.pt("tensor")
            pt_r = p.step("vector", (lambda j=j, slot=slot: lambda v: v.reciprocal(
                avr[:, j:j + 1], slot[:, 128:129]))(),
                waits=[avpt] + ([avr_mu[j]] if avr_mu[j] is not None else []))
            pt_mu = p.step("vector", (lambda j=j, slot=slot: lambda v: v.tensor_scalar_mul(
                ostg[:, j], slot[:, 0:128], avr[:, j:j + 1]))(),
                waits=[pt_r] + ([ostg_war[j]] if ostg_war[j] is not None else []))
            avr_mu[j] = pt_mu
            mm.release(bi, pt_mu)
            mu_pts.append(pt_mu)
        ti, tslot, twaits = tp.take()
        def mktr(h=h, tslot=tslot):
            def emit(te):
                ins = None
                for j in range(NT):
                    ins = te.transpose(tslot[:, j * 128:(j + 1) * 128], ostg[:, j],
                                       ident[:, 0:128])
                return ins
            return emit
        pt_tr = p.step("tensor", mktr(), waits=mu_pts + twaits)
        for j in range(NT):
            ostg_war[j] = pt_tr
        pt_ev = p.step("vector", (lambda h=h, tslot=tslot: lambda v: v.tensor_copy(
            aoT[:, h, :], tslot[:, 0:1024]))(), waits=[pt_tr])
        tp.release(ti, pt_ev)
        ao_pts.append((pt_tr, pt_ev))
        av_done[h] = p.pt("tensor")

    for h in range(H):
        par = h % 2
        kwar = [scores_done[h - 2]] if h >= 2 else [pt_qmm]
        vwar = [av_done[h - 2]] if h >= 2 else [pt_qmm]
        kv = kbuf[:, par].rearrange("p (c j) -> p c j", j=128)
        kpts = []
        vpts = []
        for r in range(2):
            kpts.append(p.step("sync", (lambda h=h, r=r, kv=kv: lambda sy: sy.dma_start(
                out=kv[:, r::2, :],
                in_=kall[r * (D + 1) + h * 128: r * (D + 1) + (h + 1) * 128, :]
                .rearrange("p (a b) -> p a b", b=128)))(),
                waits=[(s_agk, 1)] + kwar, dma=True))
            vpts.append(p.step("sync", (lambda h=h, r=r, par=par: lambda sy: sy.dma_start(
                out=vbuf[:, par, r::2, 0:128],
                in_=vall[r * TOK:(r + 1) * TOK, h * 128:(h + 1) * 128]
                .rearrange("(a p) d -> p a d", p=128)))(),
                waits=[(s_agv, 1)] + vwar, dma=True))
        kb_pts[h] = kpts
        vb_pts[h] = vpts

        esb_war = [av_done[h - 2]] if h >= 2 else []
        chunk_exp_pts = []
        for c in range(KC):
            off = jmin(c) * 128
            cpts = []
            while off < TOK:
                wd = min(512, TOK - off)
                bi, slot, bwaits = mm.take()
                p.step("tensor", (lambda par=par, c=c, off=off, wd=wd, slot=slot, h=h: lambda te: te.matmul(
                    slot[:, 0:wd], kbuf[:, par, c * 128:(c + 1) * 128], qT[:, h, off:off + wd],
                    start=True, stop=True))(), waits=kpts + [q_pts[h]] + bwaits)
                mmpt = p.pt("tensor")
                ept = p.step("scalar", (lambda par=par, c=c, off=off, wd=wd, slot=slot: lambda sc: sc.activation(
                    eSB[:, par, c, off:off + wd], slot[:, 0:wd], AF.Exp,
                    bias=negm[:, 0:1], scale=fp(SCALE)))(),
                    waits=[mmpt, pt_mprep] + esb_war)
                mm.release(bi, ept)
                cpts.append(ept)
                off += wd
            chunk_exp_pts.append(cpts)
        scores_done[h] = p.pt("tensor")
        exp_done[h] = p.pt("scalar")

        mask_pts = []
        for c in range(KC):
            m = mlo if c % 2 == 0 else mhi
            o = jmin(c) * 128
            mask_pts.append(p.step("vector", (lambda par=par, c=c, m=m, o=o: lambda v: v.tensor_tensor(
                eSB[:, par, c, o:o + 128], eSB[:, par, c, o:o + 128],
                m[:, 0:128], ALU.mult))(), waits=[chunk_exp_pts[c][0], pt_masks]))
        pt_mask = mask_pts[-1]

        if h == 0:
            dkb = dbg("kbuf0", [128, 2048], BF16)
            if dkb is not None:
                p.step("sync", (lambda dkb=dkb: lambda sy: sy.dma_start(
                    out=dkb.ap(), in_=kbuf[:, 0]))(), waits=kpts, dma=True)
            dka = dbg("kall", [2 * (D + 1), TOK], BF16)
            if dka is not None:
                p.step("gpsimd", (lambda dka=dka: lambda g: g.dma_start(
                    out=dka.ap(), in_=kall.ap()))(), waits=[(s_agk, 1)], dma=True)
            d0 = dbg("eS0", [128, KC, TOK], BF16)
            if d0 is not None:
                p.step("sync", (lambda d0=d0: lambda sy: sy.dma_start(
                    out=d0.ap(), in_=eSB[:, 0]))(), waits=[pt_mask], dma=True)
            dm = dbg("masks", [128, 3, 128], BF16)
            if dm is not None:
                p.step("sync", (lambda dm=dm: lambda sy: sy.dma_start(
                    out=dm.ap()[:, 0], in_=tri[:, 0:128]))(), waits=[pt_mask], dma=True)
                p.step("sync", (lambda dm=dm: lambda sy: sy.dma_start(
                    out=dm.ap()[:, 1], in_=mlo[:, 0:128]))(), waits=[pt_mask], dma=True)
                p.step("sync", (lambda dm=dm: lambda sy: sy.dma_start(
                    out=dm.ap()[:, 2], in_=mhi[:, 0:128]))(), waits=[pt_mask], dma=True)
            dn_ = dbg("negm", [1, 16])
            if dn_ is not None:
                p.step("sync", (lambda dn_=dn_: lambda sy: sy.dma_start(
                    out=dn_.ap(), in_=mtmp[0:1, 0:16]))(), waits=[pt_mprep], dma=True)
            dnc = dbg("negmcol", [128, 1])
            if dnc is not None:
                p.step("sync", (lambda dnc=dnc: lambda sy: sy.dma_start(
                    out=dnc.ap(), in_=negm[:, 0:1]))(), waits=[pt_mprep], dma=True)
        head_ctx[h] = (par, vpts, mask_pts, chunk_exp_pts)
        if h >= 1:
            emit_av(h - 1)
    emit_av(H - 1)
    pt_attn_v = p.pt("vector")

    d = dbg("aoT", [128, H, TOK], BF16)
    if d is not None:
        p.step("sync", lambda sy, d=d: sy.dma_start(out=d.ap(), in_=aoT[:, :, :]),
               waits=[pt_attn_v], dma=True)
    d = dbg("aoTb", [128, H, TOK], BF16)
    if d is not None:
        p.step("sync", lambda sy, d=d: sy.dma_start(out=d.ap(), in_=aoT[:, :, :]),
               waits=[pt_attn_v], dma=True)

    pt_wob = p.step("sync", lambda sy: sy.dma_start(
        out=wob_rep[:, :], in_=wob[None, :].broadcast_to([128, 2048])),
        waits=[scores_done[H - 1]], dma=True)

    da2 = dbg("aoT2", [128, H, TOK], BF16)
    if da2 is not None:
        p.step("sync", (lambda da2=da2: lambda sy: sy.dma_start(
            out=da2.ap(), in_=aoT[:, :, :]))(), waits=[pt_attn_v], dma=True)

    wo_ev = [[None] * 4 for _ in range(NT)]
    woq_mm = []
    for q in range(4):
        par = q % 2
        wwaits = [woq_mm[q - 2]] if q >= 2 else [av_done[H - 1]]
        wpt = p.step("gpsimd", (lambda q=q, par=par: lambda g: g.dma_start(
            out=wbuf[:, par], in_=wo[q]))(), waits=wwaits, dma=True)
        for t in range(NT):
            bi, slot, bwaits = mm.take()
            def mkwo(t=t, par=par, slot=slot):
                def emit(te):
                    ins = None
                    for hh in range(FC):
                        ins = te.matmul(slot[:, 0:512], aoT[:, hh, t * 128:(t + 1) * 128],
                                        wbuf[:, par, hh, :],
                                        start=(hh == 0), stop=(hh == FC - 1))
                    return ins
                return emit
            p.step("tensor", mkwo(), waits=[wpt, pt_attn_v] + bwaits)
            mmpt = p.pt("tensor")
            ept = p.step("vector", (lambda t=t, q=q, slot=slot: lambda v: v.tensor_tensor(
                ybuf[:, t, q * 512:(q + 1) * 512], slot[:, 0:512],
                wob_rep[:, q * 512:(q + 1) * 512], ALU.add))(), waits=[mmpt, pt_wob])
            mm.release(bi, ept)
            wo_ev[t][q] = ept
        woq_mm.append(p.pt("tensor"))

    lg_pts = []
    xl_mm = [None, None]

    def emit_gate_tile(t, gate_in_waits):
        gi, gslot, gwaits = aux.take()
        for cq in range(4):
            qpar = cq % 2
            bi, slot, bwaits = mm.take()
            tr_last = None
            for i in range(4):
                c = cq * 4 + i
                p.step("tensor", (lambda t=t, c=c, i=i, slot=slot: lambda te: te.transpose(
                    slot[:, i * 128:(i + 1) * 128].bitcast(F32),
                    ybuf[:, t, c * 128:(c + 1) * 128],
                    ident32[:, 0:128]))(), waits=gate_in_waits + [pt_prep] + (bwaits if i == 0 else []))
            tr_last = p.pt("tensor")
            war = [xl_mm[qpar]] if xl_mm[qpar] is not None else []
            pev = p.step("vector", (lambda qpar=qpar, slot=slot: lambda v: v.tensor_copy(
                xl32[:, qpar], slot[:, 0:512]))(), waits=[tr_last] + war)
            mm.release(bi, pev)
            def mkg(cq=cq, qpar=qpar, gslot=gslot):
                def emit(te):
                    ins = None
                    for i in range(4):
                        c = cq * 4 + i
                        ins = te.matmul(gslot[0:8, 0:128], gate_sb[:, c, :],
                                        xl32[:, qpar, i * 128:(i + 1) * 128],
                                        start=(c == 0), stop=(c == FC - 1))
                    return ins
                return emit
            p.step("tensor", mkg(), waits=[pev, pt_gate] + (gwaits if cq == 0 else []))
            xl_mm[qpar] = p.pt("tensor")
        mmpt = p.pt("tensor")
        ept = p.step("vector", (lambda t=t, gslot=gslot: lambda v: v.tensor_copy(
            logits32[0:8, t * 128:(t + 1) * 128], gslot[0:8, 0:128]))(), waits=[mmpt])
        aux.release(gi, ept)
        lg_pts.append(ept)


    ln1_pts = []
    xtf_tr_pts = []
    for t in range(NT):
        par = t % 2
        xwar = [ln1_pts[t - 2][3]] if t >= 2 else [scores_done[H - 1]]
        pt_xr = p.step("sync", (lambda t=t, par=par: lambda sy: sy.dma_start(
            out=xrb[:, par], in_=xin[t * 128:(t + 1) * 128, :]))(),
            waits=xwar, dma=True)
        pt_add = p.step("vector", (lambda t=t, par=par: lambda v: v.scalar_tensor_tensor(
            ybuf[:, t, :], ybuf[:, t, :], 0.0, xrb[:, par, :], ALU.add, ALU.add,
            accum_out=stats2[:, t:t + 1]))(), waits=wo_ev[t] + [pt_xr])
        ev_pts = [pt_add]
        dy1 = dbg("y1d", [TOK, D])
        if dy1 is not None:
            dpt = p.step("sync", (lambda t=t, dy1=dy1: lambda sy: sy.dma_start(
                out=dy1.ap()[t * 128:(t + 1) * 128, :], in_=ybuf[:, t]))(),
                waits=ev_pts, dma=True)
            ev_pts = ev_pts + [dpt]
        pt_s2 = p.step("vector", (lambda t=t: lambda v: v.tensor_scalar_mul(
            stats2[:, t:t + 1], stats2[:, t:t + 1], INVD))(), waits=ev_pts)
        sq_pts = []
        for ob in range(4):
            bi, slot, bwaits = aux.take()
            spt = p.step("scalar", (lambda t=t, ob=ob, slot=slot: lambda sc: sc.activation(
                slot[:, 0:512], ybuf[:, t, ob * 512:(ob + 1) * 512], AF.Square,
                bias=fp(0.0), scale=fp(1.0),
                accum_out=stats[:, 32 + t * 4 + ob:32 + t * 4 + ob + 1]))(),
                waits=ev_pts + bwaits)
            aux.release(bi, spt)
            sq_pts.append(spt)
        pt_s3 = p.step("vector", (lambda t=t: lambda v: v.reduce_sum(
            stats2[:, 16 + t:17 + t], stats[:, 32 + t * 4:32 + t * 4 + 4], axis=AX.X))(), waits=sq_pts)
        pt_s5 = p.step("vector", (lambda t=t: lambda v: v.tensor_scalar_mul(
            stats2[:, 16 + t:17 + t], stats2[:, 16 + t:17 + t], INVD))(), waits=[pt_s3])
        pt_s6 = p.step("vector", (lambda t=t: lambda v: v.tensor_tensor(
            stats2[:, 32 + t:33 + t], stats2[:, t:t + 1], stats2[:, t:t + 1], ALU.mult))(),
            waits=[pt_s2])
        pt_s7 = p.step("vector", (lambda t=t: lambda v: v.tensor_tensor(
            stats2[:, 16 + t:17 + t], stats2[:, 16 + t:17 + t], stats2[:, 32 + t:33 + t],
            ALU.subtract))(), waits=[pt_s5, pt_s6])
        pt_s8 = p.step("scalar", (lambda t=t: lambda sc: sc.activation(
            stats2[:, 32 + t:33 + t], stats2[:, 16 + t:17 + t], AF.Ln, bias=fp(EPS),
            scale=fp(1.0)))(), waits=[pt_s7])
        pt_s9 = p.step("scalar", (lambda t=t: lambda sc: sc.activation(
            stats2[:, 16 + t:17 + t], stats2[:, 32 + t:33 + t], AF.Exp, bias=fp(0.0),
            scale=fp(-0.5)))(), waits=[pt_s8])
        nrm_pts = []
        for ob in range(4):
            npt = p.step("vector", (lambda t=t, ob=ob: lambda v: v.tensor_scalar(
                ybuf[:, t, ob * 512:(ob + 1) * 512], ybuf[:, t, ob * 512:(ob + 1) * 512],
                stats2[:, t:t + 1], stats2[:, 16 + t:17 + t], ALU.subtract, ALU.mult))(),
                waits=[pt_s9] + sq_pts)
            nrm_pts.append(npt)
        emit_gate_tile(t, nrm_pts)
        pt_sp = p.step("sync", (lambda t=t: lambda sy: sy.dma_start(
            out=xspill[t * 128:(t + 1) * 128, :], in_=ybuf[:, t]))(), waits=nrm_pts, dma=True)
        cwar = [xtf_tr_pts[-3]] if len(xtf_tr_pts) >= 3 else [av_done[H - 1]]
        pt_cast = p.step("vector", (lambda t=t, par=par: lambda v: v.tensor_copy(
            xcastb[:, par], ybuf[:, t]))(), waits=nrm_pts + cwar)
        tprs = []
        for g8 in range(2):
            ti, tslot, twaits = tp.take()
            def mktr(par=par, g8=g8, tslot=tslot):
                def emit(te):
                    ins = None
                    for k in range(8):
                        c = g8 * 8 + k
                        ins = te.transpose(tslot[:, k * 128:(k + 1) * 128],
                                           xcastb[:, par, c * 128:(c + 1) * 128],
                                           ident[:, 0:128])
                    return ins
                return emit
            ptr = p.step("tensor", mktr(), waits=[pt_cast] + twaits)
            pev = p.step("vector", (lambda t=t, g8=g8, tslot=tslot: lambda v: v.tensor_copy(
                xTf[:, g8 * 8:(g8 + 1) * 8, t * 128:(t + 1) * 128],
                tslot.rearrange("p (a b) -> p a b", b=128)))(), waits=[ptr])
            tp.release(ti, pev)
            xtf_tr_pts.append(ptr)
            tprs.append(pev)
        ln1_pts.append((pt_sp, tprs[-1], pt_cast, p.pt("vector")))
        if t == 3:
            pt_xTf_h0 = p.pt("vector")
    pt_xTf = p.pt("vector")
    pt_xTf_T = p.pt("tensor")

    d = dbg("xln1", [TOK, D])
    if d is not None:
        p.step("gpsimd", lambda g, d=d: g.dma_start(out=d.ap(), in_=xspill.ap()),
               waits=[pt for (pt, _, _, _) in ln1_pts], dma=True)

    pb_pts = []
    for piece in range(2):
        cs = slice(piece * 512, (piece + 1) * 512)
        bi, slot, bwaits = aux.take()
        p.step("tensor", (lambda cs=cs, slot=slot: lambda te: te.matmul(
            slot[0:64, 0:512], a2[0:8, 0:64], logits32[0:8, cs], start=True, stop=True))(),
            waits=lg_pts + bwaits + [pt_prep])
        mmpt = p.pt("tensor")
        ept = p.step("vector", (lambda cs=cs, slot=slot: lambda v: v.tensor_copy(
            pbs32[0:64, cs], slot[0:64, 0:512]))(), waits=[mmpt])
        aux.release(bi, ept)
        pb_pts.append(ept)
    sc_pts = []
    for t in range(NT):
        bi, slot, bwaits = mm.take()
        ptr = p.step("tensor", (lambda t=t, slot=slot: lambda te: te.transpose(
            slot[:, 0:64].bitcast(F32), pbs32[0:64, t * 128:(t + 1) * 128],
            ident32[0:64, 0:64]))(), waits=pb_pts + bwaits)
        pcv = p.step("vector", (lambda slot=slot: lambda v: v.tensor_copy(
            gmax[:, 0:64], slot[:, 0:64]))(), waits=[ptr] + (sc_pts[-1:] if sc_pts else []))
        mm.release(bi, pcv)
        pms = p.step("vector", lambda v: v.memset(gmax[:, 0:64:9], -1e30), waits=[pcv])
        prd = p.step("vector", (lambda t=t: lambda v: v.reduce_max(
            scale_c[:, t:t + 1], gmax[:, 0:64], axis=AX.X))(), waits=[pms])
        sc_pts.append(prd)
    pt_gate_all = p.pt("vector")

    p7_last = [woq_mm[-1], woq_mm[-1]]
    y2_pts = [[None] * 4 for _ in range(NT)]

    def emit_expert():
        for q in range(4):
            par = q % 2
            wpt = p.step("gpsimd", (lambda q=q, par=par: lambda g: g.dma_start(
                out=p7buf[:, par], in_=exw[q]))(), waits=[p7_last[par]], dma=True)
            for t in range(NT):
                bi, slot, bwaits = mm.take()
                def mke(t=t, par=par, slot=slot):
                    def emit(te):
                        ins = None
                        for c in range(FC):
                            ins = te.matmul(slot[:, 0:512], xTf[:, c, t * 128:(t + 1) * 128],
                                            p7buf[:, par, c, :], start=(c == 0), stop=(c == FC - 1))
                        return ins
                    return emit
                p.step("tensor", mke(), waits=[wpt, ln1_pts[t][3]] + bwaits)
                mmpt = p.pt("tensor")
                ept = p.step("vector", (lambda t=t, q=q, slot=slot: lambda v: v.tensor_scalar_mul(
                    ybuf[:, t, q * 512:(q + 1) * 512], slot[:, 0:512], scale_c[:, t:t + 1]))(),
                    waits=[mmpt, sc_pts[t], ln1_pts[t][0], ln1_pts[t][2]])
                mm.release(bi, ept)
                y2_pts[t][q] = ept
            p7_last[par] = p.pt("tensor")

    pt_gam = p.step("sync", lambda sy: sy.dma_start(
        out=gam_rep[:, :], in_=lng[None, :].broadcast_to([128, 2048])),
        waits=[pt_xTf_T], dma=True)
    pt_bet = p.step("sync", lambda sy: sy.dma_start(
        out=bet_rep[:, :], in_=lnb[None, :].broadcast_to([128, 2048])),
        waits=[pt_xTf_T], dma=True)

    hq_pts = [None] * 16
    hq_extra_war = []
    for iq in range(4):
        for ib in range(8):
            par = ib % 2
            wpt = p.step("gpsimd", (lambda iq=iq, ib=ib, par=par: lambda g: g.dma_start(
                out=p7buf[:, par, :, 0:256], in_=upw[iq * 8 + ib]))(), waits=[p7_last[par]], dma=True)
            for icin in range(2):
                ic = ib * 2 + icin
                for piece in range(2):
                    cs = slice(piece * 512, (piece + 1) * 512)
                    bi, slot, bwaits = mm.take()
                    def mku(par=par, icin=icin, cs=cs, slot=slot):
                        def emit(te):
                            ins = None
                            for c in range(FC):
                                ins = te.matmul(slot[:, 0:512],
                                                p7buf[:, par, c, icin * 128:(icin + 1) * 128],
                                                xTf[:, c, cs], start=(c == 0), stop=(c == FC - 1))
                            return ins
                        return emit
                    hq_war = ([hq_pts[ic]] if hq_pts[ic] is not None else []) + hq_extra_war
                    p.step("tensor", mku(), waits=[wpt, pt_xTf, pt_xTf_T] + bwaits)
                    mmpt = p.pt("tensor")
                    rpt = p.step("scalar", (lambda ic=ic, cs=cs, slot=slot: lambda sc: sc.activation(
                        hq[:, ic, cs], slot[:, 0:512], AF.Relu, bias=fp(0.0), scale=fp(1.0)))(),
                        waits=[mmpt] + hq_war)
                    qpt = p.step("vector", (lambda ic=ic, cs=cs: lambda v: v.tensor_tensor(
                        hq[:, ic, cs], hq[:, ic, cs], hq[:, ic, cs], ALU.mult))(), waits=[rpt])
                    mm.release(bi, qpt)
            p7_last[par] = p.pt("tensor")
        pt_up_done = p.pt("vector")
        if iq == 0:
            emit_expert()
        if iq == 0:
            dh_ = dbg("hq0", [128, 16, 1024], BF16)
            if dh_ is not None:
                dpt = p.step("sync", (lambda dh_=dh_: lambda sy: sy.dma_start(
                    out=dh_.ap(), in_=hq[:, :, :]))(), waits=[pt_up_done], dma=True)
                hq_extra_war.append(dpt)
        for ob in range(4):
            par = ob % 2
            wpt = p.step("gpsimd", (lambda iq=iq, ob=ob, par=par: lambda g: g.dma_start(
                out=p7buf[:, par], in_=dnw[iq, ob]))(), waits=[p7_last[par]], dma=True)
            for t in range(NT):
                bi, slot, bwaits = mm.take()
                def mkd(t=t, par=par, slot=slot):
                    def emit(te):
                        ins = None
                        for ic in range(16):
                            ins = te.matmul(slot[:, 0:512], hq[:, ic, t * 128:(t + 1) * 128],
                                            p7buf[:, par, ic, :], start=(ic == 0), stop=(ic == 15))
                        return ins
                    return emit
                p.step("tensor", mkd(), waits=[wpt, pt_up_done] + bwaits)
                mmpt = p.pt("tensor")
                if iq < 3:
                    ept = p.step("vector", (lambda t=t, ob=ob, slot=slot: lambda v: v.tensor_tensor(
                        ybuf[:, t, ob * 512:(ob + 1) * 512], ybuf[:, t, ob * 512:(ob + 1) * 512],
                        slot[:, 0:512], ALU.add))(), waits=[mmpt, y2_pts[t][ob]])
                else:
                    ept = p.step("vector", (lambda t=t, ob=ob, slot=slot: lambda v: v.scalar_tensor_tensor(
                        ybuf[:, t, ob * 512:(ob + 1) * 512], slot[:, 0:512], 0.0,
                        ybuf[:, t, ob * 512:(ob + 1) * 512], ALU.add, ALU.add,
                        accum_out=stats[:, t * 4 + ob:t * 4 + ob + 1]))(),
                        waits=[mmpt, y2_pts[t][ob]])
                mm.release(bi, ept)
                y2_pts[t][ob] = ept
            p7_last[par] = p.pt("tensor")
        hq_pts = [p.pt("tensor")] * 16

    dscl = dbg("scl", [128, 8])
    if dscl is not None:
        p.step("sync", (lambda dscl=dscl: lambda sy: sy.dma_start(
            out=dscl.ap(), in_=scale_c[:, 0:8]))(), waits=sc_pts, dma=True)

    out_pts = []
    for t in range(NT):
        par = t % 2
        evs = y2_pts[t]
        dy2 = dbg("y2d", [TOK, D])
        if dy2 is not None:
            dpt = p.step("sync", (lambda t=t, dy2=dy2: lambda sy: sy.dma_start(
                out=dy2.ap()[t * 128:(t + 1) * 128, :], in_=ybuf[:, t]))(),
                waits=evs, dma=True)
            evs = evs + [dpt]
        pt_a = p.step("vector", (lambda t=t: lambda v: v.reduce_sum(
            stats2[:, t:t + 1], stats[:, t * 4:t * 4 + 4], axis=AX.X))(), waits=evs)
        pt_b = p.step("vector", (lambda t=t: lambda v: v.tensor_scalar_mul(
            stats2[:, t:t + 1], stats2[:, t:t + 1], INVD))(), waits=[pt_a])
        sq_pts = []
        for ob in range(4):
            bi, slot, bwaits = aux.take()
            spt = p.step("scalar", (lambda t=t, ob=ob, slot=slot: lambda sc: sc.activation(
                slot[:, 0:512], ybuf[:, t, ob * 512:(ob + 1) * 512], AF.Square,
                bias=fp(0.0), scale=fp(1.0),
                accum_out=stats[:, 32 + t * 4 + ob:32 + t * 4 + ob + 1]))(),
                waits=[evs[ob]] + bwaits)
            aux.release(bi, spt)
            sq_pts.append(spt)
        pt_cv = p.step("vector", (lambda t=t: lambda v: v.reduce_sum(
            stats2[:, 16 + t:17 + t], stats[:, 32 + t * 4:32 + t * 4 + 4], axis=AX.X))(), waits=sq_pts)
        pt_c2 = p.step("vector", (lambda t=t: lambda v: v.tensor_scalar_mul(
            stats2[:, 16 + t:17 + t], stats2[:, 16 + t:17 + t], INVD))(), waits=[pt_cv])
        pt_c3 = p.step("vector", (lambda t=t: lambda v: v.tensor_tensor(
            stats2[:, 32 + t:33 + t], stats2[:, t:t + 1], stats2[:, t:t + 1], ALU.mult))(), waits=[pt_b])
        pt_c4 = p.step("vector", (lambda t=t: lambda v: v.tensor_tensor(
            stats2[:, 16 + t:17 + t], stats2[:, 16 + t:17 + t], stats2[:, 32 + t:33 + t],
            ALU.subtract))(), waits=[pt_c2, pt_c3])
        pt_c5 = p.step("scalar", (lambda t=t: lambda sc: sc.activation(
            stats2[:, 32 + t:33 + t], stats2[:, 16 + t:17 + t], AF.Ln, bias=fp(EPS),
            scale=fp(1.0)))(), waits=[pt_c4])
        pt_c6 = p.step("scalar", (lambda t=t: lambda sc: sc.activation(
            stats2[:, 16 + t:17 + t], stats2[:, 32 + t:33 + t], AF.Exp, bias=fp(0.0),
            scale=fp(-0.5)))(), waits=[pt_c5])
        xw = [out_pts[t - 2]] if t >= 2 else []
        pt_xl = p.step("sync", (lambda t=t, par=par: lambda sy: sy.dma_start(
            out=xlrb[:, par], in_=xspill[t * 128:(t + 1) * 128, :]))(),
            waits=[ln1_pts[t][0]] + xw, dma=True)
        z_pts = []
        for ob in range(4):
            cs = slice(ob * 512, (ob + 1) * 512)
            p1_ = p.step("vector", (lambda t=t, cs=cs: lambda v: v.tensor_scalar(
                ybuf[:, t, cs], ybuf[:, t, cs], stats2[:, t:t + 1], stats2[:, 16 + t:17 + t],
                ALU.subtract, ALU.mult))(), waits=[pt_c6] + sq_pts)
            p2_ = p.step("gpsimd", (lambda t=t, cs=cs: lambda g: g.tensor_tensor(
                ybuf[:, t, cs], ybuf[:, t, cs], gam_rep[:, cs], ALU.mult))(),
                waits=[p1_, pt_gam])
            p3_ = p.step("gpsimd", (lambda t=t, cs=cs: lambda g: g.tensor_tensor(
                ybuf[:, t, cs], ybuf[:, t, cs], bet_rep[:, cs], ALU.add))(),
                waits=[p2_, pt_bet])
            p4_ = p.step("vector", (lambda t=t, cs=cs, ob=ob, par=par: lambda v: v.scalar_tensor_tensor(
                ybuf[:, t, cs], ybuf[:, t, cs], 0.0, xlrb[:, par, cs], ALU.add, ALU.add,
                accum_out=stats[:, t * 4 + ob:t * 4 + ob + 1]))(), waits=[p3_, pt_xl])
            z_pts.append(p4_)
        pt_d1 = p.step("vector", (lambda t=t: lambda v: v.reduce_sum(
            stats2[:, t:t + 1], stats[:, t * 4:t * 4 + 4], axis=AX.X))(), waits=z_pts)
        pt_d2 = p.step("vector", (lambda t=t: lambda v: v.tensor_scalar_mul(
            stats2[:, t:t + 1], stats2[:, t:t + 1], INVD))(), waits=[pt_d1])
        sq2_pts = []
        for ob in range(4):
            bi, slot, bwaits = aux.take()
            spt = p.step("scalar", (lambda t=t, ob=ob, slot=slot: lambda sc: sc.activation(
                slot[:, 0:512], ybuf[:, t, ob * 512:(ob + 1) * 512], AF.Square,
                bias=fp(0.0), scale=fp(1.0),
                accum_out=stats[:, 32 + t * 4 + ob:32 + t * 4 + ob + 1]))(),
                waits=[z_pts[ob]] + bwaits)
            aux.release(bi, spt)
            sq2_pts.append(spt)
        pt_d3 = p.step("vector", (lambda t=t: lambda v: v.reduce_sum(
            stats2[:, 16 + t:17 + t], stats[:, 32 + t * 4:32 + t * 4 + 4], axis=AX.X))(), waits=sq2_pts)
        pt_d4 = p.step("vector", (lambda t=t: lambda v: v.tensor_scalar_mul(
            stats2[:, 16 + t:17 + t], stats2[:, 16 + t:17 + t], INVD))(), waits=[pt_d3])
        pt_d5 = p.step("vector", (lambda t=t: lambda v: v.tensor_tensor(
            stats2[:, 32 + t:33 + t], stats2[:, t:t + 1], stats2[:, t:t + 1], ALU.mult))(), waits=[pt_d2])
        pt_d6 = p.step("vector", (lambda t=t: lambda v: v.tensor_tensor(
            stats2[:, 16 + t:17 + t], stats2[:, 16 + t:17 + t], stats2[:, 32 + t:33 + t],
            ALU.subtract))(), waits=[pt_d4, pt_d5])
        pt_d7 = p.step("scalar", (lambda t=t: lambda sc: sc.activation(
            stats2[:, 32 + t:33 + t], stats2[:, 16 + t:17 + t], AF.Ln, bias=fp(EPS),
            scale=fp(1.0)))(), waits=[pt_d6])
        pt_d8 = p.step("scalar", (lambda t=t: lambda sc: sc.activation(
            stats2[:, 16 + t:17 + t], stats2[:, 32 + t:33 + t], AF.Exp, bias=fp(0.0),
            scale=fp(-0.5)))(), waits=[pt_d7])
        fin_pts = []
        for ob in range(4):
            cs = slice(ob * 512, (ob + 1) * 512)
            fpt = p.step("vector", (lambda t=t, cs=cs: lambda v: v.tensor_scalar(
                ybuf[:, t, cs], ybuf[:, t, cs], stats2[:, t:t + 1], stats2[:, 16 + t:17 + t],
                ALU.subtract, ALU.mult))(), waits=[pt_d8] + sq2_pts)
            fin_pts.append(fpt)
        opt = p.step("sync", (lambda t=t: lambda sy: sy.dma_start(
            out=out[t * 128:(t + 1) * 128, :], in_=ybuf[:, t]))(), waits=fin_pts, dma=True)
        out_pts.append(opt)

    p.step("sync", lambda sy: sy.nop(), waits=out_pts, inc=(p.sem["sync"], 1))

    p.emit()
    return nc, DBG


def _swz_quarters(w):
    dmat, n = w.shape
    return np.ascontiguousarray(
        w.reshape(dmat // 128, 128, n // 512, 512).transpose(2, 1, 0, 3))


def _prep_weights(i):
    wts = {}
    wts["wq"] = _swz_quarters(np.asarray(i["L0_attn_wq_w"], np.float32))
    wts["wk"] = _swz_quarters(np.asarray(i["L0_attn_wk_w"], np.float32))
    wts["wv"] = _swz_quarters(np.asarray(i["L0_attn_wv_w"], np.float32))
    wts["wo"] = _swz_quarters(np.asarray(i["L0_attn_wo_w"], np.float32))
    for nm, key in (("wkl", "L0_attn_wkl_w"), ("wvl", "L0_attn_wvl_w")):
        w = np.asarray(i[key], np.float32)
        wts[nm] = np.ascontiguousarray(w.reshape(16, 128, 512).transpose(1, 0, 2))
    for nm, key in (("upk", "L0_attn_upk_w"), ("upv", "L0_attn_upv_w")):
        w = np.asarray(i[key], np.float32)
        wts[nm] = np.ascontiguousarray(w.reshape(16, 4, 128, 128).transpose(2, 0, 1, 3))
    g = np.asarray(i["L0_ffn_gate_w"], np.float32)
    wts["gate"] = np.ascontiguousarray(g.reshape(16, 128, 8).transpose(1, 0, 2))
    up = np.asarray(i["L0_ffn_up_w"], np.float32)
    wts["upw"] = np.ascontiguousarray(up.reshape(16, 128, 32, 256).transpose(2, 1, 0, 3))
    dn = np.asarray(i["L0_ffn_down_w"], np.float32)
    wts["dnw"] = np.ascontiguousarray(dn.reshape(4, 16, 128, 4, 512).transpose(0, 3, 2, 1, 4))
    wts["exw"] = _swz_quarters(np.asarray(i["L0_ffn_expert_weight"], np.float32))
    for nm, key in (("wqb", "L0_attn_wq_b"), ("wkb", "L0_attn_wk_b"), ("wvb", "L0_attn_wv_b")):
        wts[nm] = np.ascontiguousarray(np.asarray(i[key], np.float32).reshape(16, 128).T)
    for nm, key in (("wklb", "L0_attn_wkl_b"), ("wvlb", "L0_attn_wvl_b")):
        wts[nm] = np.ascontiguousarray(np.asarray(i[key], np.float32).reshape(4, 128).T)
    wts["upkb"] = np.ascontiguousarray(np.asarray(i["L0_attn_upk_b"], np.float32).T)
    wts["upvb"] = np.ascontiguousarray(np.asarray(i["L0_attn_upv_b"], np.float32).reshape(-1))
    wts["wob"] = np.ascontiguousarray(np.asarray(i["L0_attn_wo_b"], np.float32))
    wts["lng"] = np.ascontiguousarray(np.asarray(i["L0_ffn_ln_gamma"], np.float32))
    wts["lnb"] = np.ascontiguousarray(np.asarray(i["L0_ffn_ln_beta"], np.float32))
    return wts


def kernel(**inputs):
    x = np.asarray(inputs["L0_attn_x"], np.float32)
    wts = _prep_weights(inputs)
    nc, _ = build()
    in_maps = []
    for core in range(8):
        b, a = core // 2, core % 2
        xs = np.ascontiguousarray(x[b].reshape(16, 128, D)[a::2].reshape(TOK, D))
        m = {"xin": xs, "parity": np.array([a], np.float32)}
        m.update(wts)
        in_maps.append(m)
    res = run_bass_kernel_spmd(nc, in_maps, core_ids=list(range(8)))
    outp = np.empty((4, 2048, D), np.float32)
    for core in range(8):
        b, a = core // 2, core % 2
        outp[b].reshape(16, 128, D)[a::2] = res.results[core]["out"].reshape(NT, 128, D)
    return outp


# version 15
# speedup vs baseline: 1.0924x; 1.0035x over previous
import numpy as np
import concourse.bass as bass
import concourse.mybir as mybir
from concourse.bass_utils import run_bass_kernel_spmd

F32 = mybir.dt.float32
BF16 = mybir.dt.bfloat16
AF = mybir.ActivationFunctionType
ALU = mybir.AluOpType
AX = mybir.AxisListType
fp = np.float32

D = 2048
FC = 16
TOK = 1024
NT = 8
H = 16
DH = 128
DL = 512
NDL = 4
S = 2048
KC = 16
INTER = 8192
SCALE = float(1.0 / np.sqrt(np.float32(128.0)))
EPS = 1e-5
INVD = float(np.float32(1.0) / np.float32(2048.0))


def jmin(c):
    return c // 2


class Prog:
    ENGS = ("tensor", "vector", "scalar", "gpsimd", "sync")

    def __init__(self, nc):
        self.nc = nc
        self.steps = []
        self.sem = {}
        for e in self.ENGS:
            self.sem[e] = _S(nc.alloc_semaphore("sm_" + e))
        self.NQ = 4
        self.qsems = {"sync": [_S(nc.alloc_semaphore(f"sm_dsync{i}")) for i in range(self.NQ)],
                      "gpsimd": [_S(nc.alloc_semaphore(f"sm_dgps{i}")) for i in range(self.NQ)]}
        self.qidx = {"sync": 0, "gpsimd": 0}

    def extra_sem(self, name):
        return _S(self.nc.alloc_semaphore(name))

    def step(self, eng, emit, waits=(), dma=False, inc=None):
        extra = []
        if inc is not None:
            s, amt = inc
        elif dma:
            i = self.qidx[eng] % self.NQ
            self.qidx[eng] += 1
            s = self.qsems[eng][i]
            amt = 16
            if s.n > 0:
                extra = [(s, s.n)]
        else:
            s = self.sem[eng]
            amt = 1
        s.n += amt
        pt = (s, s.n)
        merged = {}
        for (ws, wv) in list(waits) + extra:
            if wv is None or wv <= 0:
                continue
            if id(ws) in merged:
                merged[id(ws)] = (ws, max(merged[id(ws)][1], wv))
            else:
                merged[id(ws)] = (ws, wv)
        self.steps.append((eng, emit, list(merged.values()), (s, amt)))
        return pt

    def pt(self, eng):
        key = eng
        return (self.sem[key], self.sem[key].n)

    def emit(self):
        nc = self.nc
        with nc.Block() as block:
            @block.tensor
            def _(e):
                self._run("tensor", e)

            @block.vector
            def _(e):
                self._run("vector", e)

            @block.scalar
            def _(e):
                self._run("scalar", e)

            @block.gpsimd
            def _(e):
                self._run("gpsimd", e)

            @block.sync
            def _(e):
                self._run("sync", e)

    def _run(self, name, eng):
        for (e, emit, waits, inc) in self.steps:
            if e != name:
                continue
            for (s, v) in waits:
                eng.wait_ge(s.h, v)
            ins = emit(eng)
            assert ins is not None, f"step on {name} returned no instruction"
            ins.then_inc(inc[0].h, inc[1])


class _S:
    __slots__ = ("h", "n")

    def __init__(self, handle):
        self.h = handle
        self.n = 0


class PsumPool:

    def __init__(self, prog, slots):
        self.p = prog
        self.slots = slots
        self.rel = [None] * len(slots)
        self.i = 0

    def take(self):
        i = self.i % len(self.slots)
        self.i += 1
        waits = [self.rel[i]] if self.rel[i] is not None else []
        return i, self.slots[i], waits

    def release(self, i, pt):
        self.rel[i] = pt


def build(debug=()):
    nc = bass.Bass("TRN2", target_bir_lowering=False)
    p = Prog(nc)

    xin = nc.dram_tensor("xin", [TOK, D], F32, kind="ExternalInput")
    wq = nc.dram_tensor("wq", [4, 128, FC, 512], F32, kind="ExternalInput")
    wk = nc.dram_tensor("wk", [4, 128, FC, 512], F32, kind="ExternalInput")
    wv = nc.dram_tensor("wv", [4, 128, FC, 512], F32, kind="ExternalInput")
    wo = nc.dram_tensor("wo", [4, 128, FC, 512], F32, kind="ExternalInput")
    wkl = nc.dram_tensor("wkl", [128, FC, DL], F32, kind="ExternalInput")
    wvl = nc.dram_tensor("wvl", [128, FC, DL], F32, kind="ExternalInput")
    upk = nc.dram_tensor("upk", [128, H, NDL, 128], F32, kind="ExternalInput")
    upv = nc.dram_tensor("upv", [128, H, NDL, 128], F32, kind="ExternalInput")
    gate = nc.dram_tensor("gate", [128, FC, 8], F32, kind="ExternalInput")
    upw = nc.dram_tensor("upw", [32, 128, FC, 256], F32, kind="ExternalInput")
    dnw = nc.dram_tensor("dnw", [4, 4, 128, FC, 512], F32, kind="ExternalInput")
    exw = nc.dram_tensor("exw", [4, 128, FC, 512], F32, kind="ExternalInput")
    wqb = nc.dram_tensor("wqb", [128, FC], F32, kind="ExternalInput")
    wkb = nc.dram_tensor("wkb", [128, FC], F32, kind="ExternalInput")
    wvb = nc.dram_tensor("wvb", [128, FC], F32, kind="ExternalInput")
    wklb = nc.dram_tensor("wklb", [128, NDL], F32, kind="ExternalInput")
    wvlb = nc.dram_tensor("wvlb", [128, NDL], F32, kind="ExternalInput")
    upkb = nc.dram_tensor("upkb", [128, H], F32, kind="ExternalInput")
    upvb = nc.dram_tensor("upvb", [D], F32, kind="ExternalInput")
    wob = nc.dram_tensor("wob", [D], F32, kind="ExternalInput")
    lng = nc.dram_tensor("lng", [D], F32, kind="ExternalInput")
    lnb = nc.dram_tensor("lnb", [D], F32, kind="ExternalInput")
    parity = nc.dram_tensor("parity", [1], F32, kind="ExternalInput")
    out = nc.dram_tensor("out", [TOK, D], F32, kind="ExternalOutput")

    kvk = nc.dram_tensor("kvk", [D + 1, TOK], BF16)
    kall = nc.dram_tensor("kall", [2 * (D + 1), TOK], BF16)
    kvv = nc.dram_tensor("kvv", [TOK, D], BF16)
    vall = nc.dram_tensor("vall", [S, D], BF16)
    xspill = nc.dram_tensor("xspill", [TOK, D], F32)

    DBG = {}

    def dbg(name, shape, dtype=F32):
        if name in debug:
            if name not in DBG:
                DBG[name] = nc.dram_tensor("dbg_" + name, shape, dtype, kind="ExternalOutput")
            return DBG[name]
        return None

    ARENA = 210112
    arena = nc.alloc_sbuf_tensor("arena", [128, ARENA], mybir.dt.int8).ap()

    def carve(off, dtype, shape):
        n = int(np.prod(shape)) * mybir.dt.size(dtype)
        assert off % 64 == 0 and off + n <= ARENA, (off, n)
        v = arena[:, off:off + n].bitcast(dtype)
        if len(shape) > 1:
            kw = {f"a{i}": shape[i] for i in range(len(shape))}
            v = v.rearrange("p (" + " ".join(kw) + ") -> p " + " ".join(kw), **kw)
        return v

    ident = carve(0, BF16, [128])
    tri = carve(256, BF16, [128])
    mlo = carve(512, BF16, [128])
    mhi = carve(768, BF16, [128])
    ones_bf = carve(1024, BF16, [1024])
    ident32 = carve(3072, F32, [128])
    gate_sb = carve(3584, F32, [FC, 8])
    wqb_c = carve(4096, F32, [16])
    wkb_c = carve(4160, F32, [16])
    wvb_c = carve(4224, F32, [16])
    wklb_c = carve(4288, F32, [16])
    wvlb_c = carve(4352, F32, [16])
    upkb_c = carve(4416, F32, [16])
    a2 = carve(4480, F32, [64])
    upvb_rep = carve(4736, BF16, [2048])
    qn_row = carve(8832, F32, [16])
    knmax_row = carve(8896, BF16, [32])
    mscr = carve(8960, BF16, [32])
    mtmp = carve(9024, F32, [16])
    negm = carve(9088, F32, [16])
    avr = carve(9152, F32, [16])
    stats = carve(9216, F32, [64])
    stats2 = carve(9472, F32, [64])
    gmax = carve(9728, F32, [64])
    scale_c = carve(9984, F32, [16])
    par_sb = carve(10048, F32, [16])
    zrow = carve(10112, BF16, [1024])

    R1 = 13504
    xld = carve(R1, F32, [2, 2048])
    xldb = carve(R1 + 16384, BF16, [2, 2048])
    kstg = carve(R1, BF16, [2, 1024])
    vstg = carve(R1 + 4096, BF16, [2, 8, 128])
    ksqb = carve(R1 + 12288, BF16, [1024])
    t1T = carve(R1 + 24576, BF16, [FC, 1024])
    latT = carve(R1 + 57344, BF16, [NDL, 1024])
    eSB = carve(R1, BF16, [2, KC, 1024])
    ybuf = carve(R1, F32, [NT, 2048])

    R2 = R1 + 65536
    xT = carve(R2, BF16, [FC, 1024])
    xTf = xT

    R3 = R2 + 32768
    qT = carve(R3, BF16, [H, 1024])
    wob_rep = carve(R3, F32, [2048])
    xrb = carve(R3 + 8192, F32, [2, 2048])
    xcastb = carve(R3 + 24576, BF16, [2, 2048])
    hq = carve(R3, BF16, [16, 1024])
    logits32 = carve(R3, F32, [1024])
    pbs32 = carve(R3 + 4096, F32, [1024])


    R4 = R3 + 32768
    wbuf = carve(R4, BF16, [2, FC, 512])
    kbuf = carve(R4, BF16, [2, 2048])
    vbuf = carve(R4 + 8192, BF16, [2, KC, 132])
    ostg = carve(R4 + 25600, BF16, [NT, 128])
    p7buf = carve(R4, BF16, [2, FC, 512])

    R5 = R4 + 32768
    upk_sb = carve(R5, BF16, [H, NDL, 128])
    xl32 = carve(R5, F32, [2, 512])
    upv_sb = carve(R5 + 16384, BF16, [H, NDL, 128])
    aoT = carve(R5, BF16, [H, 1024])
    gam_rep = carve(R5, F32, [2048])
    bet_rep = carve(R5 + 8192, F32, [2048])
    xlrb = carve(R5 + 16384, F32, [2, 2048])

    pb = [nc.alloc_psum_tensor(f"pb{i}", [128, 512], F32).ap() for i in range(6)]
    pt_ = [nc.alloc_psum_tensor(f"pt{i}", [128, 1024], BF16).ap() for i in range(2)]
    mm = PsumPool(p, pb[:5])
    aux = PsumPool(p, pb[5:6])
    tp = PsumPool(p, pt_)

    TRI = tri.bitcast(BF16)

    a2v = a2.rearrange("p (i j) -> p i j", i=8)[0:8]

    def prep_ms(g):
        g.memset(ident[:, 0:128], 0.0)
        g.memset(tri[:, 0:128], 0.0)
        g.memset(ones_bf[:, :], 1.0)
        g.memset(ident32[:, 0:128], 0.0)
        g.memset(knmax_row[0:1, :], 0.0)
        g.memset(zrow[0:1, :], 0.0)
        return g.memset(a2[0:8, :], 0.0)

    pt_ms = p.step("gpsimd", prep_ms)

    def prep_a1(g):
        g.affine_select(out=ident[:, 0:128], in_=ident[:, 0:128], compare_op=ALU.not_equal,
                        fill=1.0, base=0, pattern=[[-1, 128]], channel_multiplier=1)
        g.affine_select(out=tri[:, 0:128], in_=tri[:, 0:128], compare_op=ALU.is_gt,
                        fill=1.0, base=0, pattern=[[-1, 128]], channel_multiplier=1)
        g.affine_select(out=ident32[:, 0:128], in_=ident32[:, 0:128], compare_op=ALU.not_equal,
                        fill=1.0, base=0, pattern=[[-1, 128]], channel_multiplier=1)
        return g.affine_select(out=a2v, in_=a2v, compare_op=ALU.not_equal, fill=1.0,
                               base=0, pattern=[[-1, 8], [0, 8]], channel_multiplier=1)

    pt_a1 = p.step("gpsimd", prep_a1, waits=[pt_ms])
    pt_prep = p.step("gpsimd", lambda g: g.affine_select(
        out=a2v, in_=a2v, compare_op=ALU.not_equal, fill=1.0,
        base=0, pattern=[[0, 8], [-1, 8]], channel_multiplier=1), waits=[pt_a1])

    bias_pt = {}
    for nm, dst, srct in (("wqb", wqb_c, wqb), ("wkb", wkb_c, wkb), ("wvb", wvb_c, wvb),
                          ("upkb", upkb_c, upkb)):
        bias_pt[nm] = p.step("sync", (lambda dst=dst, srct=srct: lambda sy: sy.dma_start(
            out=dst[:, :], in_=srct[:, :]))(), dma=True)
    bias_pt["wklb"] = p.step("sync", lambda sy: sy.dma_start(out=wklb_c[:, 0:4], in_=wklb[:, :]), dma=True)
    bias_pt["wvlb"] = p.step("sync", lambda sy: sy.dma_start(out=wvlb_c[:, 0:4], in_=wvlb[:, :]), dma=True)
    pt_upvb = p.step("gpsimd", lambda g: g.dma_start(
        out=upvb_rep[:, :], in_=upvb[None, :].broadcast_to([128, 2048])), dma=True)
    pt_gate = p.step("sync", lambda sy: sy.dma_start(
        out=gate_sb.rearrange("p a b -> p (a b)"), in_=gate.rearrange("p a b -> p (a b)")), dma=True)

    pt_acol = p.step("sync", lambda sy: sy.dma_start(
        out=mtmp[:, 0:1], in_=parity[None, :].broadcast_to([128, 1])), dma=True)
    pt_msk0 = p.step("vector", lambda v: v.tensor_scalar_max(
        mlo[:, 0:128], tri[:, 0:128], mtmp[:, 0:1]), waits=[pt_acol, pt_prep])
    pt_masks = p.step("vector", lambda v: v.tensor_scalar_mul(
        mhi[:, 0:128], tri[:, 0:128], mtmp[:, 0:1]), waits=[pt_acol, pt_prep])

    cast_pts = []
    xt_evac_pts = []
    for t in range(NT):
        par = t % 2
        waits = [cast_pts[t - 2]] if t >= 2 else []
        pt_x = p.step("sync", (lambda t=t, par=par: lambda sy: sy.dma_start(
            out=xld[:, par], in_=xin[t * 128:(t + 1) * 128, :]))(), waits=waits, dma=True)
        pt_c = p.step("vector", (lambda par=par: lambda v: v.tensor_copy(
            xldb[:, par], xld[:, par]))(), waits=[pt_x])
        cast_pts.append(pt_c)
        for g8 in range(2):
            i, slot, twaits = tp.take()
            def mktr(par=par, g8=g8, slot=slot):
                def emit(te):
                    ins = None
                    for k in range(8):
                        c = g8 * 8 + k
                        ins = te.transpose(slot[:, k * 128:(k + 1) * 128],
                                           xldb[:, par, c * 128:(c + 1) * 128],
                                           ident[:, 0:128])
                    return ins
                return emit
            pt_t = p.step("tensor", mktr(), waits=[pt_c, pt_prep] + twaits)
            pt_e = p.step("vector", (lambda t=t, g8=g8, slot=slot: lambda v: v.tensor_copy(
                xT[:, g8 * 8:(g8 + 1) * 8, t * 128:(t + 1) * 128],
                slot.rearrange("p (a b) -> p a b", b=128)))(), waits=[pt_t])
            tp.release(i, pt_e)
            xt_evac_pts.append(pt_e)
    pt_xT = p.pt("vector")

    d = dbg("xT", [128, FC, TOK], BF16)
    if d is not None:
        p.step("sync", lambda sy, d=d: sy.dma_start(out=d.ap(), in_=xT[:, :, :]),
               waits=[pt_xT], dma=True)

    def proj_stream(wdram, bias_c, dst, nof, src_chunk, src_pt, extra_mm_waits=(),
                    first_war=(), bias_wait=None):
        nq = (nof + 3) // 4
        wpts = []
        qmm_pts = []
        evac_pts = [None] * nof
        for q in range(nq):
            par = q % 2
            waits = [qmm_pts[q - 2]] if q >= 2 else list(first_war)
            wpt = p.step("gpsimd", (lambda q=q, par=par: lambda g: g.dma_start(
                out=wbuf[:, par], in_=wdram[q]))(), waits=waits, dma=True)
            wpts.append(wpt)
            for oc in range(min(4, nof - q * 4)):
                of = q * 4 + oc
                for piece in range(2):
                    cs = slice(piece * 512, (piece + 1) * 512)
                    bi, slot, bwaits = mm.take()
                    def mk(par=par, oc=oc, cs=cs, slot=slot):
                        def emit(te):
                            ins = None
                            for c in range(FC):
                                ins = te.matmul(slot[:, 0:512],
                                                wbuf[:, par, c, oc * 128:(oc + 1) * 128],
                                                src_chunk(c)[:, cs],
                                                start=(c == 0), stop=(c == FC - 1))
                            return ins
                        return emit
                    p.step("tensor", mk(), waits=[wpt, src_pt] + bwaits + list(extra_mm_waits))
                    mmpt = p.pt("tensor")
                    def ev(of=of, cs=cs, slot=slot):
                        def emit(v):
                            if bias_c is not None:
                                return v.tensor_scalar_add(dst[:, of, cs], slot[:, 0:512],
                                                           bias_c[:, of:of + 1])
                            return v.tensor_copy(dst[:, of, cs], slot[:, 0:512])
                        return emit
                    ept = p.step("vector", ev(), waits=[mmpt] + ([bias_wait] if bias_wait else []))
                    mm.release(bi, ept)
                    evac_pts[of] = ept
            qmm_pts.append(p.pt("tensor"))
        return evac_pts, p.pt("tensor")

    t1_pts, pt_t1mm = proj_stream([wk[q] for q in range(4)], wkb_c, t1T, FC,
                                  lambda c: xT[:, c], pt_xT, bias_wait=bias_pt["wkb"])
    pt_t1 = p.pt("vector")

    pt_wkl = p.step("gpsimd", lambda g: g.dma_start(out=wbuf[:, 0], in_=wkl[:, :, :]),
                    waits=[pt_t1mm], dma=True)
    lat_pts = []
    for of in range(NDL):
        for piece in range(2):
            cs = slice(piece * 512, (piece + 1) * 512)
            bi, slot, bwaits = mm.take()
            def mk(of=of, cs=cs, slot=slot):
                def emit(te):
                    ins = None
                    for c in range(FC):
                        ins = te.matmul(slot[:, 0:512], wbuf[:, 0, c, of * 128:(of + 1) * 128],
                                        t1T[:, c, cs], start=(c == 0), stop=(c == FC - 1))
                    return ins
                return emit
            p.step("tensor", mk(), waits=[pt_wkl, pt_t1] + bwaits)
            mmpt = p.pt("tensor")
            ept = p.step("vector", (lambda of=of, cs=cs, slot=slot: lambda v: v.tensor_scalar_add(
                latT[:, of, cs], slot[:, 0:512], wklb_c[:, of:of + 1]))(),
                waits=[mmpt, bias_pt["wklb"]])
            mm.release(bi, ept)
            lat_pts.append(ept)
    pt_klat = p.pt("vector")
    pt_klatmm = p.pt("tensor")

    pt_upk = p.step("gpsimd", lambda g: g.dma_start(
        out=upk_sb.rearrange("p a b c -> p (a b c)"), in_=upk.rearrange("p a b c -> p (a b c)")), dma=True)
    pt_upv = p.step("gpsimd", lambda g: g.dma_start(
        out=upv_sb.rearrange("p a b c -> p (a b c)"), in_=upv.rearrange("p a b c -> p (a b c)")), dma=True)

    kvw_pts = []
    ksq_pts = []
    knmm_pts = []
    nrm_comb = []
    for h in range(H):
        par = h % 2
        kpts = []
        for piece in range(2):
            cs = slice(piece * 512, (piece + 1) * 512)
            bi, slot, bwaits = mm.take()
            def mk(h=h, cs=cs, slot=slot):
                def emit(te):
                    ins = None
                    for c in range(NDL):
                        ins = te.matmul(slot[:, 0:512], upk_sb[:, h, c, :], latT[:, c, cs],
                                        start=(c == 0), stop=(c == NDL - 1))
                    return ins
                return emit
            p.step("tensor", mk(), waits=[pt_upk, pt_klat] + bwaits)
            mmpt = p.pt("tensor")
            w = [mmpt, bias_pt["upkb"]]
            if h >= 2:
                w += [kvw_pts[h - 2], ksq_pts[h - 2]]
            ept = p.step("vector", (lambda par=par, h=h, cs=cs, slot=slot: lambda v: v.tensor_scalar_add(
                kstg[:, par, cs], slot[:, 0:512], upkb_c[:, h:h + 1]))(), waits=w)
            mm.release(bi, ept)
            kpts.append(ept)
        wpt = p.step("sync", (lambda h=h, par=par: lambda sy: sy.dma_start(
            out=kvk[h * 128:(h + 1) * 128, :], in_=kstg[:, par]))(), waits=kpts, dma=True)
        kvw_pts.append(wpt)
        sqw = kpts + ([knmm_pts[h - 1]] if h >= 1 else [])
        pt_sq = p.step("scalar", (lambda par=par: lambda sc: sc.activation(
            ksqb[:, :], kstg[:, par], AF.Square, bias=fp(0.0), scale=fp(1.0)))(), waits=sqw)
        ksq_pts.append(pt_sq)
        red_pts = []
        for piece in range(2):
            cs = slice(piece * 512, (piece + 1) * 512)
            bi, slot, bwaits = aux.take()
            pt_nm = p.step("tensor", (lambda cs=cs, slot=slot: lambda te: te.matmul(
                slot[0:1, 0:512], ones_bf[:, 0:1], ksqb[:, cs], start=True, stop=True))(),
                waits=[pt_sq, pt_prep] + bwaits)
            rpt = p.step("vector", (lambda piece=piece, slot=slot: lambda v: v.reduce_max(
                mtmp[0:1, 8 + piece:9 + piece], slot[0:1, 0:512], axis=AX.X))(),
                waits=[pt_nm] + ([nrm_comb[-1]] if nrm_comb else []))
            aux.release(bi, rpt)
            red_pts.append(rpt)
        knmm_pts.append(p.pt("tensor"))
        nrm_comb.append(p.step("vector", (lambda h=h: lambda v: v.tensor_tensor(
            knmax_row[0:1, h:h + 1], mtmp[0:1, 8:9], mtmp[0:1, 9:10], ALU.max))(), waits=red_pts))
    pt_knorm = p.pt("vector")
    pt_knT = p.pt("tensor")
    pt_zf = p.step("sync", lambda sy: sy.dma_start(
        out=kvk[D:D + 1, :], in_=zrow[0:1, :]), waits=[pt_ms], dma=True)
    kvw_pts.append(pt_zf)
    kvw_pts.append(p.step("sync", lambda sy: sy.dma_start(
        out=kvk[D:D + 1, 0:32], in_=knmax_row[0:1, 0:32]), waits=[pt_knorm, pt_zf], dma=True))

    s_agk = p.extra_sem("s_agk")
    pt_agk = p.step("gpsimd", lambda g: g.collective_compute(
        "AllGather", ALU.bypass, replica_groups=[[0, 1], [2, 3], [4, 5], [6, 7]],
        ins=[kvk.ap()], outs=[kall.ap()]), waits=kvw_pts, inc=(s_agk, 1))

    t2_pts, pt_t2mm = proj_stream([wv[q] for q in range(4)], wvb_c, t1T, FC,
                                  lambda c: xT[:, c], pt_xT, extra_mm_waits=[pt_klatmm],
                                  first_war=[pt_klatmm], bias_wait=bias_pt["wvb"])
    pt_t2 = p.pt("vector")

    pt_wvl = p.step("gpsimd", lambda g: g.dma_start(out=wbuf[:, 0], in_=wvl[:, :, :]),
                    waits=[pt_t2mm], dma=True)
    for of in range(NDL):
        for piece in range(2):
            cs = slice(piece * 512, (piece + 1) * 512)
            bi, slot, bwaits = mm.take()
            def mk(of=of, cs=cs, slot=slot):
                def emit(te):
                    ins = None
                    for c in range(FC):
                        ins = te.matmul(slot[:, 0:512], wbuf[:, 0, c, of * 128:(of + 1) * 128],
                                        t1T[:, c, cs], start=(c == 0), stop=(c == FC - 1))
                    return ins
                return emit
            p.step("tensor", mk(), waits=[pt_wvl, pt_t2] + bwaits)
            mmpt = p.pt("tensor")
            ept = p.step("vector", (lambda of=of, cs=cs, slot=slot: lambda v: v.tensor_scalar_add(
                latT[:, of, cs], slot[:, 0:512], wvlb_c[:, of:of + 1]))(),
                waits=[mmpt, bias_pt["wvlb"]])
            mm.release(bi, ept)
    pt_vlat = p.pt("vector")
    pt_vlmm = p.pt("tensor")

    vst_pts = []
    for h in range(H):
        par = h % 2
        ev_pts = []
        for t in range(NT):
            bi, slot, bwaits = mm.take()
            def mk(h=h, t=t, slot=slot):
                def emit(te):
                    ins = None
                    for c in range(NDL):
                        ins = te.matmul(slot[:, 0:128], latT[:, c, t * 128:(t + 1) * 128],
                                        upv_sb[:, h, c, :], start=(c == 0), stop=(c == NDL - 1))
                    return ins
                return emit
            p.step("tensor", mk(), waits=[pt_upv, pt_vlat] + bwaits)
            mmpt = p.pt("tensor")
            war = [vst_pts[h - 2]] if h >= 2 else []
            ept = p.step("vector", (lambda h=h, t=t, par=par, slot=slot: lambda v: v.tensor_tensor(
                vstg[:, par, t, :], slot[:, 0:128], upvb_rep[:, h * 128:(h + 1) * 128], ALU.add))(),
                waits=[mmpt, pt_upvb] + war)
            mm.release(bi, ept)
            ev_pts.append(ept)
        wpt = p.step("sync", (lambda h=h, par=par: lambda sy: sy.dma_start(
            out=kvv[:, h * 128:(h + 1) * 128].rearrange("(a p) d -> p a d", p=128),
            in_=vstg[:, par]))(), waits=ev_pts, dma=True)
        vst_pts.append(wpt)

    s_agv = p.extra_sem("s_agv")
    pt_agv = p.step("gpsimd", lambda g: g.collective_compute(
        "AllGather", ALU.bypass, replica_groups=[[0, 1], [2, 3], [4, 5], [6, 7]],
        ins=[kvv.ap()], outs=[vall.ap()]), waits=vst_pts, inc=(s_agv, 1))

    d = dbg("kvk", [D + 1, TOK], BF16)
    if d is not None:
        p.step("gpsimd", lambda g, d=d: g.dma_start(out=d.ap(), in_=kvk.ap()),
               waits=kvw_pts, dma=True)
    d = dbg("kvv", [TOK, D], BF16)
    if d is not None:
        p.step("gpsimd", lambda g, d=d: g.dma_start(out=d.ap(), in_=kvv.ap()),
               waits=vst_pts, dma=True)

    q_pts, pt_qmm = proj_stream([wq[q] for q in range(4)], wqb_c, qT, FC,
                                lambda c: xT[:, c], pt_xT, first_war=[pt_vlmm],
                                bias_wait=bias_pt["wqb"])
    qnmm_last = pt_knT
    for h in range(H):
        pt_sq = p.step("scalar", (lambda h=h: lambda sc: sc.activation(
            ksqb[:, :], qT[:, h], AF.Square, bias=fp(0.0), scale=fp(1.0)))(),
            waits=[q_pts[h], qnmm_last])
        red_pts = []
        for piece in range(2):
            cs = slice(piece * 512, (piece + 1) * 512)
            bi, slot, bwaits = aux.take()
            pt_nm = p.step("tensor", (lambda cs=cs, slot=slot: lambda te: te.matmul(
                slot[0:1, 0:512], ones_bf[:, 0:1], ksqb[:, cs], start=True, stop=True))(),
                waits=[pt_sq] + bwaits)
            rpt = p.step("vector", (lambda piece=piece, slot=slot: lambda v: v.reduce_max(
                mtmp[0:1, 8 + piece:9 + piece], slot[0:1, 0:512], axis=AX.X))(),
                waits=[pt_nm] + ([nrm_comb[-1]] if nrm_comb else []))
            aux.release(bi, rpt)
            red_pts.append(rpt)
        qnmm_last = p.pt("tensor")
        nrm_comb.append(p.step("vector", (lambda h=h: lambda v: v.tensor_tensor(
            qn_row[0:1, h:h + 1], mtmp[0:1, 8:9], mtmp[0:1, 9:10], ALU.max))(), waits=red_pts))
    pt_qnorm = p.pt("vector")
    pt_vones = p.step("gpsimd", lambda g: g.memset(vbuf[:, :, :, 128:129], 1.0),
                      waits=[pt_qmm])

    d = dbg("qT", [128, H, TOK], BF16)
    if d is not None:
        p.step("sync", lambda sy, d=d: sy.dma_start(out=d.ap(), in_=qT[:, :, :]),
               waits=[p.pt("vector")], dma=True)

    pt_m1 = p.step("sync", lambda sy: sy.dma_start(out=mscr[0:1, 0:16], in_=kall[D:D + 1, 0:16]),
                   waits=[(s_agk, 1)], dma=True)
    pt_m2 = p.step("sync", lambda sy: sy.dma_start(out=mscr[0:1, 16:32], in_=kall[2 * D + 1:2 * D + 2, 0:16]),
                   waits=[(s_agk, 1)], dma=True)
    pt_m3 = p.step("vector", lambda v: v.reduce_max(mtmp[0:1, 0:1], mscr[0:1, 0:32], axis=AX.X),
                   waits=[pt_m1, pt_m2])
    pt_m4 = p.step("vector", lambda v: v.reduce_max(mtmp[0:1, 1:2], qn_row[0:1, 0:16], axis=AX.X),
                   waits=[pt_qnorm])
    pt_m5 = p.step("vector", lambda v: v.tensor_tensor(mtmp[0:1, 2:3], mtmp[0:1, 0:1],
                                                       mtmp[0:1, 1:2], ALU.mult),
                   waits=[pt_m3, pt_m4])
    pt_m6 = p.step("scalar", lambda sc: sc.activation(mtmp[0:1, 3:4], mtmp[0:1, 2:3], AF.Ln,
                                                      bias=fp(1e-38), scale=fp(1.0)), waits=[pt_m5])
    pt_m7 = p.step("scalar", lambda sc: sc.activation(mtmp[0:1, 4:5], mtmp[0:1, 3:4], AF.Exp,
                                                      bias=fp(0.0), scale=fp(0.5)), waits=[pt_m6])
    pt_m8 = p.step("vector", lambda v: v.tensor_scalar_mul(mtmp[0:1, 5:6], mtmp[0:1, 4:5],
                                                           -SCALE), waits=[pt_m7])
    mdram = nc.dram_tensor("mdram", [1, 1], F32)
    pt_m9 = p.step("sync", lambda sy: sy.dma_start(
        out=mdram[0:1, 0:1], in_=mtmp[0:1, 5:6]), waits=[pt_m8], dma=True)
    pt_mprep = p.step("sync", lambda sy: sy.dma_start(
        out=negm[:, 0:1], in_=mdram[0:1, 0:1].broadcast_to([128, 1])), waits=[pt_m9], dma=True)

    kb_pts = [None] * (H + 2)
    vb_pts = [None] * (H + 2)
    scores_done = [None] * H
    av_done = [None] * H
    avr_mu = [None] * NT
    ostg_war = [None] * NT
    exp_done = [None] * H
    ao_pts = []

    head_ctx = {}

    def emit_av(h):
        par, vpts, mask_pts, cexp = head_ctx[h]
        mu_pts = []
        for j in range(NT):
            bi, slot, bwaits = mm.take()
            def mkav(par=par, j=j, slot=slot):
                def emit(te):
                    ins = None
                    for c in range(2 * j + 2):
                        ins = te.matmul(slot[:, 0:129], eSB[:, par, c, j * 128:(j + 1) * 128],
                                        vbuf[:, par, c, 0:129],
                                        start=(c == 0), stop=(c == 2 * j + 1))
                    return ins
                return emit
            avw = [mask_pts[2 * j + 1]] + [pt for c in range(2 * j + 2) for pt in cexp[c]]
            p.step("tensor", mkav(), waits=avw + vpts + bwaits + [pt_vones])
            avpt = p.pt("tensor")
            pt_r = p.step("vector", (lambda j=j, slot=slot: lambda v: v.reciprocal(
                avr[:, j:j + 1], slot[:, 128:129]))(),
                waits=[avpt] + ([avr_mu[j]] if avr_mu[j] is not None else []))
            pt_mu = p.step("vector", (lambda j=j, slot=slot: lambda v: v.tensor_scalar_mul(
                ostg[:, j], slot[:, 0:128], avr[:, j:j + 1]))(),
                waits=[pt_r] + ([ostg_war[j]] if ostg_war[j] is not None else []))
            avr_mu[j] = pt_mu
            mm.release(bi, pt_mu)
            mu_pts.append(pt_mu)
        ti, tslot, twaits = tp.take()
        def mktr(h=h, tslot=tslot):
            def emit(te):
                ins = None
                for j in range(NT):
                    ins = te.transpose(tslot[:, j * 128:(j + 1) * 128], ostg[:, j],
                                       ident[:, 0:128])
                return ins
            return emit
        pt_tr = p.step("tensor", mktr(), waits=mu_pts + twaits)
        for j in range(NT):
            ostg_war[j] = pt_tr
        pt_ev = p.step("vector", (lambda h=h, tslot=tslot: lambda v: v.tensor_copy(
            aoT[:, h, :], tslot[:, 0:1024]))(), waits=[pt_tr])
        tp.release(ti, pt_ev)
        ao_pts.append((pt_tr, pt_ev))
        av_done[h] = p.pt("tensor")

    for h in range(H):
        par = h % 2
        kwar = [scores_done[h - 2]] if h >= 2 else [pt_qmm]
        vwar = [av_done[h - 2]] if h >= 2 else [pt_qmm]
        kv = kbuf[:, par].rearrange("p (c j) -> p c j", j=128)
        kpts = []
        vpts = []
        for r in range(2):
            kpts.append(p.step("sync", (lambda h=h, r=r, kv=kv: lambda sy: sy.dma_start(
                out=kv[:, r::2, :],
                in_=kall[r * (D + 1) + h * 128: r * (D + 1) + (h + 1) * 128, :]
                .rearrange("p (a b) -> p a b", b=128)))(),
                waits=[(s_agk, 1)] + kwar, dma=True))
            vpts.append(p.step("sync", (lambda h=h, r=r, par=par: lambda sy: sy.dma_start(
                out=vbuf[:, par, r::2, 0:128],
                in_=vall[r * TOK:(r + 1) * TOK, h * 128:(h + 1) * 128]
                .rearrange("(a p) d -> p a d", p=128)))(),
                waits=[(s_agv, 1)] + vwar, dma=True))
        kb_pts[h] = kpts
        vb_pts[h] = vpts

        esb_war = [av_done[h - 2]] if h >= 2 else []
        chunk_exp_pts = []
        for c in range(KC):
            off = jmin(c) * 128
            cpts = []
            while off < TOK:
                wd = min(512, TOK - off)
                bi, slot, bwaits = mm.take()
                p.step("tensor", (lambda par=par, c=c, off=off, wd=wd, slot=slot, h=h: lambda te: te.matmul(
                    slot[:, 0:wd], kbuf[:, par, c * 128:(c + 1) * 128], qT[:, h, off:off + wd],
                    start=True, stop=True))(), waits=kpts + [q_pts[h]] + bwaits)
                mmpt = p.pt("tensor")
                ept = p.step("scalar", (lambda par=par, c=c, off=off, wd=wd, slot=slot: lambda sc: sc.activation(
                    eSB[:, par, c, off:off + wd], slot[:, 0:wd], AF.Exp,
                    bias=negm[:, 0:1], scale=fp(SCALE)))(),
                    waits=[mmpt, pt_mprep] + esb_war)
                mm.release(bi, ept)
                cpts.append(ept)
                off += wd
            chunk_exp_pts.append(cpts)
        scores_done[h] = p.pt("tensor")
        exp_done[h] = p.pt("scalar")

        mask_pts = []
        for c in range(KC):
            m = mlo if c % 2 == 0 else mhi
            o = jmin(c) * 128
            mask_pts.append(p.step("vector", (lambda par=par, c=c, m=m, o=o: lambda v: v.tensor_tensor(
                eSB[:, par, c, o:o + 128], eSB[:, par, c, o:o + 128],
                m[:, 0:128], ALU.mult))(), waits=[chunk_exp_pts[c][0], pt_masks]))
        pt_mask = mask_pts[-1]

        if h == 0:
            dkb = dbg("kbuf0", [128, 2048], BF16)
            if dkb is not None:
                p.step("sync", (lambda dkb=dkb: lambda sy: sy.dma_start(
                    out=dkb.ap(), in_=kbuf[:, 0]))(), waits=kpts, dma=True)
            dka = dbg("kall", [2 * (D + 1), TOK], BF16)
            if dka is not None:
                p.step("gpsimd", (lambda dka=dka: lambda g: g.dma_start(
                    out=dka.ap(), in_=kall.ap()))(), waits=[(s_agk, 1)], dma=True)
            d0 = dbg("eS0", [128, KC, TOK], BF16)
            if d0 is not None:
                p.step("sync", (lambda d0=d0: lambda sy: sy.dma_start(
                    out=d0.ap(), in_=eSB[:, 0]))(), waits=[pt_mask], dma=True)
            dm = dbg("masks", [128, 3, 128], BF16)
            if dm is not None:
                p.step("sync", (lambda dm=dm: lambda sy: sy.dma_start(
                    out=dm.ap()[:, 0], in_=tri[:, 0:128]))(), waits=[pt_mask], dma=True)
                p.step("sync", (lambda dm=dm: lambda sy: sy.dma_start(
                    out=dm.ap()[:, 1], in_=mlo[:, 0:128]))(), waits=[pt_mask], dma=True)
                p.step("sync", (lambda dm=dm: lambda sy: sy.dma_start(
                    out=dm.ap()[:, 2], in_=mhi[:, 0:128]))(), waits=[pt_mask], dma=True)
            dn_ = dbg("negm", [1, 16])
            if dn_ is not None:
                p.step("sync", (lambda dn_=dn_: lambda sy: sy.dma_start(
                    out=dn_.ap(), in_=mtmp[0:1, 0:16]))(), waits=[pt_mprep], dma=True)
            dnc = dbg("negmcol", [128, 1])
            if dnc is not None:
                p.step("sync", (lambda dnc=dnc: lambda sy: sy.dma_start(
                    out=dnc.ap(), in_=negm[:, 0:1]))(), waits=[pt_mprep], dma=True)
        head_ctx[h] = (par, vpts, mask_pts, chunk_exp_pts)
        if h >= 1:
            emit_av(h - 1)
    emit_av(H - 1)
    pt_attn_v = p.pt("vector")

    d = dbg("aoT", [128, H, TOK], BF16)
    if d is not None:
        p.step("sync", lambda sy, d=d: sy.dma_start(out=d.ap(), in_=aoT[:, :, :]),
               waits=[pt_attn_v], dma=True)
    d = dbg("aoTb", [128, H, TOK], BF16)
    if d is not None:
        p.step("sync", lambda sy, d=d: sy.dma_start(out=d.ap(), in_=aoT[:, :, :]),
               waits=[pt_attn_v], dma=True)

    pt_wob = p.step("sync", lambda sy: sy.dma_start(
        out=wob_rep[:, :], in_=wob[None, :].broadcast_to([128, 2048])),
        waits=[scores_done[H - 1]], dma=True)

    da2 = dbg("aoT2", [128, H, TOK], BF16)
    if da2 is not None:
        p.step("sync", (lambda da2=da2: lambda sy: sy.dma_start(
            out=da2.ap(), in_=aoT[:, :, :]))(), waits=[pt_attn_v], dma=True)

    wo_ev = [[None] * 4 for _ in range(NT)]
    woq_mm = []
    for q in range(4):
        par = q % 2
        wwaits = [woq_mm[q - 2]] if q >= 2 else [av_done[H - 1]]
        wpt = p.step("gpsimd", (lambda q=q, par=par: lambda g: g.dma_start(
            out=wbuf[:, par], in_=wo[q]))(), waits=wwaits, dma=True)
        for t in range(NT):
            bi, slot, bwaits = mm.take()
            def mkwo(t=t, par=par, slot=slot):
                def emit(te):
                    ins = None
                    for hh in range(FC):
                        ins = te.matmul(slot[:, 0:512], aoT[:, hh, t * 128:(t + 1) * 128],
                                        wbuf[:, par, hh, :],
                                        start=(hh == 0), stop=(hh == FC - 1))
                    return ins
                return emit
            p.step("tensor", mkwo(), waits=[wpt, pt_attn_v] + bwaits)
            mmpt = p.pt("tensor")
            ept = p.step("vector", (lambda t=t, q=q, slot=slot: lambda v: v.tensor_tensor(
                ybuf[:, t, q * 512:(q + 1) * 512], slot[:, 0:512],
                wob_rep[:, q * 512:(q + 1) * 512], ALU.add))(), waits=[mmpt, pt_wob])
            mm.release(bi, ept)
            wo_ev[t][q] = ept
        woq_mm.append(p.pt("tensor"))

    lg_pts = []
    xl_mm = [None, None]

    def emit_gate_tile(t, gate_in_waits):
        gi, gslot, gwaits = aux.take()
        for cq in range(4):
            qpar = cq % 2
            bi, slot, bwaits = mm.take()
            tr_last = None
            for i in range(4):
                c = cq * 4 + i
                p.step("tensor", (lambda t=t, c=c, i=i, slot=slot: lambda te: te.transpose(
                    slot[:, i * 128:(i + 1) * 128].bitcast(F32),
                    ybuf[:, t, c * 128:(c + 1) * 128],
                    ident32[:, 0:128]))(), waits=gate_in_waits + [pt_prep] + (bwaits if i == 0 else []))
            tr_last = p.pt("tensor")
            war = [xl_mm[qpar]] if xl_mm[qpar] is not None else []
            pev = p.step("vector", (lambda qpar=qpar, slot=slot: lambda v: v.tensor_copy(
                xl32[:, qpar], slot[:, 0:512]))(), waits=[tr_last] + war)
            mm.release(bi, pev)
            def mkg(cq=cq, qpar=qpar, gslot=gslot):
                def emit(te):
                    ins = None
                    for i in range(4):
                        c = cq * 4 + i
                        ins = te.matmul(gslot[0:8, 0:128], gate_sb[:, c, :],
                                        xl32[:, qpar, i * 128:(i + 1) * 128],
                                        start=(c == 0), stop=(c == FC - 1))
                    return ins
                return emit
            p.step("tensor", mkg(), waits=[pev, pt_gate] + (gwaits if cq == 0 else []))
            xl_mm[qpar] = p.pt("tensor")
        mmpt = p.pt("tensor")
        ept = p.step("vector", (lambda t=t, gslot=gslot: lambda v: v.tensor_copy(
            logits32[0:8, t * 128:(t + 1) * 128], gslot[0:8, 0:128]))(), waits=[mmpt])
        aux.release(gi, ept)
        lg_pts.append(ept)


    ln1_pts = []
    xtf_tr_pts = []
    for t in range(NT):
        par = t % 2
        xwar = [ln1_pts[t - 2][3]] if t >= 2 else [scores_done[H - 1]]
        pt_xr = p.step("sync", (lambda t=t, par=par: lambda sy: sy.dma_start(
            out=xrb[:, par], in_=xin[t * 128:(t + 1) * 128, :]))(),
            waits=xwar, dma=True)
        pt_add = p.step("vector", (lambda t=t, par=par: lambda v: v.scalar_tensor_tensor(
            ybuf[:, t, :], ybuf[:, t, :], 0.0, xrb[:, par, :], ALU.add, ALU.add,
            accum_out=stats2[:, t:t + 1]))(), waits=wo_ev[t] + [pt_xr])
        ev_pts = [pt_add]
        dy1 = dbg("y1d", [TOK, D])
        if dy1 is not None:
            dpt = p.step("sync", (lambda t=t, dy1=dy1: lambda sy: sy.dma_start(
                out=dy1.ap()[t * 128:(t + 1) * 128, :], in_=ybuf[:, t]))(),
                waits=ev_pts, dma=True)
            ev_pts = ev_pts + [dpt]
        pt_s2 = p.step("vector", (lambda t=t: lambda v: v.tensor_scalar_mul(
            stats2[:, t:t + 1], stats2[:, t:t + 1], INVD))(), waits=ev_pts)
        sq_pts = []
        for ob in range(4):
            bi, slot, bwaits = aux.take()
            spt = p.step("scalar", (lambda t=t, ob=ob, slot=slot: lambda sc: sc.activation(
                slot[:, 0:512], ybuf[:, t, ob * 512:(ob + 1) * 512], AF.Square,
                bias=fp(0.0), scale=fp(1.0),
                accum_out=stats[:, 32 + t * 4 + ob:32 + t * 4 + ob + 1]))(),
                waits=ev_pts + bwaits)
            aux.release(bi, spt)
            sq_pts.append(spt)
        pt_s3 = p.step("vector", (lambda t=t: lambda v: v.reduce_sum(
            stats2[:, 16 + t:17 + t], stats[:, 32 + t * 4:32 + t * 4 + 4], axis=AX.X))(), waits=sq_pts)
        pt_s5 = p.step("vector", (lambda t=t: lambda v: v.tensor_scalar_mul(
            stats2[:, 16 + t:17 + t], stats2[:, 16 + t:17 + t], INVD))(), waits=[pt_s3])
        pt_s6 = p.step("vector", (lambda t=t: lambda v: v.tensor_tensor(
            stats2[:, 32 + t:33 + t], stats2[:, t:t + 1], stats2[:, t:t + 1], ALU.mult))(),
            waits=[pt_s2])
        pt_s7 = p.step("vector", (lambda t=t: lambda v: v.tensor_tensor(
            stats2[:, 16 + t:17 + t], stats2[:, 16 + t:17 + t], stats2[:, 32 + t:33 + t],
            ALU.subtract))(), waits=[pt_s5, pt_s6])
        pt_s8 = p.step("scalar", (lambda t=t: lambda sc: sc.activation(
            stats2[:, 32 + t:33 + t], stats2[:, 16 + t:17 + t], AF.Ln, bias=fp(EPS),
            scale=fp(1.0)))(), waits=[pt_s7])
        pt_s9 = p.step("scalar", (lambda t=t: lambda sc: sc.activation(
            stats2[:, 16 + t:17 + t], stats2[:, 32 + t:33 + t], AF.Exp, bias=fp(0.0),
            scale=fp(-0.5)))(), waits=[pt_s8])
        nrm_pts = []
        for ob in range(4):
            npt = p.step("vector", (lambda t=t, ob=ob: lambda v: v.tensor_scalar(
                ybuf[:, t, ob * 512:(ob + 1) * 512], ybuf[:, t, ob * 512:(ob + 1) * 512],
                stats2[:, t:t + 1], stats2[:, 16 + t:17 + t], ALU.subtract, ALU.mult))(),
                waits=[pt_s9] + sq_pts)
            nrm_pts.append(npt)
        emit_gate_tile(t, nrm_pts)
        pt_sp = p.step("sync", (lambda t=t: lambda sy: sy.dma_start(
            out=xspill[t * 128:(t + 1) * 128, :], in_=ybuf[:, t]))(), waits=nrm_pts, dma=True)
        cwar = [xtf_tr_pts[-3]] if len(xtf_tr_pts) >= 3 else [av_done[H - 1]]
        pt_cast = p.step("scalar", (lambda t=t, par=par: lambda sc: sc.copy(
            xcastb[:, par], ybuf[:, t]))(), waits=nrm_pts + cwar)
        tprs = []
        for g8 in range(2):
            ti, tslot, twaits = tp.take()
            def mktr(par=par, g8=g8, tslot=tslot):
                def emit(te):
                    ins = None
                    for k in range(8):
                        c = g8 * 8 + k
                        ins = te.transpose(tslot[:, k * 128:(k + 1) * 128],
                                           xcastb[:, par, c * 128:(c + 1) * 128],
                                           ident[:, 0:128])
                    return ins
                return emit
            ptr = p.step("tensor", mktr(), waits=[pt_cast] + twaits)
            pev = p.step("vector", (lambda t=t, g8=g8, tslot=tslot: lambda v: v.tensor_copy(
                xTf[:, g8 * 8:(g8 + 1) * 8, t * 128:(t + 1) * 128],
                tslot.rearrange("p (a b) -> p a b", b=128)))(), waits=[ptr])
            tp.release(ti, pev)
            xtf_tr_pts.append(ptr)
            tprs.append(pev)
        ln1_pts.append((pt_sp, tprs[-1], pt_cast, p.pt("vector")))
        if t == 3:
            pt_xTf_h0 = p.pt("vector")
    pt_xTf = p.pt("vector")
    pt_xTf_T = p.pt("tensor")

    d = dbg("xln1", [TOK, D])
    if d is not None:
        p.step("gpsimd", lambda g, d=d: g.dma_start(out=d.ap(), in_=xspill.ap()),
               waits=[pt for (pt, _, _, _) in ln1_pts], dma=True)

    pb_pts = []
    for piece in range(2):
        cs = slice(piece * 512, (piece + 1) * 512)
        bi, slot, bwaits = aux.take()
        p.step("tensor", (lambda cs=cs, slot=slot: lambda te: te.matmul(
            slot[0:64, 0:512], a2[0:8, 0:64], logits32[0:8, cs], start=True, stop=True))(),
            waits=lg_pts + bwaits + [pt_prep])
        mmpt = p.pt("tensor")
        ept = p.step("vector", (lambda cs=cs, slot=slot: lambda v: v.tensor_copy(
            pbs32[0:64, cs], slot[0:64, 0:512]))(), waits=[mmpt])
        aux.release(bi, ept)
        pb_pts.append(ept)
    sc_pts = []
    for t in range(NT):
        bi, slot, bwaits = mm.take()
        ptr = p.step("tensor", (lambda t=t, slot=slot: lambda te: te.transpose(
            slot[:, 0:64].bitcast(F32), pbs32[0:64, t * 128:(t + 1) * 128],
            ident32[0:64, 0:64]))(), waits=pb_pts + bwaits)
        pcv = p.step("vector", (lambda slot=slot: lambda v: v.tensor_copy(
            gmax[:, 0:64], slot[:, 0:64]))(), waits=[ptr] + (sc_pts[-1:] if sc_pts else []))
        mm.release(bi, pcv)
        pms = p.step("vector", lambda v: v.memset(gmax[:, 0:64:9], -1e30), waits=[pcv])
        prd = p.step("vector", (lambda t=t: lambda v: v.reduce_max(
            scale_c[:, t:t + 1], gmax[:, 0:64], axis=AX.X))(), waits=[pms])
        sc_pts.append(prd)
    pt_gate_all = p.pt("vector")

    p7_last = [woq_mm[-1], woq_mm[-1]]
    y2_pts = [[None] * 4 for _ in range(NT)]

    def emit_expert():
        for q in range(4):
            par = q % 2
            wpt = p.step("gpsimd", (lambda q=q, par=par: lambda g: g.dma_start(
                out=p7buf[:, par], in_=exw[q]))(), waits=[p7_last[par]], dma=True)
            for t in range(NT):
                bi, slot, bwaits = mm.take()
                def mke(t=t, par=par, slot=slot):
                    def emit(te):
                        ins = None
                        for c in range(FC):
                            ins = te.matmul(slot[:, 0:512], xTf[:, c, t * 128:(t + 1) * 128],
                                            p7buf[:, par, c, :], start=(c == 0), stop=(c == FC - 1))
                        return ins
                    return emit
                p.step("tensor", mke(), waits=[wpt, ln1_pts[t][3]] + bwaits)
                mmpt = p.pt("tensor")
                ept = p.step("vector", (lambda t=t, q=q, slot=slot: lambda v: v.tensor_scalar_mul(
                    ybuf[:, t, q * 512:(q + 1) * 512], slot[:, 0:512], scale_c[:, t:t + 1]))(),
                    waits=[mmpt, sc_pts[t], ln1_pts[t][0], ln1_pts[t][2]])
                mm.release(bi, ept)
                y2_pts[t][q] = ept
            p7_last[par] = p.pt("tensor")

    pt_gam = p.step("sync", lambda sy: sy.dma_start(
        out=gam_rep[:, :], in_=lng[None, :].broadcast_to([128, 2048])),
        waits=[pt_xTf_T], dma=True)
    pt_bet = p.step("sync", lambda sy: sy.dma_start(
        out=bet_rep[:, :], in_=lnb[None, :].broadcast_to([128, 2048])),
        waits=[pt_xTf_T], dma=True)

    hq_pts = [None] * 16
    hq_extra_war = []
    for iq in range(4):
        for ib in range(8):
            par = ib % 2
            wpt = p.step("gpsimd", (lambda iq=iq, ib=ib, par=par: lambda g: g.dma_start(
                out=p7buf[:, par, :, 0:256], in_=upw[iq * 8 + ib]))(), waits=[p7_last[par]], dma=True)
            for icin in range(2):
                ic = ib * 2 + icin
                for piece in range(2):
                    cs = slice(piece * 512, (piece + 1) * 512)
                    bi, slot, bwaits = mm.take()
                    def mku(par=par, icin=icin, cs=cs, slot=slot):
                        def emit(te):
                            ins = None
                            for c in range(FC):
                                ins = te.matmul(slot[:, 0:512],
                                                p7buf[:, par, c, icin * 128:(icin + 1) * 128],
                                                xTf[:, c, cs], start=(c == 0), stop=(c == FC - 1))
                            return ins
                        return emit
                    hq_war = ([hq_pts[ic]] if hq_pts[ic] is not None else []) + hq_extra_war
                    p.step("tensor", mku(), waits=[wpt, pt_xTf, pt_xTf_T] + bwaits)
                    mmpt = p.pt("tensor")
                    rpt = p.step("scalar", (lambda ic=ic, cs=cs, slot=slot: lambda sc: sc.activation(
                        hq[:, ic, cs], slot[:, 0:512], AF.Relu, bias=fp(0.0), scale=fp(1.0)))(),
                        waits=[mmpt] + hq_war)
                    qpt = p.step("vector", (lambda ic=ic, cs=cs: lambda v: v.tensor_tensor(
                        hq[:, ic, cs], hq[:, ic, cs], hq[:, ic, cs], ALU.mult))(), waits=[rpt])
                    mm.release(bi, qpt)
            p7_last[par] = p.pt("tensor")
        pt_up_done = p.pt("vector")
        if iq == 0:
            emit_expert()
        if iq == 0:
            dh_ = dbg("hq0", [128, 16, 1024], BF16)
            if dh_ is not None:
                dpt = p.step("sync", (lambda dh_=dh_: lambda sy: sy.dma_start(
                    out=dh_.ap(), in_=hq[:, :, :]))(), waits=[pt_up_done], dma=True)
                hq_extra_war.append(dpt)
        for ob in range(4):
            par = ob % 2
            wpt = p.step("gpsimd", (lambda iq=iq, ob=ob, par=par: lambda g: g.dma_start(
                out=p7buf[:, par], in_=dnw[iq, ob]))(), waits=[p7_last[par]], dma=True)
            for t in range(NT):
                bi, slot, bwaits = mm.take()
                def mkd(t=t, par=par, slot=slot):
                    def emit(te):
                        ins = None
                        for ic in range(16):
                            ins = te.matmul(slot[:, 0:512], hq[:, ic, t * 128:(t + 1) * 128],
                                            p7buf[:, par, ic, :], start=(ic == 0), stop=(ic == 15))
                        return ins
                    return emit
                p.step("tensor", mkd(), waits=[wpt, pt_up_done] + bwaits)
                mmpt = p.pt("tensor")
                if iq < 3:
                    ept = p.step("vector", (lambda t=t, ob=ob, slot=slot: lambda v: v.tensor_tensor(
                        ybuf[:, t, ob * 512:(ob + 1) * 512], ybuf[:, t, ob * 512:(ob + 1) * 512],
                        slot[:, 0:512], ALU.add))(), waits=[mmpt, y2_pts[t][ob]])
                else:
                    ept = p.step("vector", (lambda t=t, ob=ob, slot=slot: lambda v: v.scalar_tensor_tensor(
                        ybuf[:, t, ob * 512:(ob + 1) * 512], slot[:, 0:512], 0.0,
                        ybuf[:, t, ob * 512:(ob + 1) * 512], ALU.add, ALU.add,
                        accum_out=stats[:, t * 4 + ob:t * 4 + ob + 1]))(),
                        waits=[mmpt, y2_pts[t][ob]])
                mm.release(bi, ept)
                y2_pts[t][ob] = ept
            p7_last[par] = p.pt("tensor")
        hq_pts = [p.pt("tensor")] * 16

    dscl = dbg("scl", [128, 8])
    if dscl is not None:
        p.step("sync", (lambda dscl=dscl: lambda sy: sy.dma_start(
            out=dscl.ap(), in_=scale_c[:, 0:8]))(), waits=sc_pts, dma=True)

    out_pts = []
    for t in range(NT):
        par = t % 2
        evs = y2_pts[t]
        dy2 = dbg("y2d", [TOK, D])
        if dy2 is not None:
            dpt = p.step("sync", (lambda t=t, dy2=dy2: lambda sy: sy.dma_start(
                out=dy2.ap()[t * 128:(t + 1) * 128, :], in_=ybuf[:, t]))(),
                waits=evs, dma=True)
            evs = evs + [dpt]
        pt_a = p.step("vector", (lambda t=t: lambda v: v.reduce_sum(
            stats2[:, t:t + 1], stats[:, t * 4:t * 4 + 4], axis=AX.X))(), waits=evs)
        pt_b = p.step("vector", (lambda t=t: lambda v: v.tensor_scalar_mul(
            stats2[:, t:t + 1], stats2[:, t:t + 1], INVD))(), waits=[pt_a])
        sq_pts = []
        for ob in range(4):
            bi, slot, bwaits = aux.take()
            spt = p.step("scalar", (lambda t=t, ob=ob, slot=slot: lambda sc: sc.activation(
                slot[:, 0:512], ybuf[:, t, ob * 512:(ob + 1) * 512], AF.Square,
                bias=fp(0.0), scale=fp(1.0),
                accum_out=stats[:, 32 + t * 4 + ob:32 + t * 4 + ob + 1]))(),
                waits=[evs[ob]] + bwaits)
            aux.release(bi, spt)
            sq_pts.append(spt)
        pt_cv = p.step("vector", (lambda t=t: lambda v: v.reduce_sum(
            stats2[:, 16 + t:17 + t], stats[:, 32 + t * 4:32 + t * 4 + 4], axis=AX.X))(), waits=sq_pts)
        pt_c2 = p.step("vector", (lambda t=t: lambda v: v.tensor_scalar_mul(
            stats2[:, 16 + t:17 + t], stats2[:, 16 + t:17 + t], INVD))(), waits=[pt_cv])
        pt_c3 = p.step("vector", (lambda t=t: lambda v: v.tensor_tensor(
            stats2[:, 32 + t:33 + t], stats2[:, t:t + 1], stats2[:, t:t + 1], ALU.mult))(), waits=[pt_b])
        pt_c4 = p.step("vector", (lambda t=t: lambda v: v.tensor_tensor(
            stats2[:, 16 + t:17 + t], stats2[:, 16 + t:17 + t], stats2[:, 32 + t:33 + t],
            ALU.subtract))(), waits=[pt_c2, pt_c3])
        pt_c5 = p.step("scalar", (lambda t=t: lambda sc: sc.activation(
            stats2[:, 32 + t:33 + t], stats2[:, 16 + t:17 + t], AF.Ln, bias=fp(EPS),
            scale=fp(1.0)))(), waits=[pt_c4])
        pt_c6 = p.step("scalar", (lambda t=t: lambda sc: sc.activation(
            stats2[:, 16 + t:17 + t], stats2[:, 32 + t:33 + t], AF.Exp, bias=fp(0.0),
            scale=fp(-0.5)))(), waits=[pt_c5])
        xw = [out_pts[t - 2]] if t >= 2 else []
        pt_xl = p.step("sync", (lambda t=t, par=par: lambda sy: sy.dma_start(
            out=xlrb[:, par], in_=xspill[t * 128:(t + 1) * 128, :]))(),
            waits=[ln1_pts[t][0]] + xw, dma=True)
        z_pts = []
        for ob in range(4):
            cs = slice(ob * 512, (ob + 1) * 512)
            p1_ = p.step("vector", (lambda t=t, cs=cs: lambda v: v.tensor_scalar(
                ybuf[:, t, cs], ybuf[:, t, cs], stats2[:, t:t + 1], stats2[:, 16 + t:17 + t],
                ALU.subtract, ALU.mult))(), waits=[pt_c6] + sq_pts)
            p2_ = p.step("gpsimd", (lambda t=t, cs=cs: lambda g: g.tensor_tensor(
                ybuf[:, t, cs], ybuf[:, t, cs], gam_rep[:, cs], ALU.mult))(),
                waits=[p1_, pt_gam])
            p3_ = p.step("gpsimd", (lambda t=t, cs=cs: lambda g: g.tensor_tensor(
                ybuf[:, t, cs], ybuf[:, t, cs], bet_rep[:, cs], ALU.add))(),
                waits=[p2_, pt_bet])
            p4_ = p.step("vector", (lambda t=t, cs=cs, ob=ob, par=par: lambda v: v.scalar_tensor_tensor(
                ybuf[:, t, cs], ybuf[:, t, cs], 0.0, xlrb[:, par, cs], ALU.add, ALU.add,
                accum_out=stats[:, t * 4 + ob:t * 4 + ob + 1]))(), waits=[p3_, pt_xl])
            z_pts.append(p4_)
        pt_d1 = p.step("vector", (lambda t=t: lambda v: v.reduce_sum(
            stats2[:, t:t + 1], stats[:, t * 4:t * 4 + 4], axis=AX.X))(), waits=z_pts)
        pt_d2 = p.step("vector", (lambda t=t: lambda v: v.tensor_scalar_mul(
            stats2[:, t:t + 1], stats2[:, t:t + 1], INVD))(), waits=[pt_d1])
        sq2_pts = []
        for ob in range(4):
            bi, slot, bwaits = aux.take()
            spt = p.step("scalar", (lambda t=t, ob=ob, slot=slot: lambda sc: sc.activation(
                slot[:, 0:512], ybuf[:, t, ob * 512:(ob + 1) * 512], AF.Square,
                bias=fp(0.0), scale=fp(1.0),
                accum_out=stats[:, 32 + t * 4 + ob:32 + t * 4 + ob + 1]))(),
                waits=[z_pts[ob]] + bwaits)
            aux.release(bi, spt)
            sq2_pts.append(spt)
        pt_d3 = p.step("vector", (lambda t=t: lambda v: v.reduce_sum(
            stats2[:, 16 + t:17 + t], stats[:, 32 + t * 4:32 + t * 4 + 4], axis=AX.X))(), waits=sq2_pts)
        pt_d4 = p.step("vector", (lambda t=t: lambda v: v.tensor_scalar_mul(
            stats2[:, 16 + t:17 + t], stats2[:, 16 + t:17 + t], INVD))(), waits=[pt_d3])
        pt_d5 = p.step("vector", (lambda t=t: lambda v: v.tensor_tensor(
            stats2[:, 32 + t:33 + t], stats2[:, t:t + 1], stats2[:, t:t + 1], ALU.mult))(), waits=[pt_d2])
        pt_d6 = p.step("vector", (lambda t=t: lambda v: v.tensor_tensor(
            stats2[:, 16 + t:17 + t], stats2[:, 16 + t:17 + t], stats2[:, 32 + t:33 + t],
            ALU.subtract))(), waits=[pt_d4, pt_d5])
        pt_d7 = p.step("scalar", (lambda t=t: lambda sc: sc.activation(
            stats2[:, 32 + t:33 + t], stats2[:, 16 + t:17 + t], AF.Ln, bias=fp(EPS),
            scale=fp(1.0)))(), waits=[pt_d6])
        pt_d8 = p.step("scalar", (lambda t=t: lambda sc: sc.activation(
            stats2[:, 16 + t:17 + t], stats2[:, 32 + t:33 + t], AF.Exp, bias=fp(0.0),
            scale=fp(-0.5)))(), waits=[pt_d7])
        fin_pts = []
        for ob in range(4):
            cs = slice(ob * 512, (ob + 1) * 512)
            fpt = p.step("vector", (lambda t=t, cs=cs: lambda v: v.tensor_scalar(
                ybuf[:, t, cs], ybuf[:, t, cs], stats2[:, t:t + 1], stats2[:, 16 + t:17 + t],
                ALU.subtract, ALU.mult))(), waits=[pt_d8] + sq2_pts)
            fin_pts.append(fpt)
        opt = p.step("sync", (lambda t=t: lambda sy: sy.dma_start(
            out=out[t * 128:(t + 1) * 128, :], in_=ybuf[:, t]))(), waits=fin_pts, dma=True)
        out_pts.append(opt)

    p.step("sync", lambda sy: sy.nop(), waits=out_pts, inc=(p.sem["sync"], 1))

    p.emit()
    return nc, DBG


def _swz_quarters(w):
    dmat, n = w.shape
    return np.ascontiguousarray(
        w.reshape(dmat // 128, 128, n // 512, 512).transpose(2, 1, 0, 3))


def _prep_weights(i):
    wts = {}
    wts["wq"] = _swz_quarters(np.asarray(i["L0_attn_wq_w"], np.float32))
    wts["wk"] = _swz_quarters(np.asarray(i["L0_attn_wk_w"], np.float32))
    wts["wv"] = _swz_quarters(np.asarray(i["L0_attn_wv_w"], np.float32))
    wts["wo"] = _swz_quarters(np.asarray(i["L0_attn_wo_w"], np.float32))
    for nm, key in (("wkl", "L0_attn_wkl_w"), ("wvl", "L0_attn_wvl_w")):
        w = np.asarray(i[key], np.float32)
        wts[nm] = np.ascontiguousarray(w.reshape(16, 128, 512).transpose(1, 0, 2))
    for nm, key in (("upk", "L0_attn_upk_w"), ("upv", "L0_attn_upv_w")):
        w = np.asarray(i[key], np.float32)
        wts[nm] = np.ascontiguousarray(w.reshape(16, 4, 128, 128).transpose(2, 0, 1, 3))
    g = np.asarray(i["L0_ffn_gate_w"], np.float32)
    wts["gate"] = np.ascontiguousarray(g.reshape(16, 128, 8).transpose(1, 0, 2))
    up = np.asarray(i["L0_ffn_up_w"], np.float32)
    wts["upw"] = np.ascontiguousarray(up.reshape(16, 128, 32, 256).transpose(2, 1, 0, 3))
    dn = np.asarray(i["L0_ffn_down_w"], np.float32)
    wts["dnw"] = np.ascontiguousarray(dn.reshape(4, 16, 128, 4, 512).transpose(0, 3, 2, 1, 4))
    wts["exw"] = _swz_quarters(np.asarray(i["L0_ffn_expert_weight"], np.float32))
    for nm, key in (("wqb", "L0_attn_wq_b"), ("wkb", "L0_attn_wk_b"), ("wvb", "L0_attn_wv_b")):
        wts[nm] = np.ascontiguousarray(np.asarray(i[key], np.float32).reshape(16, 128).T)
    for nm, key in (("wklb", "L0_attn_wkl_b"), ("wvlb", "L0_attn_wvl_b")):
        wts[nm] = np.ascontiguousarray(np.asarray(i[key], np.float32).reshape(4, 128).T)
    wts["upkb"] = np.ascontiguousarray(np.asarray(i["L0_attn_upk_b"], np.float32).T)
    wts["upvb"] = np.ascontiguousarray(np.asarray(i["L0_attn_upv_b"], np.float32).reshape(-1))
    wts["wob"] = np.ascontiguousarray(np.asarray(i["L0_attn_wo_b"], np.float32))
    wts["lng"] = np.ascontiguousarray(np.asarray(i["L0_ffn_ln_gamma"], np.float32))
    wts["lnb"] = np.ascontiguousarray(np.asarray(i["L0_ffn_ln_beta"], np.float32))
    return wts


def kernel(**inputs):
    x = np.asarray(inputs["L0_attn_x"], np.float32)
    wts = _prep_weights(inputs)
    nc, _ = build()
    in_maps = []
    for core in range(8):
        b, a = core // 2, core % 2
        xs = np.ascontiguousarray(x[b].reshape(16, 128, D)[a::2].reshape(TOK, D))
        m = {"xin": xs, "parity": np.array([a], np.float32)}
        m.update(wts)
        in_maps.append(m)
    res = run_bass_kernel_spmd(nc, in_maps, core_ids=list(range(8)))
    outp = np.empty((4, 2048, D), np.float32)
    for core in range(8):
        b, a = core // 2, core % 2
        outp[b].reshape(16, 128, D)[a::2] = res.results[core]["out"].reshape(NT, 128, D)
    return outp
